# version 16
# baseline (speedup 1.0000x reference)
import numpy as np
import contextlib
from functools import reduce
import concourse.bass as bass
import concourse.mybir as mybir
from concourse.bass_utils import run_bass_kernel_spmd

F32 = mybir.dt.float32
BF16 = mybir.dt.bfloat16
I32 = mybir.dt.int32
U8 = mybir.dt.uint8
AF = mybir.ActivationFunctionType
ALU = mybir.AluOpType
AX = mybir.AxisListType

NCORES = 8
T = 8192
D = 4096
TPC = T // NCORES
EPS_LN = 1e-5
RMS_EPS = 1.1920929e-07
MAMBA_EPS = 1e-5
RG = [list(range(NCORES))]



class PhasedArena:

    def __init__(self, nc):
        self.nc = nc
        self.tensors = []
        self.size = 0
        self.cur = 0

    def sb(self, name, shape, dtype):
        per_part = reduce(lambda a, c: a * c, shape[1:], 1) * mybir.dt.size(dtype)
        pre = self.cur
        self.cur = (pre + per_part + 31) & ~31
        self.size = max(self.size, self.cur)
        handle = self.nc.virtual_sbuf_tensor(name, shape, dtype)
        self.nc._tensor(handle.name, handle.shape, handle.dtype, type="SB")
        self.tensors.append((handle, pre))
        return handle

    def reset(self):
        self.cur = 0

    def allocate(self, stack):
        handle = stack.enter_context(
            self.nc.sbuf_tensor("arena", [128, self.size], U8))
        offset = self.nc.lookup_mloc(handle).addr
        for hnd, addr in self.tensors:
            mloc = self.nc.lookup_mloc(hnd)
            mloc.allocated = not self.nc.target_bir_lowering
            mloc.addr = offset + addr


class KB:

    ENGS = ("sync", "tensor", "vector", "scalar", "gpsimd")

    def __init__(self):
        self.nc = bass.Bass("TRN2", target_bir_lowering=False, debug=False)
        self.q = {e: [] for e in self.ENGS}
        self.stack = contextlib.ExitStack()
        self.esem, self.ecnt, self.dsem, self.dcnt = {}, {}, {}, {}
        for e in self.ENGS:
            self.esem[e] = self.stack.enter_context(self.nc.semaphore(f"es_{e}"))
            self.ecnt[e] = 0
        self.arena = PhasedArena(self.nc)
        self.prefix = ""
        self.guard = []
        self.guard_pending = set()
        self.ccsem = self.stack.enter_context(self.nc.semaphore("ccsem"))
        self.cccnt = 0

    def phase(self, prefix):
        toks = [(self.esem[e], self.ecnt[e]) for e in self.ENGS if self.ecnt[e]]
        toks += [(self.dsem[k], self.dcnt[k]) for k in self.dsem]
        self.guard = toks
        self.guard_pending = set(self.ENGS)
        self.prefix = prefix
        self.arena.reset()

    def _inject(self, eng, deps):
        if eng in self.guard_pending:
            self.guard_pending.discard(eng)
            return list(deps) + self.guard
        return list(deps)

    def sb(self, name, shape, dtype):
        return self.arena.sb(self.prefix + name, shape, dtype)

    def ps(self, name, shape, dtype=F32):
        return self.stack.enter_context(self.nc.psum_tensor(name, shape, dtype))

    def dram(self, name, shape, dtype, kind=None, addr_space="Local"):
        if kind:
            return self.nc.dram_tensor(name, shape, dtype, kind=kind).ap()
        return self.nc.dram_tensor(self.prefix + name, shape, dtype,
                                   addr_space=addr_space).ap()

    def step(self, eng, fn, deps=()):
        sem = self.esem[eng]
        self.ecnt[eng] += 1
        val = self.ecnt[eng]
        deps = self._inject(eng, deps)
        deps2 = _dedup([d for d in deps if d is not None and d[0] is not sem])
        if eng in ("gpsimd", "vector", "scalar") and val > 1:
            deps2.append((sem, val - 1))
        deps2 = tuple(deps2)

        def run(e, fn=fn, deps2=deps2, sem=sem):
            for (s, v) in deps2:
                e.wait_ge(s, v)
            inst = fn(e)
            if inst is None:
                inst = e.nop()
            inst.then_inc(sem, 1)
        self.q[eng].append(run)
        return (sem, val)

    def dma(self, eng, fn, n, deps=(), lane="0"):
        key = (eng, lane)
        if key not in self.dsem:
            self.dsem[key] = self.stack.enter_context(
                self.nc.semaphore(f"ds_{eng}_{lane}"))
            self.dcnt[key] = 0
        sem = self.dsem[key]
        self.dcnt[key] += 16 * n
        val = self.dcnt[key]
        deps = self._inject(eng, deps)
        deps2 = tuple(_dedup([d for d in deps if d is not None]))

        def run(e, fn=fn, deps2=deps2, sem=sem):
            for (s, v) in deps2:
                e.wait_ge(s, v)
            fn(e, sem)
        self.q[eng].append(run)
        return (sem, val)

    def cc(self, fn, deps=()):
        self.cccnt += 1
        val = self.cccnt
        deps = self._inject("gpsimd", deps)
        deps2 = tuple(_dedup([d for d in deps if d is not None]))

        def run(e, fn=fn, deps2=deps2, sem=self.ccsem):
            for (s, v) in deps2:
                e.wait_ge(s, v)
            fn(e).then_inc(sem, 1)
        self.q["gpsimd"].append(run)
        return (self.ccsem, val)

    def finish(self, final_deps):
        def fin(e, deps=tuple(d for d in final_deps if d is not None)):
            for (s, v) in deps:
                e.wait_ge(s, v)
        self.q["sync"].append(fin)
        nc = self.nc
        self.arena.allocate(self.stack)
        with nc.Block() as block:
            for eng in self.ENGS:
                ops = self.q[eng]
                if not ops:
                    continue
                def mk(ops):
                    def run(e):
                        for f in ops:
                            f(e)
                    return run
                getattr(block, eng)(mk(ops))
        self.stack.close()
        return nc


def _dedup(deps):
    best = {}
    for (s, v) in deps:
        k = id(s)
        if k not in best or best[k][1] < v:
            best[k] = (s, v)
    return list(best.values())


def brd(ap, pattern):
    return bass.AP(ap.tensor, ap.offset, [list(ap.ap[0])] + [list(p) for p in pattern])


def row_bcast(dram_row_ap, ncols):
    return bass.AP(dram_row_ap.tensor, dram_row_ap.offset, [[0, 128], [1, ncols]])


def emit_ident(b, name, dtype):
    t = b.sb(name, [128, 128], dtype)
    b.step("gpsimd", lambda g: g.memset(t[:, :], 1.0))
    tok = b.step("gpsimd", lambda g: g.affine_select(
        out=t[:, :], in_=t[:, :], compare_op=ALU.is_equal, fill=0.0, base=0,
        pattern=[[-1, 128]], channel_multiplier=1))
    return t, tok



def emit_ln_finalize(b, ps_s, ps_q, stat, scr, scr_row, rrep, brep, dep, ntok,
                     srep=None, nchan=D, eps=EPS_LN):
    stat1, stat2 = stat
    M = stat1[0:1, 0:ntok]
    Q = stat2[0:1, 0:ntok]
    V_ = rrep[0:1, 0:ntok]
    R = brep[0:1, 0:ntok]
    S2 = stat2[0:1, 0:ntok]
    Bv = rrep[0:1, 0:ntok]

    b.step("vector", lambda v: v.tensor_single_scalar(
        M, ps_s[0:1, 0:ntok], 1.0 / nchan, ALU.mult), deps=[dep])
    b.step("vector", lambda v: v.tensor_single_scalar(
        Q, ps_q[0:1, 0:ntok], 1.0 / nchan, ALU.mult))
    b.step("vector", lambda v: v.tensor_tensor(V_, M, M, ALU.mult))
    b.step("vector", lambda v: v.tensor_tensor(V_, Q, V_, ALU.subtract))
    b.step("vector", lambda v: v.tensor_single_scalar(Q, V_, eps, ALU.add))
    t1 = b.step("vector", lambda v: v.reciprocal(R, Q))
    if srep is not None:
        sr = srep[0:1, 0:ntok]
        b.step("vector", lambda v: v.tensor_tensor(sr, V_, R, ALU.mult))
        b.step("vector", lambda v: v.tensor_single_scalar(sr, sr, RMS_EPS, ALU.add))
        t1 = b.step("vector", lambda v: v.reciprocal(sr, sr))
        S2 = sr
    t2 = b.step("scalar", lambda sc: sc.activation(R, R, AF.Sqrt), deps=[t1])
    if srep is not None:
        t2 = b.step("scalar", lambda sc: sc.activation(S2, S2, AF.Sqrt), deps=[t2])
    t3 = b.step("vector", lambda v: v.tensor_tensor(Bv, M, R, ALU.mult), deps=[t2])

    nrows = 3 if srep is not None else 2

    def w1(e, s):
        e.dma_start(out=scr[scr_row:scr_row + 1, 0:ntok], in_=R).then_inc(s, 16)
        e.dma_start(out=scr[scr_row + 1:scr_row + 2, 0:ntok], in_=Bv).then_inc(s, 16)
        if srep is not None:
            e.dma_start(out=scr[scr_row + 2:scr_row + 3, 0:ntok], in_=S2).then_inc(s, 16)
    t4 = b.dma("gpsimd", w1, nrows, deps=[t3], lane="lnrep")

    def w2(e, s):
        e.dma_start(out=rrep[:, 0:ntok],
                    in_=row_bcast(scr[scr_row:scr_row + 1, 0:ntok], ntok)).then_inc(s, 16)
        e.dma_start(out=brep[:, 0:ntok],
                    in_=row_bcast(scr[scr_row + 1:scr_row + 2, 0:ntok], ntok)).then_inc(s, 16)
        if srep is not None:
            e.dma_start(out=srep[:, 0:ntok],
                        in_=row_bcast(scr[scr_row + 2:scr_row + 3, 0:ntok], ntok)).then_inc(s, 16)
    t5 = b.dma("gpsimd", w2, nrows, deps=[t4], lane="lnrep")
    return t5



def emit_ffn0(b, P, xln, v_xln, srep, v_srep, out_cb):
    ntok = TPC
    ntc = ntok // 128
    wcat_sb, wbr_sb, hb, idf, scr = P["wcat_sb"], P["wbr_sb"], P["hb"], P["idf"], P["scr"]
    ps_log, ps_misc = P["ps_log"], P["ps_misc"]
    ps_y = P["ps_y"] if "ps_y" in P else P["ps_mm"][0]

    logits = b.sb("f0_logits", [24, ntok], F32)
    toks = []
    ev_prev = None
    for tb in range(ntok // 512):
        def mml(t, tb=tb):
            for cc in range(32):
                i = t.matmul(ps_log[0:24, 0:512], wcat_sb[:, 24 * cc:24 * (cc + 1)],
                             xln[:, ntok * cc + 512 * tb:ntok * cc + 512 * tb + 512],
                             start=(cc == 0), stop=(cc == 31))
            return i
        tok = b.step("tensor", mml, deps=[v_xln, P["tok_w"]] + ([ev_prev] if ev_prev else []))
        def ev1(v, tb=tb):
            return v.tensor_tensor(logits[0:24, 512 * tb:512 * (tb + 1)],
                                   ps_log[0:24, 0:512], srep[0:24, 512 * tb:512 * (tb + 1)],
                                   ALU.mult)
        tok = b.step("vector", ev1, deps=[tok, v_srep])
        ev_prev = tok
        toks.append(tok)
    tok = b.step("scalar", lambda sc: sc.activation(
        logits[:, :], logits[:, :], AF.Identity, bias=hb[0:24, 0:1]), deps=toks)
    v_log = tok

    Ztm = b.sb("f0_Ztm", [128, ntc * 24], F32)
    tok_t = None
    for tc in range(ntc):
        tok_t = b.step("tensor", lambda t, tc=tc: t.transpose(
            ps_misc[:, 0:24], logits[0:24, 128 * tc:128 * (tc + 1)], idf[0:24, 0:24]),
            deps=[v_log] if tc == 0 else [tok_t])
        tok_t = b.step("vector", lambda v, tc=tc: v.tensor_copy(
            Ztm[:, 24 * tc:24 * (tc + 1)], ps_misc[:, 0:24]), deps=[tok_t])
    v_tm = tok_t

    Zs = b.sb("f0_Zs", [128, ntc * 16], F32)
    Tt = b.sb("f0_T", [128, ntc * 16], F32)
    u = b.sb("f0_u", [128, ntc * 4], F32)
    vv_ = b.sb("f0_v", [128, ntc * 4], F32)
    red = b.sb("f0_red", [128, 2 * ntc * 4], F32)
    Hres = b.sb("f0_Hres", [128, ntc * 16], F32)
    sp = b.sb("f0_sp", [128, ntc * 4], F32)
    spo = b.sb("f0_spo", [128, ntc * 4], F32)

    def prep(v):
        for tc in range(ntc):
            v.tensor_single_scalar(Zs[:, 16 * tc:16 * (tc + 1)],
                                   Ztm[:, 24 * tc:24 * tc + 16], 10.0, ALU.mult)
            v.tensor_copy(sp[:, 4 * tc:4 * (tc + 1)], Ztm[:, 24 * tc + 16:24 * tc + 20])
            v.tensor_copy(spo[:, 4 * tc:4 * (tc + 1)], Ztm[:, 24 * tc + 20:24 * tc + 24])
        v.memset(u[:, :], 0.0)
        return v.memset(vv_[:, :], 0.0)
    tok = b.step("vector", prep, deps=[v_tm])

    zv = brd(Zs[:, :], [[16, ntc], [4, 4], [1, 4]])
    tv = brd(Tt[:, :], [[16, ntc], [4, 4], [1, 4]])
    ub = brd(u[:, :], [[4, ntc], [1, 4], [0, 4]])
    vb = brd(vv_[:, :], [[4, ntc], [0, 4], [1, 4]])
    tred_j = brd(Tt[:, :], [[4, ntc * 4], [1, 4]])
    tred_i = brd(Tt[:, :], [[16, ntc], [1, 4], [4, 4]])
    red_j = brd(red[:, 0:ntc * 4], [[1, ntc * 4]])
    red_i = brd(red[:, 0:ntc * 4], [[4, ntc], [1, 4]])

    for it in range(50):
        tok = b.step("vector", lambda v: v.tensor_tensor(tv, zv, vb, ALU.subtract), deps=[tok])
        tok = b.step("scalar", lambda sc: sc.activation(Tt[:, :], Tt[:, :], AF.Exp), deps=[tok])
        tok = b.step("vector", lambda v: v.tensor_reduce(red_j, tred_j, AX.X, ALU.add), deps=[tok])
        tok = b.step("scalar", lambda sc: sc.activation(u[:, :], red[:, 0:ntc * 4], AF.Ln), deps=[tok])
        tok = b.step("vector", lambda v: v.tensor_tensor(tv, zv, ub, ALU.subtract), deps=[tok])
        tok = b.step("scalar", lambda sc: sc.activation(Tt[:, :], Tt[:, :], AF.Exp), deps=[tok])
        tok = b.step("vector", lambda v: v.tensor_reduce(red_i, tred_i, AX.X, ALU.add), deps=[tok])
        tok = b.step("scalar", lambda sc: sc.activation(vv_[:, :], red[:, 0:ntc * 4], AF.Ln), deps=[tok])
    tok = b.step("vector", lambda v: v.tensor_tensor(tv, zv, ub, ALU.subtract), deps=[tok])
    tok = b.step("vector", lambda v: v.tensor_tensor(tv, tv, vb, ALU.subtract), deps=[tok])
    tok = b.step("scalar", lambda sc: sc.activation(Hres[:, :], Tt[:, :], AF.Exp), deps=[tok])
    v_hraw = tok

    gsml = b.sb("f0_gsml", [1, 2], F32)
    g_pp = b.sb("f0_gpp", [128, 1], F32)
    og_pp = b.sb("f0_ogpp", [128, 1], F32)
    hbg = b.sb("f0_hbg", [1, 1], F32)
    tokg0 = b.dma("gpsimd", lambda e, s: (
        e.dma_start(out=hbg[0:1, 0:1],
                    in_=P["hbase_dram"][24:25].rearrange("(p o) -> p o", o=1)).then_inc(s, 16),), 1,
        lane="gate")
    tg = b.step("scalar", lambda sc: sc.activation(gsml[0:1, 0:1], hbg[0:1, 0:1],
                                                   AF.Sigmoid), deps=[tokg0])
    tg = b.step("vector", lambda v: v.tensor_scalar(
        gsml[0:1, 1:2], gsml[0:1, 0:1], -1.0, 1.0, ALU.mult, ALU.add), deps=[tg])
    tg = b.dma("gpsimd", lambda e, s: (
        e.dma_start(out=scr[2:3, 0:2], in_=gsml[0:1, 0:2]).then_inc(s, 16),), 1, deps=[tg],
        lane="gate")
    tg = b.dma("gpsimd", lambda e, s: (
        e.dma_start(out=g_pp[:, :], in_=row_bcast(scr[2:3, 0:1], 1)).then_inc(s, 16),
        e.dma_start(out=og_pp[:, :], in_=row_bcast(scr[2:3, 1:2], 1)).then_inc(s, 16)),
        2, deps=[tg], lane="gate")

    Ipat = b.sb("f0_I", [128, ntc * 16], BF16)
    b.step("gpsimd", lambda g: g.memset(Ipat[:, :], 1.0))
    ti = b.step("gpsimd", lambda g: g.affine_select(
        out=Ipat[:, :], in_=Ipat[:, :], compare_op=ALU.is_equal, fill=0.0, base=0,
        pattern=[[0, ntc], [1, 4], [-1, 4]], channel_multiplier=0))

    b.step("vector", lambda v: v.tensor_scalar(
        Hres[:, :], Hres[:, :], g_pp[:, :], 0.0, ALU.mult, ALU.add),
        deps=[v_hraw, tg, ti])
    tok = b.step("vector", lambda v: v.scalar_tensor_tensor(
        Hres[:, :], Ipat[:, :], og_pp[:, :], Hres[:, :], ALU.mult, ALU.add))
    v_hres = tok

    last_sm = tok
    for src in (sp, spo):
        sv = brd(src[:, :], [[4, ntc], [1, 4]])
        r0 = red[:, 0:ntc]
        r0v = brd(r0, [[1, ntc]])
        r0b = brd(r0, [[1, ntc], [0, 4]])
        r1 = red[:, ntc:2 * ntc]
        r1v = brd(r1, [[1, ntc]])
        r1b = brd(r1, [[1, ntc], [0, 4]])
        b.step("vector", lambda v, sv=sv, r0v=r0v: v.tensor_reduce(
            r0v, sv, AX.X, ALU.max), deps=[v_tm, last_sm])
        last_sm = b.step("vector", lambda v, src=src, r0b=r0b: v.tensor_tensor(
            src[:, :], src[:, :], r0b, ALU.subtract))
        last_sm = b.step("scalar", lambda sc, src=src: sc.activation(
            src[:, :], src[:, :], AF.Exp), deps=[last_sm])
        b.step("vector", lambda v, sv=sv, r0v=r0v: v.tensor_reduce(
            r0v, sv, AX.X, ALU.add), deps=[last_sm])
        b.step("vector", lambda v, r0v=r0v, r1v=r1v: v.reciprocal(r1v, r0v))
        last_sm = b.step("vector", lambda v, src=src, r1b=r1b: v.tensor_tensor(
            src[:, :], src[:, :], r1b, ALU.mult))
    v_sm = last_sm

    tmp = P.get("f0_tmp") or b.sb("f0_tmp", [128, ntok], F32)
    acc = P.get("f0_acc") or b.sb("f0_acc", [128, ntok], F32)
    H_fm = acc
    tok_tb = None
    for tc in range(ntc):
        for (src, w, dst0) in ((Hres, 16, 0), (sp, 4, 32), (spo, 4, 64)):
            tok_tb = b.step("tensor", lambda t, tc=tc, src=src, w=w: t.transpose(
                ps_misc[0:w, 0:128], src[:, w * tc:w * (tc + 1)], idf[:, :]),
                deps=[v_hres, v_sm] if tok_tb is None else [tok_tb])
            tok_tb = b.step("vector", lambda v, tc=tc, w=w, dst0=dst0: v.tensor_copy(
                H_fm[dst0:dst0 + w, 128 * tc:128 * (tc + 1)], ps_misc[0:w, 0:128]),
                deps=[tok_tb])
    v_hfm = tok_tb

    tok_hscr = b.dma("gpsimd", lambda e, s: (
        e.dma_start(out=scr[8:24, 0:ntok], in_=H_fm[0:16, :]).then_inc(s, 16),
        e.dma_start(out=scr[24:28, 0:ntok], in_=H_fm[32:36, :]).then_inc(s, 16),
        e.dma_start(out=scr[28:32, 0:ntok], in_=H_fm[64:68, :]).then_inc(s, 16)), 3,
        deps=[v_hfm], lane="hrep")

    Hrep = P["fbf"][0:128, 8 * ntok:13 * ntok]

    def Hs(slot):
        return Hrep[:, ntok * slot:ntok * (slot + 1)]

    def rep_rows(rows, dep_tok, extra=()):
        def reps(e, s):
            for slot, r in enumerate(rows):
                e.dma_start(out=Hs(slot),
                            in_=row_bcast(scr[8 + r:9 + r, 0:ntok], ntok)).then_inc(s, 16)
        return b.dma("gpsimd", reps, len(rows), deps=[dep_tok] + list(extra), lane="hrep")

    v_hrep = rep_rows([16, 17, 18, 19], tok_hscr)
    xpre = P["fbf"][0:128, 0:8 * ntok]
    tok_xp = None
    for cc in range(8):
        dep0 = [v_hrep, v_xln] if cc == 0 else [tok_xp]
        for n in range(4):
            src = xln[:, ntok * (n * 8 + cc):ntok * (n * 8 + cc + 1)]
            if n == 0:
                tok_xp = b.step("vector", lambda v, src=src: v.tensor_tensor(
                    acc[:, :], src, Hs(0), ALU.mult), deps=dep0)
            else:
                b.step("vector", lambda v, src=src, n=n: v.tensor_tensor(
                    tmp[:, :], src, Hs(n), ALU.mult))
                dst = acc[:, :] if n < 3 else xpre[:, ntok * cc:ntok * (cc + 1)]
                tok_xp = b.step("vector", lambda v, dst=dst: v.tensor_tensor(
                    dst, acc[:, :], tmp[:, :], ALU.add))
    v_xpre = tok_xp

    ybc = P["fbf"][0:128, 13 * ntok:14 * ntok]
    tok_f = v_xpre
    tok_ev = None
    for i in range(4):
        tok_f = rep_rows([i * 4 + 0, i * 4 + 1, i * 4 + 2, i * 4 + 3, 20 + i],
                         tok_hscr, extra=[tok_f])
        for cc in range(8):
            for tb in range(ntok // 512):
                def mmy(t, cc=cc, tb=tb):
                    for ccc in range(8):
                        i = t.matmul(ps_y[:, :],
                                     wbr_sb[:, 1024 * ccc + 128 * cc:1024 * ccc + 128 * (cc + 1)],
                                     xpre[:, ntok * ccc + 512 * tb:ntok * ccc + 512 * tb + 512],
                                     start=(ccc == 0), stop=(ccc == 7))
                    return i
                tok_mmy = b.step("tensor", mmy,
                                 deps=([v_xpre, P["tok_w"]] + (P.get("ps_mm_guard") or []))
                                 if tok_ev is None else [tok_ev, tok_f])
                tok_ev = b.step("scalar", lambda sc, cc=cc, tb=tb: sc.copy(
                    ybc[:, 512 * tb:512 * (tb + 1)], ps_y[:, :]), deps=[tok_mmy])
            if P.get("pending_stats"):
                P["pending_stats"]()
                P["pending_stats"] = None

            for j in range(4):
                src = xln[:, ntok * (j * 8 + cc):ntok * (j * 8 + cc + 1)]
                if j == 0:
                    b.step("vector", lambda v, src=src: v.tensor_tensor(
                        acc[:, :], src, Hs(0), ALU.mult), deps=[tok_f, tok_ev])
                else:
                    b.step("vector", lambda v, src=src, j=j: v.tensor_tensor(
                        tmp[:, :], src, Hs(j), ALU.mult))
                    b.step("vector", lambda v: v.tensor_tensor(
                        acc[:, :], acc[:, :], tmp[:, :], ALU.add))
            b.step("vector", lambda v: v.tensor_tensor(
                tmp[:, :], ybc[:, :], Hs(4), ALU.mult), deps=[tok_ev])
            tok_f = b.step("vector", lambda v: v.tensor_tensor(
                acc[:, :], acc[:, :], tmp[:, :], ALU.add))
            tok_f = out_cb(i * 8 + cc, acc, tok_f)
            tok_ev = tok_f
    if P.get("pending_stats"):
        P["pending_stats"]()
        P["pending_stats"] = None
    P["ps_mm_guard"] = [(b.esem["vector"], b.ecnt["vector"]),
                        (b.esem["scalar"], b.ecnt["scalar"])]
    return tok_f



def emit_block_ffn0(b, P, xsrc, xres_out, xln_next_out, half_scale,
                    src_dep=None, first_ln_srep=True):
    ntok = TPC
    scr = P["scr"]
    xcb = P["xcb"]
    ps_s, ps_q = P["ps_s"], P["ps_q"]
    ones_f, ones_b = P["ones_f"], P["ones_b"]
    stat = P["stat"]
    xsq = P["xsq"]

    tok_mm = None
    prev_mm = [src_dep, src_dep]
    for cc in range(32):
        buf = xcb[cc % 2]
        tok_l = b.dma("sync", lambda e, s, cc=cc, buf=buf: (
            e.dma_start(out=buf[:, :], in_=xsrc[128 * cc:128 * (cc + 1), :]).then_inc(s, 16),),
            1, deps=[prev_mm[cc % 2]], lane=f"ld{cc % 2}")
        tok_sq = b.step("scalar", lambda sc, buf=buf: sc.activation(
            xsq[:, :], buf[:, :], AF.Square), deps=[tok_l, tok_mm])

        def mm(t, cc=cc, buf=buf):
            for tb in range(2):
                t.matmul(ps_s[0:1, 512 * tb:512 * (tb + 1)], ones_f[:, :],
                         buf[:, 512 * tb:512 * (tb + 1)], start=(cc == 0), stop=(cc == 31))
            for tb in range(2):
                i = t.matmul(ps_q[0:1, 512 * tb:512 * (tb + 1)], ones_b[:, :],
                             xsq[:, 512 * tb:512 * (tb + 1)], start=(cc == 0), stop=(cc == 31))
            return i
        tok_mm = b.step("tensor", mm, deps=[tok_l, tok_sq, P["tok_ones"]])
        prev_mm[cc % 2] = tok_mm
    v_stats = tok_mm

    rrep, brep, srep = P["rrep"], P["brep"], P["srep"]
    tok_rep = emit_ln_finalize(b, ps_s, ps_q, stat, scr, 0, rrep, brep, v_stats,
                               ntok, srep=srep)

    xln = P["xln"]
    tok_ap = None
    prev_ap = [v_stats, v_stats]
    for cc in range(32):
        buf = xcb[cc % 2]
        tok_l = b.dma("sync", lambda e, s, cc=cc, buf=buf: (
            e.dma_start(out=buf[:, :], in_=xsrc[128 * cc:128 * (cc + 1), :]).then_inc(s, 16),),
            1, deps=[prev_ap[cc % 2]], lane=f"ld{cc % 2}")

        b.step("vector", lambda v, buf=buf: v.tensor_tensor(
            buf[:, :], buf[:, :], rrep[:, :], ALU.mult), deps=[tok_l, tok_rep])
        tok_ap = b.step("vector", lambda v, cc=cc, buf=buf: v.tensor_tensor(
            xln[:, ntok * cc:ntok * (cc + 1)], buf[:, :], brep[:, :], ALU.subtract))
        prev_ap[cc % 2] = tok_ap
    v_xln = tok_ap

    x1sb = P["x1sb"]
    xsq2 = P["xsq2"]
    obuf = [P["xc3"], P["xc3"]]
    state = {"tok_mm": None, "tok_use": [None, None], "tok_st": None,
             "tok_l": {}}
    st_toks = []

    def pre_load(cc):
        if cc > 31 or cc in state["tok_l"]:
            return
        state["tok_l"][cc] = b.dma("sync", lambda e, s, cc=cc: (
            e.dma_start(out=obuf[cc % 2][:, :],
                        in_=xsrc[128 * cc:128 * (cc + 1), :]).then_inc(s, 16),),
            1, deps=[state["tok_use"][cc % 2],
                     v_xln if cc < 2 else None], lane="ocb")

    def out_cb(chunk, f_ap, dep):
        cc = chunk
        pre_load(cc)
        tok_l = state["tok_l"].pop(cc)
        tok1 = b.step("vector", lambda v, cc=cc: v.scalar_tensor_tensor(
            x1sb[:, :], f_ap[:, :], half_scale, obuf[cc % 2][:, :],
            ALU.mult, ALU.add),
            deps=[tok_l, dep, state["tok_mm"], state["tok_st"]])
        state["tok_use"][0] = state["tok_use"][1] = tok1
        pre_load(cc + 1)
        tok_st = b.dma("sync", lambda e, s, cc=cc: (
            e.dma_start(out=xres_out[128 * cc:128 * (cc + 1), :], in_=x1sb[:, :]).then_inc(s, 16),),
            1, deps=[tok1], lane="st1")
        state["tok_st"] = tok_st
        st_toks.append(tok_st)
        tok_sq = b.step("scalar", lambda sc: sc.activation(
            xsq2[:, :], x1sb[:, :], AF.Square), deps=[tok1, state["tok_mm"]])

        def flush(cc=cc, tok_sq=tok_sq, tok1=tok1):
            def mm(t, cc=cc):
                first = (cc == 0)
                last = (cc == 31)
                for tb in range(2):
                    t.matmul(ps_s[0:1, 512 * tb:512 * (tb + 1)], ones_f[:, :],
                             x1sb[:, 512 * tb:512 * (tb + 1)], start=first, stop=last)
                for tb in range(2):
                    i = t.matmul(ps_q[0:1, 512 * tb:512 * (tb + 1)], ones_b[:, :],
                                 xsq2[:, 512 * tb:512 * (tb + 1)], start=first, stop=last)
                return i
            state["tok_mm"] = b.step("tensor", mm, deps=[tok_sq, tok1])
        P["pending_stats"] = flush
        return tok1

    emit_ffn0(b, P, xln, v_xln, srep, tok_rep, out_cb)
    v_stats2 = state["tok_mm"]

    rrep2, brep2 = P["rrep2"], P["brep2"]
    srep2 = P.get("srep2")
    tok_rep2 = emit_ln_finalize(b, ps_s, ps_q, stat, scr, 4, rrep2, brep2,
                                v_stats2, ntok, srep=srep2)

    xlt = P["xlt"]
    st2 = []
    tok_prev = None
    for cc in range(32):
        buf = xcb[cc % 2]
        tok_l = b.dma("sync", lambda e, s, cc=cc, buf=buf: (
            e.dma_start(out=buf[:, :], in_=xres_out[128 * cc:128 * (cc + 1), :]).then_inc(s, 16),),
            1, deps=[st_toks[-1], tok_prev], lane=f"ld{cc % 2}")

        b.step("vector", lambda v, buf=buf: v.tensor_tensor(
            buf[:, :], buf[:, :], rrep2[:, :], ALU.mult), deps=[tok_l, tok_rep2])
        tok_a = b.step("vector", lambda v, buf=buf: v.tensor_tensor(
            xlt[:, :], buf[:, :], brep2[:, :], ALU.subtract))
        tok_s = b.dma("sync", lambda e, s, cc=cc: (
            e.dma_start(out=xln_next_out[128 * cc:128 * (cc + 1), :], in_=xlt[:, :]).then_inc(s, 16),),
            1, deps=[tok_a], lane="st2")
        st2.append(tok_s)
        tok_prev = tok_s
    return st_toks, st2, tok_rep2


def make_shared_pool(b, PS, with_ffn0=True, fbf=None):
    P = {}
    P["scr"] = b.dram("scr", [64, TPC], F32)
    P["idf"], tok_idf = emit_ident(b, "idf", F32)
    P["xcb"] = [b.sb(f"xc{i}", [128, TPC], F32) for i in range(2)]
    P["xc3"] = P["xcb"][1]
    P["xsq2"] = b.sb("xsq2", [128, TPC], BF16)
    P["xsq"] = P["xsq2"]
    P["x1sb"] = b.sb("x1sb", [128, TPC], F32)
    P["xlt"] = P["xsq2"]
    P["ones_f"] = b.sb("ones_f", [128, 1], F32)
    P["ones_b"] = b.sb("ones_b", [128, 1], BF16)
    P["tok_ones"] = b.step("gpsimd", lambda g: (g.memset(P["ones_f"][:, :], 1.0),
                                                g.memset(P["ones_b"][:, :], 1.0))[-1])
    P["stat"] = (b.sb("stat1", [128, TPC], F32), b.sb("stat2", [128, TPC], F32))
    P["rrep"] = b.sb("rrep", [128, TPC], F32)
    P["brep"] = b.sb("brep", [128, TPC], F32)
    P["rrep2"] = P["rrep"]
    P["brep2"] = P["brep"]
    P["ps_s"] = PS["ps_s"]
    P["ps_q"] = PS["ps_q"]
    if with_ffn0:
        P["srep"] = P["x1sb"]
        P["wcat_sb"] = b.sb("wcat_sb", [128, 32 * 24], BF16)
        P["wbr_sb"] = b.sb("wbr_sb", [128, 8 * 1024], BF16)
        P["hb"] = b.sb("hb", [32, 1], F32)
        P["xln"] = b.sb("xln", [128, 32 * TPC], BF16)
        P["ps_log"] = PS["ps_log"]
        P["ps_misc"] = PS["ps_misc"]
        if fbf is None:
            fbf = b.sb("fbf", [128, 14336], BF16)
        P["fbf"] = fbf
    return P


def load_ffn0_weights(b, P, wcat, wbr, hbase):
    P["hbase_dram"] = hbase
    P["tok_w"] = b.dma("gpsimd", lambda e, s: (
        e.dma_start(out=P["wcat_sb"][:, :].rearrange("p (cc o) -> p cc o", o=24),
                    in_=wcat.rearrange("(cc p) o -> p cc o", p=128)).then_inc(s, 16),
        e.dma_start(out=P["wbr_sb"][:, :].rearrange("p (cc o) -> p cc o", o=1024),
                    in_=wbr.rearrange("(cc p) o -> p cc o", p=128)).then_inc(s, 16),
        e.dma_start(out=P["hb"][:, :],
                    in_=hbase.rearrange("(p o) -> p o", o=1)).then_inc(s, 16)), 3, lane="wl")



def emit_dequant(b, PS, wp, ws, dq_own):
    idb = b.sb("dq_idb", [128, 128], BF16)
    b.step("gpsimd", lambda g: g.memset(idb[:, :], 1.0))
    tok_id = b.step("gpsimd", lambda g: g.affine_select(
        out=idb[:, :], in_=idb[:, :], compare_op=ALU.is_equal, fill=0.0, base=0,
        pattern=[[-1, 128]], channel_multiplier=1))
    wp_sb = b.sb("dq_wp", [128, 256], I32)
    ws_sb = b.sb("dq_ws", [128, 4], F32)
    lo = b.sb("dq_lo", [128, 256], I32)
    dqb = b.sb("dq_dqb", [128, 512], BF16)
    tp = b.sb("dq_tp", [128, 512], BF16)
    pst = PS["dq_ps"]
    dq_v = dq_own.rearrange("(ic p) o -> p ic o", p=128)
    tok_store = None
    tok_prev = None
    for oc in range(32):
        tok_l = b.dma("sync", lambda e, s, oc=oc: (
            e.dma_start(out=wp_sb[:, :], in_=wp[128 * oc:128 * (oc + 1), :]).then_inc(s, 16),
            e.dma_start(out=ws_sb[:, :], in_=ws[128 * oc:128 * (oc + 1), :]).then_inc(s, 16)),
            2, deps=[tok_prev], lane="dql")
        d0 = dqb[:, :]
        dl = bass.AP(d0.tensor, d0.offset, [list(d0.ap[0]), [2, 256]])
        dh = bass.AP(d0.tensor, dqb[:, 1:2].offset, [list(d0.ap[0]), [2, 256]])
        wv = brd(ws_sb[:, 0:4], [[1, 4], [0, 128]])
        dv = brd(d0, [[128, 4], [1, 128]])
        b.step("vector", lambda v: v.tensor_single_scalar(
            lo[:, :], wp_sb[:, :], 15, ALU.bitwise_and),
            deps=[tok_l, tok_store])
        b.step("vector", lambda v, dl=dl: v.tensor_single_scalar(
            dl, lo[:, :], 8, ALU.subtract))
        b.step("vector", lambda v: v.tensor_single_scalar(
            lo[:, :], wp_sb[:, :], 4, ALU.logical_shift_right))
        b.step("vector", lambda v, dh=dh: v.tensor_single_scalar(
            dh, lo[:, :], 8, ALU.subtract))
        tok_u = b.step("vector", lambda v, dv=dv, wv=wv: v.tensor_tensor(
            dv, dv, wv, ALU.mult))
        ev_toks = []
        tok_t = None
        for ic in range(4):
            tok_t = b.step("tensor", lambda t, ic=ic: t.transpose(
                pst[:, :], dqb[:, 128 * ic:128 * (ic + 1)], idb[:, :]),
                deps=[tok_u, tok_id] if ic == 0 else [ev_toks[-1]])
            tok_e = b.step("vector", lambda v, ic=ic: v.tensor_copy(
                tp[:, 128 * ic:128 * (ic + 1)], pst[:, :]), deps=[tok_t])
            ev_toks.append(tok_e)
        tok_prev = ev_toks[-1]
        tok_store = b.dma("sync", lambda e, s, oc=oc: (
            e.dma_start(
                out=dq_v[:, :, 128 * oc:128 * (oc + 1)],
                in_=tp[:, :].rearrange("p (ic o) -> p ic o", o=128)).then_inc(s, 16),),
            1, deps=[ev_toks[-1]], lane="dqs")
    return tok_store



NOC = 19
NTB = 16
NCH = 64


def emit_a2a(b, PS, d):
    winT = d["winT"]
    convw = d["convw"]
    convb = d["convb"]
    dtb = d["dtb"]
    xconv_d, gate_d, dt_d = d["xconv_d"], d["gate_d"], d["dt_d"]
    xgA, xgB = d["x_guards"]

    xvH = [a.rearrange("(r cc p) t -> p r cc t", p=128, cc=16)
           for a in d["xallH"]]

    convw_sb = b.sb("convw_sb", [128, 40], F32)
    convb_sb = b.sb("convb_sb", [128, 10], F32)
    dtb_sb = b.sb("dtb_sb", [16, 1], F32)
    tok_sml = b.dma("gpsimd", lambda e, s: (
        e.dma_start(out=convw_sb[:, :].rearrange("p (o k) -> p o k", k=4),
                    in_=convw.rearrange("(o p) k -> p o k", p=128)).then_inc(s, 16),
        e.dma_start(out=convb_sb[:, :],
                    in_=convb.rearrange("(o p) -> p o", p=128), allow_slow_non_contiguous=True).then_inc(s, 16),
        e.dma_start(out=dtb_sb[:, :], in_=dtb.rearrange("(p o) -> p o", o=1)).then_inc(s, 16)),
        3, lane="sml")

    wh = b.sb("wh", [128, 32 * 10 * 128], BF16)
    xtb = [b.sb(f"xt{i}", [128, 32 * 512], BF16) for i in range(2)]
    ext = b.sb("ext", [128, 515], F32)
    halos = b.sb("halos", [128, 30], F32)
    gevac = b.sb("gevac", [128, 512], BF16)
    cvout = b.sb("cvout", [128, 512], BF16)
    devac = b.sb("devac", [128, 512], F32)
    dwork = b.sb("dwork", [64, 512], F32)
    dwork2 = b.sb("dwork2", [16, 512], F32)
    ps1 = [PS["a2_ps1a"], PS["a2_ps1b"]]

    tok_hal = b.step("gpsimd", lambda g: g.memset(halos[:, :], 0.0))
    halves = [list(range(0, 10)), list(range(10, NOC))]
    ev_tok = {}
    st_g = None
    st_c = None
    st_d = None
    half_mm_barrier = None
    for hidx, ocs in enumerate(halves):
        no = len(ocs)
        tok_w = b.dma("gpsimd", lambda e, s, ocs=ocs, no=no: (
            e.dma_start(out=wh[:, 0:32 * no * 128].rearrange("p (cc o) -> p cc o", cc=32),
                        in_=winT.rearrange("(cc p) o -> p cc o", p=128)[:, :, 128 * ocs[0]:128 * (ocs[-1] + 1)]).then_inc(s, 16),),
            1, deps=[half_mm_barrier], lane="wl")
        tb_barrier = [None, None]
        for tb in range(NTB):
            xt = xtb[tb % 2]
            tok_xa = b.dma("sync", lambda e, s, tb=tb, xt=xt: (
                e.dma_start(out=xt[:, 0:16 * 512].rearrange("p (cc t) -> p cc t", cc=16),
                            in_=xvH[0][:, tb // 2, :, (tb % 2) * 512:(tb % 2) * 512 + 512]).then_inc(s, 16),),
                1, deps=[tb_barrier[tb % 2], xgA], lane="xl")
            tok_xb = b.dma("sync", lambda e, s, tb=tb, xt=xt: (
                e.dma_start(out=xt[:, 16 * 512:32 * 512].rearrange("p (cc t) -> p cc t", cc=16),
                            in_=xvH[1][:, tb // 2, :, (tb % 2) * 512:(tb % 2) * 512 + 512]).then_inc(s, 16),),
                1, deps=[xgB], lane="xl")
            for j, oc in enumerate(ocs):
                bank = (tb * no + j) % 2
                ps = ps1[bank]

                def mmA(t, j=j, no=no, ps=ps, xt=xt):
                    for cc in range(16):
                        i = t.matmul(ps[:, :], wh[:, (cc * no + j) * 128:(cc * no + j) * 128 + 128],
                                     xt[:, 512 * cc:512 * cc + 512],
                                     start=(cc == 0), stop=False)
                    return i

                def mmB(t, j=j, no=no, ps=ps, xt=xt):
                    for cc in range(16, 32):
                        i = t.matmul(ps[:, :], wh[:, (cc * no + j) * 128:(cc * no + j) * 128 + 128],
                                     xt[:, 512 * cc:512 * cc + 512],
                                     start=False, stop=(cc == 31))
                    return i
                b.step("tensor", mmA, deps=[tok_w, tok_xa, ev_tok.get(bank)])
                tok_mm = b.step("tensor", mmB, deps=[tok_xb])
                if oc <= 7:
                    tok_e = b.step("scalar", lambda sc, ps=ps: sc.copy(gevac[:, :], ps[:, :]),
                                   deps=[tok_mm, st_g])
                    ev_tok[bank] = tok_e
                    st_g = b.dma("sync", lambda e, s, oc=oc, tb=tb: (
                        e.dma_start(out=gate_d[128 * oc:128 * (oc + 1), 512 * tb:512 * (tb + 1)],
                                    in_=gevac[:, :]).then_inc(s, 16),), 1,
                        deps=[tok_e], lane="sg")
                elif oc <= 17:
                    ock = oc - 8
                    tok_e = b.step("scalar", lambda sc, ps=ps: sc.copy(ext[:, 3:515], ps[:, :]),
                                   deps=[tok_mm])
                    ev_tok[bank] = tok_e
                    tok_h = b.step("vector", lambda v, ock=ock: v.tensor_copy(
                        ext[:, 0:3], halos[:, 3 * ock:3 * ock + 3]),
                        deps=[tok_e, tok_hal, tok_sml])
                    b.step("vector", lambda v, ock=ock: v.tensor_scalar(
                        devac[:, :], ext[:, 0:512], convw_sb[:, 4 * ock:4 * ock + 1],
                        0.0, ALU.mult, ALU.add), deps=[tok_h, st_d])
                    for k in range(1, 4):
                        b.step("vector", lambda v, ock=ock, k=k: v.scalar_tensor_tensor(
                            devac[:, :], ext[:, k:k + 512], convw_sb[:, 4 * ock + k:4 * ock + k + 1],
                            devac[:, :], ALU.mult, ALU.add))
                    b.step("vector", lambda v, ock=ock: v.tensor_scalar(
                        devac[:, :], devac[:, :], convb_sb[:, ock:ock + 1], 0.0,
                        ALU.add, ALU.add))
                    tok_hs = b.step("vector", lambda v, ock=ock: v.tensor_copy(
                        halos[:, 3 * ock:3 * ock + 3], ext[:, 512:515]))
                    tok_sl = b.step("scalar", lambda sc: sc.activation(
                        cvout[:, :], devac[:, :], AF.Silu), deps=[tok_hs, st_c])
                    st_c = b.dma("sync", lambda e, s, ock=ock, tb=tb: (
                        e.dma_start(out=xconv_d[128 * ock:128 * (ock + 1), 512 * tb:512 * (tb + 1)],
                                    in_=cvout[:, :]).then_inc(s, 16),), 1,
                        deps=[tok_sl], lane="sc")
                else:
                    tok_e = b.step("scalar", lambda sc, ps=ps: sc.copy(devac[:, :], ps[:, :]),
                                   deps=[tok_mm, st_d])
                    ev_tok[bank] = tok_e
                    tok_v1 = b.step("vector", lambda v: v.tensor_scalar(
                        devac[0:16, :], devac[0:16, :], dtb_sb[0:16, 0:1], 0.0,
                        ALU.add, ALU.add), deps=[tok_e, tok_sml])
                    b.step("scalar", lambda sc: sc.activation(
                        dwork[0:16, :], devac[0:16, :], AF.Abs), deps=[tok_v1])
                    b.step("scalar", lambda sc: sc.activation(
                        dwork[0:16, :], dwork[0:16, :], AF.Exp, scale=-1.0))
                    tok_l1 = b.step("scalar", lambda sc: sc.activation(
                        dwork[0:16, :], dwork[0:16, :], AF.Ln, bias=1.0))
                    tok_l2 = b.step("scalar", lambda sc: sc.activation(
                        dwork2[0:16, :], devac[0:16, :], AF.Relu))
                    b.step("vector", lambda v: v.tensor_tensor(
                        devac[0:16, :], dwork[0:16, :], dwork2[0:16, :], ALU.add),
                        deps=[tok_l1, tok_l2])
                    b.step("vector", lambda v: v.tensor_single_scalar(
                        devac[0:16, :], devac[0:16, :], 100.0, ALU.min))
                    tok_d = b.step("vector", lambda v: v.tensor_single_scalar(
                        devac[0:16, :], devac[0:16, :], 0.001, ALU.max))
                    st_d = b.dma("sync", lambda e, s, tb=tb: (
                        e.dma_start(out=dt_d[:, 512 * tb:512 * (tb + 1)],
                                    in_=devac[0:16, :]).then_inc(s, 16),), 1,
                        deps=[tok_d], lane="sd")
            tb_barrier[tb % 2] = (b.esem["tensor"], b.ecnt["tensor"])
        half_mm_barrier = (b.esem["tensor"], b.ecnt["tensor"])
    return [st_g, st_c, st_d, (b.esem["vector"], b.ecnt["vector"]),
            (b.esem["scalar"], b.ecnt["scalar"])]


def emit_a2b(b, PS, d, ag_cb):
    xconv_d, gate_d, dt_d, y_d, scr2 = (d["xconv_d"], d["gate_d"], d["dt_d"],
                                        d["y_d"], d["scr2"])
    avec, dexp, nwv = d["avec"], d["dexp"], d["nwv"]

    idf, tok_idf = emit_ident(b, "idf", F32)
    idb, tok_idb = emit_ident(b, "idb", BF16)
    oneh = b.sb("oneh", [16, 2048], F32)
    b.step("gpsimd", lambda g: g.memset(oneh[:, :], 1.0))
    tok_oneh = b.step("gpsimd", lambda g: g.affine_select(
        out=oneh[:, :], in_=oneh[:, :], compare_op=ALU.is_equal, fill=0.0,
        base=0, pattern=[[-1, 16], [0, 128]], channel_multiplier=1))
    ones16 = b.sb("ones16", [16, 128], F32)
    tok_o16 = b.step("gpsimd", lambda g: g.memset(ones16[:, :], 1.0))
    I16f = b.sb("I16f", [16, 16], F32)
    b.step("gpsimd", lambda g: g.memset(I16f[:, :], 1.0))
    tok_i16 = b.step("gpsimd", lambda g: g.affine_select(
        out=I16f[:, :], in_=I16f[:, :], compare_op=ALU.is_equal, fill=0.0,
        base=0, pattern=[[-1, 16]], channel_multiplier=1))
    avec_sb = b.sb("avec_sb", [16, 1], F32)
    dcol_sb = b.sb("dcol_sb", [128, 8], F32)
    nw_sb = b.sb("nw_sb", [128, 8], F32)
    tok_sml = b.dma("gpsimd", lambda e, s: (
        e.dma_start(out=avec_sb[:, :], in_=avec.rearrange("(p o) -> p o", o=1)).then_inc(s, 16),
        e.dma_start(out=dcol_sb[:, :], in_=dexp.rearrange("(o p) -> p o", p=128), allow_slow_non_contiguous=True).then_inc(s, 16),
        e.dma_start(out=nw_sb[:, :], in_=nwv.rearrange("(o p) -> p o", p=128), allow_slow_non_contiguous=True).then_inc(s, 16)),
        3, lane="sml")

    BUF = []
    for i in range(2):
        BUF.append(dict(
            xc_b=b.sb(f"xc_b{i}", [128, 8 * 128], BF16),
            BC=b.sb(f"BC{i}", [128, 2 * 128], BF16),
            sta=b.sb(f"sta{i}", [128, 128], F32),
            staT=b.sb(f"staT{i}", [128, 128], F32),
            negl=b.sb(f"negl{i}", [48, 1], F32),
            rep=b.sb(f"rep{i}", [128, 2048], F32),
            repm=b.sb(f"repm{i}", [128, 2048], F32),
            e_all=b.sb(f"e_all{i}", [128, 2048], BF16),
            G=b.sb(f"G{i}", [128, 2048], BF16),
            Mt=b.sb(f"Mt{i}", [128, 2048], BF16),
            rhs_st=b.sb(f"rhs_st{i}", [128, 2048], BF16),
            rhs_yo=b.sb(f"rhs_yo{i}", [128, 2048], BF16),
            xdtT=b.sb(f"xdtT{i}", [128, 1024], BF16),
            BcT=b.sb(f"BcT{i}", [128, 128], BF16),
            scT=b.sb(f"scT{i}", [128, 128], BF16),
            cdrep=b.sb(f"cdrep{i}", [128, 16], F32),
            cdd=b.sb(f"cdd{i}", [16, 16], F32),
            csb=b.sb(f"csb{i}", [16, 128], F32),
        ))
    prevT = b.sb("prevT", [128, 1024], F32)
    prevTb = b.sb("prevTb", [128, 1024], BF16)
    cdexp = b.sb("cdexp", [128, 1024], F32)
    ysb = [b.sb(f"ysb{i}", [128, 1024], BF16) for i in range(2)]
    maskT = b.sb("maskT", [128, 128], F32)
    ps_sc = PS["a2_ps_sc"]
    ps_yp = [PS["a2_ps_ypa"], PS["a2_ps_ypb"]]
    ps_bc = PS["a2_ps_bc"]
    ps_st2 = PS["a2_ps_st2"]
    ps_trb = PS["b16"]
    ps_q3 = PS["a2_ps_q3"]

    b.step("gpsimd", lambda g: g.memset(maskT[:, :], 0.0))
    tok_msk = b.step("gpsimd", lambda g: g.affine_select(
        out=maskT[:, :], in_=maskT[:, :], compare_op=ALU.is_ge, fill=-1e30,
        base=0, pattern=[[1, 128]], channel_multiplier=-1))
    tok_prev = b.step("vector", lambda v: v.memset(prevT[:, :], 0.0))
    tok_prevb = b.step("vector", lambda v: v.memset(prevTb[:, :], 0.0))

    heads_done = {}
    prep_done = {}
    load_done = {}
    sce_done = {}
    state = {"tok_su": tok_prevb, "sub_tok": tok_prevb, "evp": {},
             "y_st": [None, None], "bc_guard": [None, None]}

    def emit_load(c):
        i = c % 2
        B_ = BUF[i]
        guard = heads_done.get(c - 2)
        ga = prep_a_done.get(c - 2, [])
        tok_ld = b.dma("sync", lambda e, s, c=c, B_=B_: (
            e.dma_start(out=B_["xc_b"][:, :].rearrange("p (cc l) -> p cc l", cc=8),
                        in_=xconv_d[0:1024].rearrange("(cc p) t -> p cc t", p=128)[:, :, 128 * c:128 * (c + 1)]).then_inc(s, 16),
            e.dma_start(out=B_["BC"][:, 0:128], in_=xconv_d[1024:1152, 128 * c:128 * (c + 1)]).then_inc(s, 16),
            e.dma_start(out=B_["BC"][:, 128:256], in_=xconv_d[1152:1280, 128 * c:128 * (c + 1)]).then_inc(s, 16)),
            3, deps=[guard] + ga, lane=f"p2x{i}")
        tok_dt = b.dma("sync", lambda e, s, c=c, B_=B_: (
            e.dma_start(out=B_["sta"][0:16, :], in_=dt_d[:, 128 * c:128 * (c + 1)]).then_inc(s, 16),),
            1, deps=[guard] + ga, lane=f"p2d{i}")
        load_done[c] = (tok_ld, tok_dt)

    prep_a_done = {}

    def emit_prep_a(c):
        i = c % 2
        B_ = BUF[i]
        tok_ld, tok_dt = load_done[c]
        b.step("vector", lambda v: v.tensor_scalar(
            B_["sta"][32:48, :], B_["sta"][0:16, :], avec_sb[0:16, 0:1], 0.0,
            ALU.mult, ALU.add), deps=[tok_dt, tok_sml])
        tok_cs = b.step("vector", lambda v: v.tensor_tensor_scan(
            B_["sta"][32:48, :], B_["sta"][32:48, :], B_["sta"][32:48, :], 0.0,
            ALU.add, ALU.bypass))
        tok_ng = b.step("vector", lambda v: v.tensor_copy(
            B_["negl"][32:48, 0:1], B_["sta"][32:48, 127:128]))
        tok_dec = b.step("scalar", lambda sc: sc.activation(
            B_["sta"][64:80, :], B_["sta"][32:48, :], AF.Exp,
            bias=B_["negl"][32:48, 0:1], scale=-1.0), deps=[tok_ng])
        tok_tr = b.step("tensor", lambda t: t.transpose(
            ps_sc[:, :], B_["sta"][:, :], idf[:, :]),
            deps=[tok_dec, tok_idf, tok_cs, sce_done.get(c - 1)])
        tok_trev = b.step("vector", lambda v: v.tensor_copy(
            B_["staT"][:, :], ps_sc[:, :]), deps=[tok_tr])
        tok_csb = b.step("vector", lambda v: v.tensor_copy(
            B_["csb"][:, :], B_["sta"][32:48, :]), deps=[tok_cs])
        tok_cdd = b.step("vector", lambda v: v.tensor_tensor(
            B_["cdd"][:, :], brd(B_["csb"][:, 127:128], [[0, 16]]),
            I16f[:, :], ALU.mult), deps=[tok_csb, tok_i16])
        bg = state["bc_guard"]
        tok_cdm = b.step("tensor", lambda t: t.matmul(
            ps_bc[:, 0:16], ones16[:, :], B_["cdd"][:, :],
            start=True, stop=True),
            deps=[tok_cdd, tok_o16, bg[0]])
        tok_cdr = b.step("vector", lambda v: v.tensor_copy(
            B_["cdrep"][:, :], ps_bc[:, 0:16]), deps=[tok_cdm])
        ev_bank = [tok_cdr, bg[1]]
        for r4 in range(4):
            bank = r4 % 2
            tok_rm = b.step("tensor", lambda t, r4=r4, bank=bank: [
                t.matmul(ps_bc[:, 512 * bank + 128 * hh:512 * bank + 128 * (hh + 1)],
                         oneh[:, 128 * (4 * r4 + hh):128 * (4 * r4 + hh + 1)],
                         B_["csb"][:, :], start=True, stop=True)
                for hh in range(4)][-1],
                deps=[tok_csb, tok_oneh, ev_bank[bank]])
            ev_bank[bank] = b.step("vector", lambda v, r4=r4, bank=bank: v.tensor_copy(
                B_["rep"][:, 512 * r4:512 * (r4 + 1)],
                ps_bc[:, 512 * bank:512 * bank + 512]),
                deps=[tok_rm])
        tok_rr = ev_bank[1]
        state["bc_guard"] = ev_bank
        tok_scm = b.step("tensor", lambda t: t.matmul(
            ps_sc[:, :], B_["BC"][:, 0:128], B_["BC"][:, 128:256],
            start=True, stop=True), deps=[tok_ld, tok_trev])
        tok_sce = b.step("vector", lambda v: v.tensor_copy(
            B_["scT"][:, :], ps_sc[:, :]), deps=[tok_scm])
        sce_done[c] = tok_sce
        B_["toks"] = dict(ld=tok_ld, trev=tok_trev, cdr=tok_cdr, rr=tok_rr,
                          sce=tok_sce)
        prep_a_done[c] = [tok_rr, tok_cdr]

    def emit_prep_b(c):
        i = c % 2
        B_ = BUF[i]
        tk = B_["toks"]
        tok_ld, tok_trev, tok_rr = tk["ld"], tk["trev"], tk["rr"]
        tok_cde = b.step("scalar", lambda sc: sc.activation(
            B_["cdrep"][:, :], B_["cdrep"][:, :], AF.Exp), deps=[tk["cdr"]])
        tok_ea = b.step("scalar", lambda sc: sc.activation(
            B_["e_all"][:, :], B_["rep"][:, :], AF.Exp), deps=[tok_rr])
        b.step("vector", lambda v: v.tensor_tensor(
            B_["repm"][:, :], B_["rep"][:, :], brd(maskT[:, :], [[0, 16], [1, 128]]),
            ALU.add), deps=[tok_rr, tok_msk])
        tok_m = None
        for hf in range(2):
            sl = slice(1024 * hf, 1024 * (hf + 1))
            tok_g1 = b.step("vector", lambda v, sl=sl, hf=hf: v.tensor_tensor(
                B_["repm"][:, sl], B_["repm"][:, sl],
                brd(B_["staT"][:, 32 + 8 * hf:40 + 8 * hf], [[1, 8], [0, 128]]),
                ALU.subtract), deps=[tok_trev])
            tok_g = b.step("scalar", lambda sc, sl=sl: sc.activation(
                B_["G"][:, sl], B_["repm"][:, sl], AF.Exp),
                deps=[tok_g1, tok_ea])
            tok_m = b.step("vector", lambda v, sl=sl, hf=hf: v.tensor_tensor(
                B_["Mt"][:, sl], B_["G"][:, sl],
                brd(B_["scT"][:, :], [[0, 8], [1, 128]]),
                ALU.mult), deps=[tok_g, tk["sce"]])
        prev_pb = prep_done.get(c - 1)
        tok_bt = b.step("tensor", lambda t: t.transpose(
            ps_trb[:, :], B_["BC"][:, 0:128], idb[:, :]),
            deps=[tok_ld, tok_idb, tk["sce"],
                  prev_pb[3] if prev_pb else None])
        tok_bte = b.step("vector", lambda v: v.tensor_copy(
            B_["BcT"][:, :], ps_trb[:, :]), deps=[tok_bt])
        tok_rs = b.step("vector", lambda v: v.tensor_tensor(
            B_["rhs_st"][:, :].rearrange("p (h n) -> p h n", h=16),
            brd(B_["BcT"][:, :], [[0, 16], [1, 128]]),
            brd(B_["staT"][:, 64:80], [[1, 16], [0, 128]]), ALU.mult),
            deps=[tok_bte, tok_trev])
        tok_ry = b.step("vector", lambda v: v.tensor_tensor(
            B_["rhs_yo"][:, :].rearrange("p (h l) -> p h l", h=16),
            brd(B_["BC"][:, 128:256], [[0, 16], [1, 128]]),
            B_["e_all"][:, :].rearrange("p (h l) -> p h l", h=16),
            ALU.mult), deps=[tok_ea, tok_ld])
        tok_xdt = tok_bte
        for hp in range(8):
            tok_t2 = b.step("tensor", lambda t, hp=hp: t.transpose(
                ps_trb[:, :], B_["xc_b"][:, 128 * hp:128 * (hp + 1)], idb[:, :]),
                deps=[tok_ld, tok_xdt])
            tok_xdt = b.step("vector", lambda v, hp=hp: v.tensor_tensor(
                B_["xdtT"][:, 128 * hp:128 * (hp + 1)].rearrange("p (h q) -> p h q", h=2),
                ps_trb[:, :].rearrange("p (h q) -> p h q", h=2),
                brd(B_["staT"][:, 2 * hp:2 * hp + 2], [[1, 2], [0, 64]]), ALU.mult),
                deps=[tok_t2, tok_trev])
        prep_done[c] = [tok_m, tok_rs, tok_ry, tok_xdt, tok_cde]

    def emit_heads(c):
        i = c % 2
        B_ = BUF[i]
        evp = state["evp"]
        pd = prep_done[c]
        tok_c2 = b.step("tensor", lambda t, B_=B_: [
            t.matmul(ps_st2[:, 64 * h:64 * h + 64],
                     B_["rhs_st"][:, 128 * h:128 * (h + 1)],
                     B_["xdtT"][:, 64 * h:64 * h + 64], start=True, stop=True)
            for h in range(16)][-1],
            deps=pd + [state["tok_su"]])
        for hp in range(8):
            ps = ps_yp[hp % 2]

            def mmab(t, hp=hp, ps=ps, B_=B_):
                for hh in range(2):
                    h = 2 * hp + hh
                    t.matmul(ps[64 * hh:64 * hh + 64, :],
                             B_["xdtT"][:, 64 * h:64 * h + 64],
                             B_["Mt"][:, 128 * h:128 * (h + 1)],
                             start=True, stop=False)
                    i2 = t.matmul(ps[64 * hh:64 * hh + 64, :],
                                  prevTb[:, 64 * h:64 * h + 64],
                                  B_["rhs_yo"][:, 128 * h:128 * (h + 1)],
                                  start=False, stop=True)
                return i2
            tok_ab = b.step("tensor", mmab,
                            deps=pd + [evp.get(hp % 2), state["sub_tok"]])
            evp[hp % 2] = b.step("vector", lambda v, hp=hp, ps=ps, B_=B_, i=i: v.scalar_tensor_tensor(
                ysb[i][:, 128 * hp:128 * (hp + 1)], B_["xc_b"][:, 128 * hp:128 * (hp + 1)],
                dcol_sb[:, hp:hp + 1], ps[:, :], ALU.mult, ALU.add),
                deps=[tok_ab, tok_sml, state["y_st"][i]])
        tok_cde = pd[4]
        b.step("vector", lambda v, B_=B_: v.tensor_copy(
            cdexp[:, :].rearrange("p (h q) -> p h q", q=64),
            brd(B_["cdrep"][:, :], [[1, 16], [0, 64]])), deps=[tok_cde])
        tok_su = b.step("vector", lambda v: v.tensor_tensor(
            prevT[:, :], prevT[:, :], cdexp[:, :], ALU.mult), deps=[tok_c2])
        tok_su = b.step("vector", lambda v: v.tensor_tensor(
            prevT[:, :], prevT[:, :], ps_st2[:, 0:1024], ALU.add), deps=[tok_su])
        state["tok_su"] = tok_su
        state["sub_tok"] = b.step("vector", lambda v: v.tensor_copy(
            prevTb[:, :], prevT[:, :]))
        heads_done[c] = b.step("vector", lambda v, B_=B_: v.tensor_copy(
            B_["staT"][0:1, 0:1], B_["staT"][0:1, 0:1]),
            deps=[evp.get(0), evp.get(1), state["sub_tok"]])
        state["y_st"][i] = b.dma("sync", lambda e, s, c=c, i=i: (
            e.dma_start(out=y_d.rearrange("(hp p) t -> p hp t", p=128)[:, :, 128 * c:128 * (c + 1)],
                        in_=ysb[i][:, :].rearrange("p (hp l) -> p hp l", hp=8)).then_inc(s, 16),),
            1, deps=[evp.get(0), evp.get(1)], lane=f"p2y{i}")

    emit_load(0)
    emit_load(1)
    emit_prep_a(0)
    emit_prep_a(1)
    emit_prep_b(0)
    for c in range(NCH):
        emit_heads(c)
        if c + 2 < NCH:
            emit_load(c + 2)
            emit_prep_a(c + 2)
        if c + 1 < NCH:
            emit_prep_b(c + 1)
    y_done = [state["y_st"][0], state["y_st"][1]]

    ones_b2 = b.sb("ones_b2", [128, 1], BF16)
    tok_ones2 = b.step("gpsimd", lambda g: g.memset(ones_b2[:, :], 1.0))
    NBP3 = 4
    ybuf = [b.sb(f"ybuf{i}", [128, 512], BF16) for i in range(NBP3)]
    gbuf = [b.sb(f"gbuf{i}", [128, 512], BF16) for i in range(NBP3)]
    sqb = [b.sb(f"sqb{i}", [128, 512], BF16) for i in range(NBP3)]
    youtb = [b.sb(f"youtb{i}", [128, 512], BF16) for i in range(NBP3)]
    rsacc = b.sb("rsacc", [1, 8192], F32)
    rrepA = b.sb("rrepA", [128, 8192], F32)

    use_guard = [None] * NBP3
    last_mm3 = None
    tok_ev3 = None
    for tb3 in range(16):
        for cc in range(8):
            i = (tb3 * 8 + cc) % NBP3
            tok_ly = b.dma("sync", lambda e, s, tb3=tb3, cc=cc, i=i: (
                e.dma_start(out=ybuf[i][:, :],
                            in_=y_d[128 * cc:128 * (cc + 1), 512 * tb3:512 * (tb3 + 1)]).then_inc(s, 16),
                e.dma_start(out=gbuf[i][:, :],
                            in_=gate_d[128 * cc:128 * (cc + 1), 512 * tb3:512 * (tb3 + 1)]).then_inc(s, 16)),
                2, deps=[y_done[0], y_done[1], use_guard[i]], lane=f"p3l{i % 2}")
            tok_sg = b.step("scalar", lambda sc, i=i: sc.activation(
                gbuf[i][:, :], gbuf[i][:, :], AF.Silu), deps=[tok_ly])
            tok_yg = b.step("vector", lambda v, i=i: v.tensor_tensor(
                sqb[i][:, :], ybuf[i][:, :], gbuf[i][:, :], ALU.mult),
                deps=[tok_sg])
            tok_sq = b.step("scalar", lambda sc, i=i: sc.activation(
                sqb[i][:, :], sqb[i][:, :], AF.Square), deps=[tok_yg])
            tok_mm = b.step("tensor", lambda t, cc=cc, i=i: t.matmul(
                ps_q3[0:1, 0:512], ones_b2[:, :], sqb[i][:, :],
                start=(cc == 0), stop=(cc == 7)),
                deps=[tok_sq, last_mm3, tok_ones2])
            last_mm3 = tok_mm
            use_guard[i] = tok_mm
        tok_ev3 = b.step("vector", lambda v, tb3=tb3: v.tensor_single_scalar(
            rsacc[0:1, 512 * tb3:512 * (tb3 + 1)], ps_q3[0:1, 0:512],
            1.0 / 1024.0, ALU.mult), deps=[last_mm3, tok_ev3])
    b.step("vector", lambda v: v.tensor_single_scalar(
        rsacc[0:1, :], rsacc[0:1, :], MAMBA_EPS, ALU.add), deps=[tok_ev3])
    tok_rv = b.step("vector", lambda v: v.reciprocal(rsacc[0:1, :], rsacc[0:1, :]))
    tok_rs3 = b.step("scalar", lambda sc: sc.activation(
        rsacc[0:1, :], rsacc[0:1, :], AF.Sqrt), deps=[tok_rv])
    tok_rw3 = b.dma("gpsimd", lambda e, s: (
        e.dma_start(out=scr2[4:5, 0:8192], in_=rsacc[0:1, :]).then_inc(s, 16),),
        1, deps=[tok_rs3], lane="r3")
    tok_rp3 = b.dma("gpsimd", lambda e, s: (
        e.dma_start(out=rrepA[:, :], in_=row_bcast(scr2[4:5, 0:8192], 8192)).then_inc(s, 16),),
        1, deps=[tok_rw3], lane="r3")

    ag_toks = []
    use_guard = [last_mm3] * NBP3
    st_guard = [None] * NBP3
    for cc in range(8):
        v_cc = d["ygv_cc"][cc]
        st_toks = []
        for tb3 in range(16):
            i = (cc * 16 + tb3) % NBP3
            tok_ly = b.dma("sync", lambda e, s, tb3=tb3, cc=cc, i=i: (
                e.dma_start(out=ybuf[i][:, :],
                            in_=y_d[128 * cc:128 * (cc + 1), 512 * tb3:512 * (tb3 + 1)]).then_inc(s, 16),
                e.dma_start(out=gbuf[i][:, :],
                            in_=gate_d[128 * cc:128 * (cc + 1), 512 * tb3:512 * (tb3 + 1)]).then_inc(s, 16)),
                2, deps=[use_guard[i]], lane=f"p3l{i % 2}")
            tok_sg = b.step("scalar", lambda sc, i=i: sc.activation(
                gbuf[i][:, :], gbuf[i][:, :], AF.Silu), deps=[tok_ly])
            tok_yg = b.step("vector", lambda v, i=i: v.tensor_tensor(
                sqb[i][:, :], ybuf[i][:, :], gbuf[i][:, :], ALU.mult),
                deps=[tok_sg, tok_rp3])
            use_guard[i] = tok_yg
            tok_o1 = b.step("vector", lambda v, tb3=tb3, i=i: v.tensor_tensor(
                youtb[i][:, :], sqb[i][:, :], rrepA[:, 512 * tb3:512 * (tb3 + 1)],
                ALU.mult), deps=[st_guard[i]])
            tok_o2 = b.step("vector", lambda v, cc=cc, i=i: v.tensor_scalar(
                youtb[i][:, :], youtb[i][:, :], nw_sb[:, cc:cc + 1], 0.0,
                ALU.mult, ALU.add), deps=[tok_sml])
            tok_st = b.dma("sync", lambda e, s, tb3=tb3, i=i, v_cc=v_cc: (
                e.dma_start(out=v_cc[:, 512 * tb3:512 * (tb3 + 1)],
                            in_=youtb[i][:, :]).then_inc(s, 16),), 1,
                deps=[tok_o2], lane=f"p3s{i % 2}")
            st_guard[i] = tok_st
            st_toks.append(tok_st)
        tok = ag_cb(cc, st_toks)
        if tok is not None:
            ag_toks.append(tok)
    return ag_toks



def emit_bigmm_block(b, P, lhsT_dram, cast, nk, rhs_sb, scale, biascol,
                     addin_dram, xres_in, xres_out, name, k0=0,
                     want_stats=True, acc_out_dram=None, final_out=False):
    ntok = TPC
    whb = P["whb"]
    oacc = P["oacc"]
    ps_guard = P.get("ps_mm_guard") or []
    ps_mm = P["ps_mm"]
    ps_s, ps_q = P["ps_s"], P["ps_q"]
    ones_f, ones_b = P["ones_f"], P["ones_b"]
    xsq2 = P["xsq2"]
    xc3 = P["xc3"]
    K = lhsT_dram.shape[0]
    st_tok = None
    mm_tok = {}
    prev_use = [None, None]
    st_stats = None
    for oc in range(32):
        buf = whb[oc % 2]
        eng = "gpsimd" if cast else "sync"
        tok_w = b.dma(eng, lambda e, s, oc=oc, buf=buf: (
            e.dma_start(out=buf[:, :].rearrange("p (kc o) -> p kc o", o=128),
                        in_=lhsT_dram.rearrange("(kc p) o -> p kc o", p=128)[:, k0:k0 + nk, 128 * oc:128 * (oc + 1)]).then_inc(s, 16),),
            1, deps=[prev_use[oc % 2], P.get("w_guard")], lane=f"{name}w{oc % 2}")
        for tb in range(2):
            bank = tb
            ps = ps_mm[bank]

            def mm(t, oc=oc, tb=tb, buf=buf, ps=ps):
                for kc in range(nk):
                    i = t.matmul(ps[:, :], buf[:, 128 * kc:128 * (kc + 1)],
                                 rhs_sb[:, ntok * kc + 512 * tb:ntok * kc + 512 * tb + 512],
                                 start=(kc == 0), stop=(kc == nk - 1))
                return i
            rhs_tok = P.get("rhs_tok")
            rhs_tok = rhs_tok if isinstance(rhs_tok, list) else [rhs_tok]
            tok_mm = b.step("tensor", mm,
                            deps=[tok_w, mm_tok.get(bank)] + rhs_tok +
                                 ([] if mm_tok.get(bank) else ps_guard))
            prev_use[oc % 2] = tok_mm
            if acc_out_dram is not None:
                tok_e = b.step("vector", lambda v, tb=tb, ps=ps: v.tensor_copy(
                    oacc[:, 512 * tb:512 * (tb + 1)], ps[:, :]),
                    deps=[tok_mm, st_tok] + ([] if st_tok else ps_guard))
                mm_tok[bank] = tok_e
            else:
                if biascol is not None:
                    tok_e = b.step("vector", lambda v, tb=tb, oc=oc, ps=ps: v.tensor_scalar(
                        oacc[:, 512 * tb:512 * (tb + 1)], ps[:, :],
                        biascol[:, oc:oc + 1], 0.0, ALU.add, ALU.add),
                        deps=[tok_mm, st_tok, P.get("bias_tok")] +
                             ([] if st_tok else ps_guard))
                    if scale != 1.0:
                        tok_e = b.step("vector", lambda v, tb=tb: v.tensor_single_scalar(
                            oacc[:, 512 * tb:512 * (tb + 1)],
                            oacc[:, 512 * tb:512 * (tb + 1)], scale, ALU.mult))
                else:
                    tok_e = b.step("vector", lambda v, tb=tb, ps=ps: v.tensor_single_scalar(
                        oacc[:, 512 * tb:512 * (tb + 1)], ps[:, :], scale, ALU.mult),
                        deps=[tok_mm, st_tok] + ([] if st_tok else ps_guard))
                mm_tok[bank] = tok_e
        if acc_out_dram is not None:
            st_tok = b.dma("sync", lambda e, s, oc=oc: (
                e.dma_start(out=acc_out_dram[128 * oc:128 * (oc + 1), :],
                            in_=oacc[:, :]).then_inc(s, 16),), 1,
                deps=[mm_tok[0], mm_tok[1]], lane=f"{name}s")
            continue
        tok_add = [mm_tok[0], mm_tok[1]]
        if addin_dram is not None:
            tok_la = b.dma("sync", lambda e, s, oc=oc: (
                e.dma_start(out=xc3[:, :], in_=addin_dram[128 * oc:128 * (oc + 1), :]).then_inc(s, 16),),
                1, deps=[P[f"{name}_a_guard"]], lane=f"{name}a")
            tok_add = [b.step("vector", lambda v: v.tensor_tensor(
                oacc[:, :], oacc[:, :], xc3[:, :], ALU.add),
                deps=[tok_la] + tok_add)]
        tok_lx = b.dma("sync", lambda e, s, oc=oc: (
            e.dma_start(out=xc3[:, :], in_=xres_in[128 * oc:128 * (oc + 1), :]).then_inc(s, 16),),
            1, deps=tok_add, lane=f"{name}a")
        tok_x2 = b.step("vector", lambda v: v.tensor_tensor(
            oacc[:, :], oacc[:, :], xc3[:, :], ALU.add), deps=[tok_lx])
        P[f"{name}_a_guard"] = tok_x2
        st_tok = b.dma("sync", lambda e, s, oc=oc: (
            e.dma_start(out=xres_out[128 * oc:128 * (oc + 1), :], in_=oacc[:, :]).then_inc(s, 16),),
            1, deps=[tok_x2], lane=f"{name}s")
        if want_stats:
            tok_sq = b.step("scalar", lambda sc: sc.activation(
                xsq2[:, :], oacc[:, :], AF.Square), deps=[tok_x2, st_stats])

            def mms(t, oc=oc):
                for tb in range(2):
                    t.matmul(ps_s[0:1, 512 * tb:512 * (tb + 1)], ones_f[:, :],
                             oacc[:, 512 * tb:512 * (tb + 1)], start=(oc == 0), stop=(oc == 31))
                for tb in range(2):
                    i = t.matmul(ps_q[0:1, 512 * tb:512 * (tb + 1)], ones_b[:, :],
                                 xsq2[:, 512 * tb:512 * (tb + 1)], start=(oc == 0), stop=(oc == 31))
                return i
            st_stats = b.step("tensor", mms, deps=[tok_sq, tok_x2, P["tok_ones"]])
    P["ps_mm_guard"] = [(b.esem["vector"], b.ecnt["vector"]),
                        (b.esem["scalar"], b.ecnt["scalar"]), st_tok]
    return st_tok, st_stats


def emit_ln_apply(b, P, xsrc_dram, dst_sb_or_none, dst_dram_or_none, tok_rep,
                  name, prev_guard=None, dst2_dram=None):
    ntok = TPC
    xcb = P["xcb"]
    rrep2, brep2 = P["rrep2"], P["brep2"]
    xlt = P["xlt"]
    tok_prev = None
    prev_ap = [prev_guard, prev_guard]
    last = None
    half_tok = None
    for cc in range(32):
        buf = xcb[cc % 2]
        tok_l = b.dma("sync", lambda e, s, cc=cc, buf=buf: (
            e.dma_start(out=buf[:, :], in_=xsrc_dram[128 * cc:128 * (cc + 1), :]).then_inc(s, 16),),
            1, deps=[prev_ap[cc % 2]], lane=f"ld{cc % 2}")
        b.step("vector", lambda v, buf=buf: v.tensor_tensor(
            buf[:, :], buf[:, :], rrep2[:, :], ALU.mult), deps=[tok_l, tok_rep])
        if dst_sb_or_none is not None:
            tok_a = b.step("vector", lambda v, cc=cc, buf=buf: v.tensor_tensor(
                dst_sb_or_none[:, ntok * cc:ntok * (cc + 1)], buf[:, :], brep2[:, :],
                ALU.subtract))
            last = tok_a
            prev_ap[cc % 2] = tok_a
            if dst_dram_or_none is not None:
                last = b.dma("sync", lambda e, s, cc=cc: (
                    e.dma_start(out=dst_dram_or_none[128 * cc:128 * (cc + 1), :],
                                in_=dst_sb_or_none[:, ntok * cc:ntok * (cc + 1)]).then_inc(s, 16),),
                    1, deps=[tok_a], lane="st2")
        else:
            tok_a = b.step("vector", lambda v, buf=buf: v.tensor_tensor(
                xlt[:, :], buf[:, :], brep2[:, :], ALU.subtract), deps=[tok_prev])
            tok_st = b.dma("sync", lambda e, s, cc=cc: (
                e.dma_start(out=dst_dram_or_none[128 * cc:128 * (cc + 1), :],
                            in_=xlt[:, :]).then_inc(s, 16),), 1,
                deps=[tok_a], lane="st2")
            tok_prev = tok_st
            prev_ap[cc % 2] = tok_a
            last = tok_st
            if cc == 15:
                half_tok = tok_st
    return [half_tok, last] if dst_sb_or_none is None else last


def emit_perm_gather(b, P, src_dram, pidx_sb, dst_sb, dep, name):
    last = [None, None]
    for ic in range(32):
        def g(e, s, ic=ic):
            e.indirect_dma_start(
                out=dst_sb[:, TPC * ic:TPC * (ic + 1)],
                out_offset=None,
                in_=src_dram,
                in_offset=bass.IndirectOffsetOnAxis(pidx_sb[:, ic:ic + 1], 0),
            ).then_inc(s, 16)
        last[ic % 2] = b.dma("gpsimd", g, 1,
                             deps=[dep, P.get("pg_guard")] if ic < 2 else [last[ic % 2]],
                             lane=f"pg{ic % 2}")
    return last


def emit_idx_gather(b, src_dram, idx_sb, dst_sb, k0, nk, deps, lane):
    last = [None, None]
    for i in range(nk):
        kc = k0 + i
        src = src_dram[kc] if isinstance(src_dram, list) else src_dram

        def g(e, s, i=i, kc=kc, src=src):
            e.indirect_dma_start(
                out=dst_sb[:, TPC * i:TPC * (i + 1)],
                out_offset=None,
                in_=src,
                in_offset=bass.IndirectOffsetOnAxis(idx_sb[:, kc:kc + 1], 0),
            ).then_inc(s, 16)
        last[i % 2] = b.dma("gpsimd", g, 1,
                            deps=list(deps) if i < 2 else [last[i % 2]],
                            lane=f"{lane}{i % 2}")
    return last



def emit_a3(b, PS, d):
    yall = d["yall"]
    x1 = d["x1"]
    woT = d["woT"]
    wcat = d["wcat"]
    hbase = d["hbase"]
    wbr = d["wbr"]
    deqT = d["deqT"]
    fbias = d["fbias"]
    perm = d["perm"]
    yidx = d["yidx"]
    x4o = d["x4"]
    xln5o = d["xln5_own"]
    y_guardA = d["y_guardA"]
    y_guardB = d["y_guardB"]
    dq_guard = d["dq_guard"]

    x2_d = b.dram("x2_d", [D, TPC], F32)
    x3_d = b.dram("x3_d", [D, TPC], F32)
    acc_d = b.dram("acc_d", [D, TPC], F32)
    xln4_d = b.dram("xln4_d", [D, TPC], BF16)

    big2 = b.sb("big2", [128, 32 * TPC], BF16)
    P = make_shared_pool(b, dict(ps_s=PS["pool_ps_s"], ps_q=PS["pool_ps_q"],
                                 ps_log=PS["psc"], ps_misc=PS["psc"]),
                         with_ffn0=True, fbf=big2)
    P["ps_mm"] = [PS["psa"], PS["psb"]]
    P["whb"] = [b.sb(f"whb{i}", [128, 32 * 128], BF16) for i in range(2)]
    P["oacc"] = b.sb("oacc", [128, TPC], F32)
    P["f0_acc"] = P["oacc"]
    P["f0_tmp"] = P["xcb"][0]
    load_ffn0_weights(b, P, wcat, wbr, hbase)
    fb_sb = b.sb("fb_sb", [128, 32], F32)
    pidx = b.sb("pidx", [128, 32], I32)
    yidx_sb = b.sb("yidx_sb", [128, 64], I32)
    tok_fb = b.dma("gpsimd", lambda e, s: (
        e.dma_start(out=fb_sb[:, :], in_=fbias.rearrange("(o p) -> p o", p=128),
                    allow_slow_non_contiguous=True).then_inc(s, 16),
        e.dma_start(out=pidx[:, :], in_=perm.rearrange("(c p) -> p c", p=128),
                    allow_slow_non_contiguous=True).then_inc(s, 16),
        e.dma_start(out=yidx_sb[:, :], in_=yidx).then_inc(s, 16)), 3, lane="fb")

    tok_y1 = emit_idx_gather(b, yall, yidx_sb, big2, 0, 32,
                             [tok_fb, y_guardA], "yl")
    P["rhs_tok"] = tok_y1
    P["w_guard"] = d["woT_guard"]
    st1, _ = emit_bigmm_block(b, P, woT, False, 32, big2, 1.0, None, None, None,
                              None, "op1", k0=0, want_stats=False,
                              acc_out_dram=acc_d)
    tok_y2 = emit_idx_gather(b, yall, yidx_sb, big2, 32, 32,
                             tok_y1 + [y_guardB,
                              (b.esem["tensor"], b.ecnt["tensor"])], "yl")
    P["rhs_tok"] = tok_y2
    P["op2_a_guard"] = st1
    st2, stats3 = emit_bigmm_block(b, P, woT, False, 32, big2, 1.0, None, acc_d,
                                   x1, x2_d, "op2", k0=32)
    tok_rep3 = emit_ln_finalize(b, P["ps_s"], P["ps_q"], P["stat"], P["scr"], 0,
                                P["rrep"], P["brep"], stats3, TPC, srep=P["srep"])
    P["x2_ready"] = st2
    P["rrep2"], P["brep2"] = P["rrep"], P["brep"]
    tok_xln3 = emit_ln_apply(b, P, x2_d, P["xln"], None, tok_rep3, "l3",
                             prev_guard=st2)
    ntok = TPC
    obuf = [P["xc3"], P["xc3"]]
    state = {"tok_mm": None, "tok_use": [None, None], "tok_st": None,
             "tok_l": {}}
    st_toks = []

    def pre_load(cc):
        if cc > 31 or cc in state["tok_l"]:
            return
        state["tok_l"][cc] = b.dma("sync", lambda e, s, cc=cc: (
            e.dma_start(out=obuf[cc % 2][:, :],
                        in_=x2_d[128 * cc:128 * (cc + 1), :]).then_inc(s, 16),),
            1, deps=[state["tok_use"][cc % 2], P["x2_ready"],
                     tok_xln3 if cc < 2 else None], lane="ocb")

    def out_cb(cc, f_ap, dep):
        pre_load(cc)
        tok_l = state["tok_l"].pop(cc)
        tok1 = b.step("vector", lambda v, cc=cc: v.scalar_tensor_tensor(
            P["x1sb"][:, :], f_ap[:, :], 0.5, obuf[cc % 2][:, :],
            ALU.mult, ALU.add),
            deps=[tok_l, dep, state["tok_mm"], state["tok_st"]])
        state["tok_use"][0] = state["tok_use"][1] = tok1
        pre_load(cc + 1)
        tok_st = b.dma("sync", lambda e, s, cc=cc: (
            e.dma_start(out=x3_d[128 * cc:128 * (cc + 1), :], in_=P["x1sb"][:, :]).then_inc(s, 16),),
            1, deps=[tok1], lane="st1")
        state["tok_st"] = tok_st
        st_toks.append(tok_st)
        tok_sq = b.step("scalar", lambda sc: sc.activation(
            P["xsq2"][:, :], P["x1sb"][:, :], AF.Square), deps=[tok1, state["tok_mm"]])

        def flush(cc=cc, tok_sq=tok_sq, tok1=tok1):
            def mm(t, cc=cc):
                for tb in range(2):
                    t.matmul(P["ps_s"][0:1, 512 * tb:512 * (tb + 1)], P["ones_f"][:, :],
                             P["x1sb"][:, 512 * tb:512 * (tb + 1)], start=(cc == 0), stop=(cc == 31))
                for tb in range(2):
                    i = t.matmul(P["ps_q"][0:1, 512 * tb:512 * (tb + 1)], P["ones_b"][:, :],
                                 P["xsq2"][:, 512 * tb:512 * (tb + 1)], start=(cc == 0), stop=(cc == 31))
                return i
            state["tok_mm"] = b.step("tensor", mm, deps=[tok_sq, tok1])
        P["pending_stats"] = flush
        return tok1

    P["ps_y"] = PS["psa"]
    emit_ffn0(b, P, P["xln"], tok_xln3, P["srep"], tok_rep3, out_cb)
    stats4 = state["tok_mm"]
    tok_rep4 = emit_ln_finalize(b, P["ps_s"], P["ps_q"], P["stat"], P["scr"], 4,
                                P["rrep"], P["brep"], stats4, TPC)
    tok_xln4 = emit_ln_apply(b, P, x3_d, None, xln4_d, tok_rep4, "l4",
                             prev_guard=st_toks[-1])
    P["pg_guard"] = tok_fb
    tok_pg = emit_perm_gather(b, P, xln4_d, pidx, big2, tok_xln4[-1], "pg")
    P["rhs_tok"] = tok_pg
    P["f1_a_guard"] = None
    P["bias_tok"] = tok_fb
    P["w_guard"] = dq_guard
    st4, stats5 = emit_bigmm_block(b, P, deqT, False, 32, big2, 0.5, fb_sb,
                                   None, x3_d, x4o, "f1")
    tok_rep5 = emit_ln_finalize(b, P["ps_s"], P["ps_q"], P["stat"], P["scr"], 8,
                                P["rrep"], P["brep"], stats5, TPC)
    tok_xln5 = emit_ln_apply(b, P, x4o, None, xln5o, tok_rep5, "l5",
                             prev_guard=st4)
    return tok_xln5, st4



def emit_c(b, PS, d):
    aall = d["aall"]
    x4 = d["x4"]
    wo = d["wo"]
    bo = d["bo"]
    deqT = d["deqT"]
    fbias = d["fbias"]
    perm = d["perm"]
    aidx = d["aidx"]
    x6o = d["x6"]
    a_guard = d["a_guard"]
    dq_guard = d["dq_guard"]

    x5_d = b.dram("x5_d", [D, TPC], F32)
    xln6_d = b.dram("xln6_d", [D, TPC], BF16)

    big2 = b.sb("big2", [128, 32 * TPC], BF16)
    P = make_shared_pool(b, dict(ps_s=PS["pool_ps_s"], ps_q=PS["pool_ps_q"]),
                         with_ffn0=False)
    P["ps_mm"] = [PS["psa"], PS["psb"]]
    P["whb"] = [b.sb(f"whb{i}", [128, 32 * 128], BF16) for i in range(2)]
    P["oacc"] = b.sb("oacc", [128, TPC], F32)
    fb_sb = b.sb("fb_sb", [128, 32], F32)
    bo_sb = b.sb("bo_sb", [128, 32], F32)
    pidx = b.sb("pidx", [128, 32], I32)
    aidx_sb = b.sb("aidx_sb", [128, 32], I32)
    tok_fb = b.dma("gpsimd", lambda e, s: (
        e.dma_start(out=fb_sb[:, :], in_=fbias.rearrange("(o p) -> p o", p=128),
                    allow_slow_non_contiguous=True).then_inc(s, 16),
        e.dma_start(out=bo_sb[:, :], in_=bo.rearrange("(o p) -> p o", p=128),
                    allow_slow_non_contiguous=True).then_inc(s, 16),
        e.dma_start(out=pidx[:, :], in_=perm.rearrange("(c p) -> p c", p=128),
                    allow_slow_non_contiguous=True).then_inc(s, 16),
        e.dma_start(out=aidx_sb[:, :], in_=aidx).then_inc(s, 16)), 4, lane="fb")
    tok_af = emit_idx_gather(b, aall, aidx_sb, big2, 0, 32,
                             [tok_fb, a_guard], "yl")
    P["rhs_tok"] = tok_af
    P["ao_a_guard"] = None
    P["bias_tok"] = tok_fb
    P["w_guard"] = d["wo_guard"]
    st1, stats6 = emit_bigmm_block(b, P, wo, False, 32, big2, 1.0, bo_sb,
                                   None, x4, x5_d, "ao")
    tok_rep6 = emit_ln_finalize(b, P["ps_s"], P["ps_q"], P["stat"], P["scr"], 0,
                                P["rrep"], P["brep"], stats6, TPC)
    P["rrep2"], P["brep2"] = P["rrep"], P["brep"]
    tok_xln6 = emit_ln_apply(b, P, x5_d, None, xln6_d, tok_rep6, "l6",
                             prev_guard=st1)
    P["pg_guard"] = tok_fb
    tok_pg = emit_perm_gather(b, P, xln6_d, pidx, big2, tok_xln6[-1], "pg")
    P["rhs_tok"] = tok_pg
    P["w_guard"] = dq_guard
    st2, _ = emit_bigmm_block(b, P, deqT, False, 32, big2, 0.5, fb_sb,
                              None, x5_d, x6o, "f1", want_stats=False)
    return st2



def emit_b(b, PS, d):
    wqkv = d["wqkv"]
    cosf = d["cosf"]
    sinf = d["sinf"]
    aov = d["aov"]
    xgA, xgB = d["x_guards"]

    xvH = [a.rearrange("(r cc p) t -> p r cc t", p=128, cc=16)
           for a in d["xall5H"]]
    scr = b.dram("scrB", [4, T], F32)

    wsb = b.sb("wsb", [128, 32 * 768], BF16)
    xt = b.sb("xt", [128, 32 * 256], BF16)
    Q = b.sb("Q", [128, 4 * T], BF16)
    K = b.sb("K", [128, T], BF16)
    Vf = b.sb("Vf", [128, T], BF16)
    Vt = b.sb("Vt", [128, 64 * 128], BF16)
    Pt = b.sb("Pt", [128, 2048], BF16)
    msk = b.sb("msk", [128, 4 * 512], BF16)
    sumr = b.sb("sumr", [1, 512], F32)
    rrep = b.sb("rrepB", [128, 512], F32)
    aob = b.sb("aob", [128, 512], BF16)
    nrm = b.sb("nrm", [1, 128], F32)
    bpp = b.sb("bpp", [128, 1], F32)
    ones_b = b.sb("ones_bB", [128, 1], BF16)
    idb = b.sb("idbB", [128, 128], BF16)
    ps_sc = PS["psbig"]
    ps_o = [PS["psa"], PS["psb"]]
    ps_sum = PS["b_ps_sum"]
    ps_tb = PS["b16"]

    tok_on = b.step("gpsimd", lambda g: g.memset(ones_b[:, :], 1.0))
    b.step("gpsimd", lambda g: g.memset(idb[:, :], 1.0))
    tok_id = b.step("gpsimd", lambda g: g.affine_select(
        out=idb[:, :], in_=idb[:, :], compare_op=ALU.is_equal, fill=0.0, base=0,
        pattern=[[-1, 128]], channel_multiplier=1))
    tok_mg = None
    for j in range(4):
        b.step("gpsimd", lambda g, j=j: g.memset(msk[:, 512 * j:512 * (j + 1)], 1.0))
        tok_mg = b.step("gpsimd", lambda g, j=j: g.affine_select(
            out=msk[:, 512 * j:512 * (j + 1)], in_=msk[:, 512 * j:512 * (j + 1)],
            compare_op=ALU.is_ge, fill=0.0, base=-128 * j, pattern=[[1, 512]],
            channel_multiplier=-1))
    tok_w = b.dma("gpsimd", lambda e, s: (
        e.dma_start(out=wsb[:, :].rearrange("p (cc o) -> p cc o", o=768),
                    in_=wqkv.rearrange("(cc p) o -> p cc o", p=128)).then_inc(s, 16),),
        1, lane="wl")

    ev = {}
    prev_x = None
    for tb in range(32):
        tok_xa = b.dma("sync", lambda e, s, tb=tb: (
            e.dma_start(out=xt[:, 0:16 * 256].rearrange("p (cc t) -> p cc t", cc=16),
                        in_=xvH[0][:, tb // 4, :, (tb % 4) * 256:(tb % 4) * 256 + 256]).then_inc(s, 16),),
            1, deps=[prev_x, xgA], lane="xl")
        tok_xb = b.dma("sync", lambda e, s, tb=tb: (
            e.dma_start(out=xt[:, 16 * 256:32 * 256].rearrange("p (cc t) -> p cc t", cc=16),
                        in_=xvH[1][:, tb // 4, :, (tb % 4) * 256:(tb % 4) * 256 + 256]).then_inc(s, 16),),
            1, deps=[xgB], lane="xl")
        for oc in range(6):
            bank = oc % 2
            ps = ps_o[bank]

            def mmA(t, oc=oc, ps=ps):
                for cc in range(16):
                    i = t.matmul(ps[:, 0:256], wsb[:, 768 * cc + 128 * oc:768 * cc + 128 * (oc + 1)],
                                 xt[:, 256 * cc:256 * (cc + 1)],
                                 start=(cc == 0), stop=False)
                return i

            def mmB(t, oc=oc, ps=ps):
                for cc in range(16, 32):
                    i = t.matmul(ps[:, 0:256], wsb[:, 768 * cc + 128 * oc:768 * cc + 128 * (oc + 1)],
                                 xt[:, 256 * cc:256 * (cc + 1)],
                                 start=False, stop=(cc == 31))
                return i
            b.step("tensor", mmA, deps=[tok_w, tok_xa, ev.get(bank)])
            tok_mm = b.step("tensor", mmB, deps=[tok_xb])
            if oc < 4:
                dst = Q[:, T * oc + 256 * tb:T * oc + 256 * (tb + 1)]
            elif oc == 4:
                dst = K[:, 256 * tb:256 * (tb + 1)]
            else:
                dst = Vf[:, 256 * tb:256 * (tb + 1)]
            ev[bank] = b.step("scalar", lambda sc, dst=dst, ps=ps: sc.copy(dst, ps[:, 0:256]),
                              deps=[tok_mm])
        prev_x = (b.esem["tensor"], b.ecnt["tensor"])
    v_proj = [ev.get(0), ev.get(1)]

    tok_vt = None
    for kb in range(64):
        tok_t = b.step("tensor", lambda t, kb=kb: t.transpose(
            ps_tb[:, :], Vf[:, 128 * kb:128 * (kb + 1)], idb[:, :]),
            deps=[v_proj[0], v_proj[1], tok_id] if kb == 0 else [tok_vt])
        tok_vt = b.step("vector", lambda v, kb=kb: v.tensor_copy(
            Vt[:, 128 * kb:128 * (kb + 1)], ps_tb[:, :]), deps=[tok_t])
    v_vt = tok_vt

    cos2 = wsb[0:128, 0:T]
    sin2 = wsb[0:128, T:2 * T]
    tmp3 = wsb[0:64, 2 * T:3 * T]
    tok_cs2 = b.dma("gpsimd", lambda e, s: (
        e.dma_start(out=cos2[0:64, 0:T], in_=cosf).then_inc(s, 16),
        e.dma_start(out=cos2[64:128, 0:T], in_=cosf).then_inc(s, 16),
        e.dma_start(out=sin2[0:64, 0:T], in_=sinf).then_inc(s, 16),
        e.dma_start(out=sin2[64:128, 0:T], in_=sinf).then_inc(s, 16)), 4,
        deps=[v_proj[0], v_proj[1], (b.esem["tensor"], b.ecnt["tensor"])], lane="wl")
    tok_r = v_vt
    for hch in range(5):
        src = Q[:, T * hch:T * (hch + 1)] if hch < 4 else K[:, :]
        tok_r = b.step("vector", lambda v, src=src: v.tensor_tensor(
            Vf[0:64, 0:T], src[0:64, 0:T], cos2[0:64, 0:T], ALU.mult),
            deps=[tok_r, v_vt, tok_cs2])
        tok_r = b.step("vector", lambda v, src=src: v.tensor_tensor(
            Vf[64:128, 0:T], src[0:64, 0:T], sin2[0:64, 0:T], ALU.mult))
        tok_r = b.step("vector", lambda v, src=src: v.tensor_tensor(
            tmp3[0:64, 0:T], src[64:128, 0:T], sin2[64:128, 0:T], ALU.mult))
        tok_r = b.step("vector", lambda v, src=src: v.tensor_tensor(
            src[64:128, 0:T], src[64:128, 0:T], cos2[64:128, 0:T], ALU.mult))
        tok_r = b.step("vector", lambda v, src=src: v.tensor_tensor(
            src[0:64, 0:T], Vf[0:64, 0:T], tmp3[0:64, 0:T], ALU.subtract))
        tok_r = b.step("vector", lambda v, src=src: v.tensor_tensor(
            src[64:128, 0:T], Vf[64:128, 0:T], src[64:128, 0:T], ALU.add))
    v_rope = tok_r

    sq = b.sb("sqB", [128, 512], BF16)
    tok_nm = None
    tok_q2 = None
    for hch in range(5):
        src = Q[:, T * hch:T * (hch + 1)] if hch < 4 else K[:, :]
        for tb in range(16):
            idx = hch * 16 + tb
            tok_q = b.step("scalar", lambda sc, src=src, tb=tb: sc.activation(
                sq[:, :], src[:, 512 * tb:512 * (tb + 1)], AF.Square),
                deps=[v_rope, tok_q2])
            tok_q2 = b.step("tensor", lambda t: t.matmul(
                ps_sum[0:1, :], ones_b[:, :], sq[:, :], start=True, stop=True),
                deps=[tok_q, tok_on, tok_nm])
            tok_nm = b.step("vector", lambda v, idx=idx: v.tensor_reduce(
                nrm[0:1, idx:idx + 1], ps_sum[0:1, 0:512], AX.X, ALU.max),
                deps=[tok_q2])
    tok_b1 = b.step("vector", lambda v: v.tensor_reduce(
        nrm[0:1, 80:81], nrm[0:1, 0:64], AX.X, ALU.max), deps=[tok_nm])
    tok_b2 = b.step("vector", lambda v: v.tensor_reduce(
        nrm[0:1, 81:82], nrm[0:1, 64:80], AX.X, ALU.max))
    tok_b3 = b.step("scalar", lambda sc: sc.activation(
        nrm[0:1, 80:82], nrm[0:1, 80:82], AF.Sqrt), deps=[tok_b2])
    tok_b4 = b.step("vector", lambda v: v.tensor_tensor(
        nrm[0:1, 82:83], nrm[0:1, 80:81], nrm[0:1, 81:82], ALU.mult), deps=[tok_b3])
    tok_b5 = b.step("vector", lambda v: v.tensor_single_scalar(
        nrm[0:1, 82:83], nrm[0:1, 82:83], -1.0 / (128.0 ** 0.5), ALU.mult))
    tok_b6 = b.dma("gpsimd", lambda e, s: (
        e.dma_start(out=scr[0:1, 0:1], in_=nrm[0:1, 82:83]).then_inc(s, 16),), 1,
        deps=[tok_b5], lane="bb")
    tok_bpp = b.dma("gpsimd", lambda e, s: (
        e.dma_start(out=bpp[:, :], in_=row_bcast(scr[0:1, 0:1], 1)).then_inc(s, 16),),
        1, deps=[tok_b6], lane="bb")

    ISQ = 1.0 / (128.0 ** 0.5)
    tok_o_ev = {}
    tok_ao = None
    tok_rrep = None
    tok_exp_h = [None, None]
    tok_cons_h = [None, None]
    tok_se = tok_nm
    for qb in range(16):
        for h in range(4):
            bank = (qb * 4 + h) % 2
            pso = ps_o[bank]
            qsl = Q[:, T * h + 512 * qb:T * h + 512 * (qb + 1)]
            nsb = 2 * (qb + 1)
            exp_s = {}
            for s in range(nsb + 1):
                if s < nsb:
                    half = s % 2
                    psh = ps_sc[:, 1024 * half:1024 * (half + 1)]
                    pth = Pt[:, 1024 * half:1024 * (half + 1)]

                    def mmsc(t, s=s, qsl=qsl, psh=psh):
                        for j in range(2):
                            kb = 2 * s + j
                            i = t.matmul(psh[:, 512 * j:512 * (j + 1)],
                                         K[:, 128 * kb:128 * (kb + 1)], qsl,
                                         start=True, stop=True)
                        return i
                    tok_sc = b.step("tensor", mmsc,
                                    deps=[v_rope, tok_exp_h[half], tok_bpp])
                    tok_exp = b.step("scalar", lambda sc, psh=psh, pth=pth: sc.activation(
                        pth, psh, AF.Exp, bias=bpp[:, 0:1], scale=ISQ),
                        deps=[tok_sc, tok_bpp, tok_cons_h[half]])
                    if s >= nsb - 2:
                        moff = 1024 * (s - (nsb - 2))
                        tok_exp = b.step("vector", lambda v, pth=pth, moff=moff: v.tensor_tensor(
                            pth, pth, msk[:, moff:moff + 1024], ALU.mult),
                            deps=[tok_exp, tok_mg])
                    tok_exp_h[half] = tok_exp
                    exp_s[s] = tok_exp
                if s >= 1:
                    sp = s - 1
                    half = sp % 2
                    pth = Pt[:, 1024 * half:1024 * (half + 1)]

                    def mmsum(t, sp=sp, nsb=nsb, pth=pth):
                        for j in range(2):
                            i = t.matmul(ps_sum[0:1, :], ones_b[:, :],
                                         pth[:, 512 * j:512 * (j + 1)],
                                         start=(sp == 0 and j == 0), stop=(sp == nsb - 1 and j == 1))
                        return i
                    tok_sm = b.step("tensor", mmsum,
                                    deps=[exp_s[sp], tok_on] + ([tok_se] if sp == 0 else []))

                    def mmpv(t, sp=sp, nsb=nsb, pso=pso, pth=pth):
                        for j in range(2):
                            kb = 2 * sp + j
                            i = t.matmul(pso[:, :], Vt[:, 128 * kb:128 * (kb + 1)],
                                         pth[:, 512 * j:512 * (j + 1)],
                                         start=(sp == 0 and j == 0), stop=(sp == nsb - 1 and j == 1))
                        return i
                    tok_pv = b.step("tensor", mmpv,
                                    deps=[v_vt, tok_o_ev.get(bank)] if sp == 0 else [])
                    tok_cons_h[half] = tok_pv
            tok_se = b.step("vector", lambda v: v.tensor_copy(
                sumr[0:1, :], ps_sum[0:1, :]), deps=[tok_sm, tok_rrep])
            tok_rc = b.step("vector", lambda v: v.reciprocal(sumr[0:1, :], sumr[0:1, :]))
            tok_rw = b.dma("gpsimd", lambda e, s, qb=qb, h=h: (
                e.dma_start(out=scr[1:2, 0:512], in_=sumr[0:1, :]).then_inc(s, 16),),
                1, deps=[tok_rc], lane="rr")
            tok_rrep = b.dma("gpsimd", lambda e, s: (
                e.dma_start(out=rrep[:, :], in_=row_bcast(scr[1:2, 0:512], 512)).then_inc(s, 16),),
                1, deps=[tok_rw], lane="rr")
            tok_oe = b.step("vector", lambda v, pso=pso: v.tensor_tensor(
                aob[:, :], pso[:, :], rrep[:, :], ALU.mult),
                deps=[tok_pv, tok_rrep, tok_ao])
            tok_o_ev[bank] = tok_oe
            tok_ao = b.dma("sync", lambda e, s, qb=qb, h=h: (
                e.dma_start(out=aov[128 * h:128 * (h + 1), 512 * qb:512 * (qb + 1)],
                            in_=aob[:, :]).then_inc(s, 16),), 1,
                deps=[tok_oe], lane="aos")
    return tok_ao



def build_full():
    b = KB()
    PSBIG = b.ps("PSBIG", [128, 2048])
    PSA = b.ps("PSA", [128, 512])
    PSB = b.ps("PSB", [128, 512])
    PSC = b.ps("PSC", [128, 512])
    PB16 = b.ps("PB16", [128, 128], BF16)
    PS = {
        "psbig": PSBIG, "psa": PSA, "psb": PSB, "psc": PSC, "b16": PB16,
        "pool_ps_s": PSBIG[0:1, 0:1024],
        "pool_ps_q": PSBIG[0:1, 1024:2048],
        "dq_ps": PB16,
        "a2_ps1a": PSBIG[:, 0:512],
        "a2_ps1b": PSBIG[:, 512:1024],
        "a2_ps_sc": PSC[:, 0:128],
        "a2_ps_q3": PSBIG[0:1, 1536:2048],
        "a2_ps_ypa": PSA[:, 0:128],
        "a2_ps_ypb": PSB[:, 0:128],
        "a2_ps_st2": PSBIG[:, 0:1024],
        "a2_ps_bc": PSBIG[:, 1024:2048],
        "b_ps_sum": PSC[0:1, 0:512],
    }

    x = b.dram("x", [D, TPC], F32, kind="ExternalInput")
    wcat = b.dram("wcat", [D, 24], F32, kind="ExternalInput")
    hbase = b.dram("hbase", [32], F32, kind="ExternalInput")
    wbr = b.dram("wbr", [1024, 1024], F32, kind="ExternalInput")
    wp = b.dram("wp", [4096, 256], I32, kind="ExternalInput")
    ws = b.dram("ws", [4096, 4], F32, kind="ExternalInput")
    winT = b.dram("winT", [D, NOC * 128], F32, kind="ExternalInput")
    convw = b.dram("convw", [1280, 4], F32, kind="ExternalInput")
    convb = b.dram("convb", [1280], F32, kind="ExternalInput")
    dtb = b.dram("dtb", [16], F32, kind="ExternalInput")
    avec = b.dram("avec", [16], F32, kind="ExternalInput")
    dexp = b.dram("dexp", [1024], F32, kind="ExternalInput")
    nwv = b.dram("nwv", [1024], F32, kind="ExternalInput")
    woTs = b.dram("woTs", [1024, D], F32, kind="ExternalInput")
    fbias = b.dram("fbias", [D], F32, kind="ExternalInput")
    perm = b.dram("perm", [D], I32, kind="ExternalInput")
    yidx = b.dram("yidx", [128, 64], I32, kind="ExternalInput")
    wqkv = b.dram("wqkv", [D, 768], F32, kind="ExternalInput")
    cosf = b.dram("cosf", [64, T], F32, kind="ExternalInput")
    sinf = b.dram("sinf", [64, T], F32, kind="ExternalInput")
    wos = b.dram("wos", [512, D], F32, kind="ExternalInput")
    bo = b.dram("bo", [D], F32, kind="ExternalInput")
    aidx = b.dram("aidx", [128, 32], I32, kind="ExternalInput")
    x6o = b.dram("x6", [D, TPC], F32, kind="ExternalOutput")

    x1_d = b.dram("x1_d", [D, TPC], F32)
    x4_d = b.dram("x4_d", [D, TPC], F32)
    xln2_own = b.dram("xln2_own", [D, TPC], BF16)
    xln2_allH = [b.dram(f"xln2_all{h}", [NCORES * 2048, TPC], BF16,
                        addr_space="Shared") for h in range(2)]
    dq_own = b.dram("dq_own", [512, D], BF16)
    deqT_all = b.dram("deqT_all", [D, D], BF16, addr_space="Shared")
    yg_ownH = [b.dram(f"ygoh{h}", [4096, 1024], BF16) for h in range(2)]
    yg_agH = [b.dram(f"ygagh{h}", [NCORES * 4096, 1024], BF16, addr_space="Shared")
              for h in range(2)]
    xconv_d = b.dram("xconv_d", [1280, T], BF16)
    gate_d = b.dram("gate_d", [1024, T], BF16)
    dt_d = b.dram("dt_d", [16, T], F32)
    y_d = b.dram("y_d", [1024, T], BF16)
    scr2 = b.dram("scr2", [8, 8192], F32)
    xln5_own = b.dram("xln5_own", [D, TPC], BF16)
    xln5_allH = [b.dram(f"xln5_all{h}", [NCORES * 2048, TPC], BF16,
                        addr_space="Shared") for h in range(2)]
    ao_own = b.dram("ao_own", [4096, 1024], BF16)
    ao_all = b.dram("ao_all", [NCORES * 4096, 1024], BF16, addr_space="Shared")
    woT_bf_own = b.dram("woT_bf_own", [1024, D], BF16)
    woT_bf = b.dram("woT_bf", [8192, D], BF16, addr_space="Shared")
    wo_bf_own = b.dram("wo_bf_own", [512, D], BF16)
    wo_bf = b.dram("wo_bf", [D, D], BF16, addr_space="Shared")

    def ag(src, dst, deps):
        return b.cc(lambda g: g.collective_compute(
            "AllGather", ALU.bypass, replica_groups=RG,
            ins=[src.opt()], outs=[dst.opt()]), deps)

    b.phase("a1_")
    wcb = [b.sb(f"wcast{i}", [128, D], BF16) for i in range(2)]
    tok_wc = [None, None]

    def cast_rows(src, dst, n):
        for i in range(n):
            tok_l = b.dma("gpsimd", lambda e, s, i=i, src=src: (
                e.dma_start(out=wcb[i % 2][:, :], in_=src[128 * i:128 * (i + 1), :]).then_inc(s, 16),),
                1, deps=[tok_wc[i % 2]], lane="wc")
            tok_wc[i % 2] = b.dma("sync", lambda e, s, i=i, dst=dst: (
                e.dma_start(out=dst[128 * i:128 * (i + 1), :], in_=wcb[i % 2][:, :]).then_inc(s, 16),),
                1, deps=[tok_l], lane="wc")
        return list(tok_wc)

    tok_wcast1 = cast_rows(woTs, woT_bf_own, 8)
    tok_wcast2 = cast_rows(wos, wo_bf_own, 4)
    cc_woT = ag(woT_bf_own, woT_bf, tok_wcast1)
    cc_wo = ag(wo_bf_own, wo_bf, tok_wcast2)
    P = make_shared_pool(b, dict(ps_s=PS["pool_ps_s"], ps_q=PS["pool_ps_q"],
                                 ps_log=PSC, ps_misc=PSC))
    P["ps_y"] = PSA
    load_ffn0_weights(b, P, wcat, wbr, hbase)
    st1, st2, _ = emit_block_ffn0(b, P, x, x1_d, xln2_own, 0.5)
    tok_dq = emit_dequant(b, PS, wp, ws, dq_own)
    cc_x2a = ag(xln2_own[0:2048, :], xln2_allH[0], [st2[15]])
    cc_x2b = ag(xln2_own[2048:4096, :], xln2_allH[1], [st2[-1]])
    cc_dq = ag(dq_own, deqT_all, [tok_dq])

    b.phase("a2a_")
    ph1 = emit_a2a(b, PS, dict(
        xallH=xln2_allH, winT=winT, convw=convw, convb=convb, dtb=dtb,
        xconv_d=xconv_d, gate_d=gate_d, dt_d=dt_d,
        x_guards=[cc_x2a, cc_x2b]))
    b.phase("a2b_")
    agstate = {}

    def yg_ag_cb(cc, st_toks):
        if cc == 3:
            agstate["A"] = ag(yg_ownH[0], yg_agH[0], st_toks[-2:])
            return agstate["A"]
        if cc == 7:
            agstate["B"] = ag(yg_ownH[1], yg_agH[1], st_toks[-2:])
            return agstate["B"]
        return None

    ygv_cc = []
    for cc in range(8):
        vH = yg_ownH[cc // 4].rearrange("(q c bb) t -> q c (bb t)", q=4, bb=8)
        ygv_cc.append(vH[cc % 4])
    emit_a2b(b, PS, dict(
        xconv_d=xconv_d, gate_d=gate_d, dt_d=dt_d, y_d=y_d, scr2=scr2,
        avec=avec, dexp=dexp, nwv=nwv, ygv_cc=ygv_cc), yg_ag_cb)
    yg_srcs = [yg_agH[0]] * 32 + [yg_agH[1]] * 32

    b.phase("a3_")
    tok_xln5, st4 = emit_a3(b, PS, dict(
        yall=yg_srcs, x1=x1_d, woT=woT_bf, wcat=wcat, hbase=hbase, wbr=wbr,
        deqT=deqT_all, fbias=fbias, perm=perm, yidx=yidx,
        x4=x4_d, xln5_own=xln5_own, y_guardA=agstate["A"],
        y_guardB=agstate["B"], dq_guard=cc_dq, woT_guard=cc_woT))
    cc_x5a = ag(xln5_own[0:2048, :], xln5_allH[0], [tok_xln5[0]])
    cc_x5b = ag(xln5_own[2048:4096, :], xln5_allH[1], [tok_xln5[1]])

    b.phase("b_")
    tok_ao = emit_b(b, PS, dict(
        xall5H=xln5_allH, wqkv=wqkv, cosf=cosf, sinf=sinf,
        aov=ao_own.rearrange("(c bb) t -> c (bb t)", bb=8),
        x_guards=[cc_x5a, cc_x5b]))
    cc_ao = ag(ao_own, ao_all, [tok_ao])

    b.phase("c_")
    st6 = emit_c(b, PS, dict(
        aall=ao_all, x4=x4_d, wo=wo_bf, bo=bo, deqT=deqT_all, fbias=fbias,
        perm=perm, aidx=aidx, x6=x6o, a_guard=cc_ao, dq_guard=cc_dq,
        wo_guard=cc_wo))
    return b.finish([st6])



def _t(a):
    return np.ascontiguousarray(np.asarray(a).T)


def _rope_tables():
    dim = 128
    freqs = 1.0 / 10000.0 ** (np.arange(0, dim, 2, dtype=np.float32) / dim)
    angle = np.arange(T, dtype=np.float32)[:, None] * freqs[None, :]
    return np.cos(angle).astype(np.float32), np.sin(angle).astype(np.float32)


def stage_a2(inp, g):
    wT = np.ascontiguousarray(inp["L0_attn_in_proj_w"].T)
    winT = np.zeros((4096, 2432), np.float32)
    winT[:, 0:1024] = wT[:, 1024 * g:1024 * (g + 1)]
    winT[:, 1024:2048] = wT[:, 8192 + 1024 * g:8192 + 1024 * (g + 1)]
    winT[:, 2048:2176] = wT[:, 16384 + 128 * g:16384 + 128 * (g + 1)]
    winT[:, 2176:2304] = wT[:, 17408 + 128 * g:17408 + 128 * (g + 1)]
    winT[:, 2304:2320] = wT[:, 18432 + 16 * g:18432 + 16 * (g + 1)]
    cw = inp["L0_attn_conv_w"][:, 0, :]
    rows = np.concatenate([np.arange(1024 * g, 1024 * (g + 1)),
                           np.arange(8192 + 128 * g, 8192 + 128 * (g + 1)),
                           np.arange(9216 + 128 * g, 9216 + 128 * (g + 1))])
    return {
        "winT": winT,
        "convw": np.ascontiguousarray(cw[rows]),
        "convb": np.ascontiguousarray(inp["L0_attn_conv_b"][rows]),
        "dtb": np.ascontiguousarray(inp["L0_attn_dt_bias"][16 * g:16 * (g + 1)]),
        "avec": -(np.arange(16, dtype=np.float32) + 16 * g + 1),
        "dexp": np.repeat(inp["L0_attn_D"][16 * g:16 * (g + 1)], 64).astype(np.float32),
        "nwv": np.ascontiguousarray(inp["L0_attn_norm_w"][1024 * g:1024 * (g + 1)]),
    }


def stage_b(inp, g):
    sh = np.concatenate([np.arange(0, 128, 2), np.arange(1, 128, 2)])
    wq = inp["L1_attn_Wq"]
    wk = inp["L1_attn_Wk"]
    wv = inp["L1_attn_Wv"]
    cols = []
    for h in range(4):
        cols.append(wq[:, (4 * g + h) * 128:(4 * g + h + 1) * 128][:, sh])
    cols.append(wk[:, g * 128:(g + 1) * 128][:, sh])
    cols.append(wv[:, g * 128:(g + 1) * 128])
    return np.ascontiguousarray(np.concatenate(cols, axis=1))


_BUILD_CACHE = {}


def _built(name, fn):
    if name not in _BUILD_CACHE:
        _BUILD_CACHE[name] = fn()
    return _BUILD_CACHE[name]


def kernel(**inputs):
    inp = {k: np.asarray(v) for k, v in inputs.items()}
    cores = list(range(NCORES))

    xT = _t(inp["L0_attn_hidden_states"][0])
    wcat = _t(np.concatenate([inp["L0_ffn_W_res"], inp["L0_ffn_W_pre"],
                              inp["L0_ffn_W_post"]], axis=0))
    hbase = np.zeros(32, np.float32)
    hbase[0:16] = np.asarray(inp["L0_ffn_H_res_base"]).reshape(16)
    hbase[16:20] = inp["L0_ffn_H_pre_base"]
    hbase[20:24] = inp["L0_ffn_H_post_base"]
    hbase[24] = np.asarray(inp["L0_ffn_gate"])[0]
    wbr = np.ascontiguousarray(inp["L0_ffn_W_branch"])
    woT = _t(inp["L0_attn_out_proj_w"])
    kcs_perm = np.array([g * 8 + cc for g in range(8) for cc in range(4)]
                        + [g * 8 + cc for g in range(8) for cc in range(4, 8)])
    woT = np.ascontiguousarray(
        woT.reshape(64, 128, 4096)[kcs_perm].reshape(8192, 4096))
    cos, sin = _rope_tables()
    cosf = np.ascontiguousarray(cos.T)
    sinf = np.ascontiguousarray(sin.T)
    wo = np.ascontiguousarray(inp["L1_attn_Wo"])
    bo = np.ascontiguousarray(inp["L1_attn_bo"])
    fbias = np.ascontiguousarray(inp["L1_ffn_bias"])
    perm = np.ascontiguousarray(inp["L1_ffn_perm"])

    p_ = np.arange(128)
    kcs_all = np.array([g * 8 + cc for g in range(8) for cc in range(4)]
                       + [g * 8 + cc for g in range(8) for cc in range(4, 8)])
    kc64 = kcs_all
    yidx = ((kc64[None, :] // 8) * 4096 + ((kc64[None, :] % 8) % 4) * 1024
            + p_[:, None] * 8).astype(np.int32)
    kc32 = np.arange(32)
    C2 = kc32[None, :] * 128 + p_[:, None]
    aidx_base = ((C2 // 512) * 4096 + (C2 % 512) * 8).astype(np.int32)

    nc = _built("full", build_full)
    in_maps = []
    for j in range(NCORES):
        m = {
            "x": np.ascontiguousarray(xT[:, TPC * j:TPC * (j + 1)]),
            "wcat": wcat, "hbase": hbase, "wbr": wbr,
            "wp": np.ascontiguousarray(inp["L1_ffn_w_packed"][:, 256 * j:256 * (j + 1)]),
            "ws": np.ascontiguousarray(inp["L1_ffn_w_scales"][:, 4 * j:4 * (j + 1)]),
            "woTs": np.ascontiguousarray(woT[1024 * j:1024 * (j + 1)]),
            "fbias": fbias, "perm": perm,
            "yidx": (yidx + j).astype(np.int32),
            "cosf": cosf, "sinf": sinf,
            "wos": np.ascontiguousarray(wo[512 * j:512 * (j + 1)]),
            "bo": bo,
            "aidx": (aidx_base + j).astype(np.int32),
            "wqkv": stage_b(inp, j),
        }
        m.update(stage_a2(inp, j))
        in_maps.append(m)
    res = run_bass_kernel_spmd(nc, in_maps, core_ids=cores)
    x6 = np.concatenate([np.asarray(res.results[j]["x6"]) for j in range(NCORES)], axis=1)
    return np.ascontiguousarray(x6.T)[None].astype(np.float32)


# version 17 (final)
# speedup vs baseline: 1.1002x; 1.1002x over previous
import numpy as np
import contextlib
from functools import reduce
import concourse.bass as bass
import concourse.mybir as mybir
from concourse.bass_utils import run_bass_kernel_spmd

F32 = mybir.dt.float32
BF16 = mybir.dt.bfloat16
I32 = mybir.dt.int32
U8 = mybir.dt.uint8
AF = mybir.ActivationFunctionType
ALU = mybir.AluOpType
AX = mybir.AxisListType

NCORES = 8
T = 8192
D = 4096
TPC = T // NCORES
EPS_LN = 1e-5
RMS_EPS = 1.1920929e-07
MAMBA_EPS = 1e-5
RG = [list(range(NCORES))]



class PhasedArena:

    def __init__(self, nc):
        self.nc = nc
        self.tensors = []
        self.size = 0
        self.cur = 0

    def sb(self, name, shape, dtype):
        per_part = reduce(lambda a, c: a * c, shape[1:], 1) * mybir.dt.size(dtype)
        pre = self.cur
        self.cur = (pre + per_part + 31) & ~31
        self.size = max(self.size, self.cur)
        handle = self.nc.virtual_sbuf_tensor(name, shape, dtype)
        self.nc._tensor(handle.name, handle.shape, handle.dtype, type="SB")
        self.tensors.append((handle, pre))
        return handle

    def reset(self):
        self.cur = 0

    def allocate(self, stack):
        handle = stack.enter_context(
            self.nc.sbuf_tensor("arena", [128, self.size], U8))
        offset = self.nc.lookup_mloc(handle).addr
        for hnd, addr in self.tensors:
            mloc = self.nc.lookup_mloc(hnd)
            mloc.allocated = not self.nc.target_bir_lowering
            mloc.addr = offset + addr


class KB:

    ENGS = ("sync", "tensor", "vector", "scalar", "gpsimd")

    def __init__(self):
        self.nc = bass.Bass("TRN2", target_bir_lowering=False, debug=False)
        self.q = {e: [] for e in self.ENGS}
        self.stack = contextlib.ExitStack()
        self.esem, self.ecnt, self.dsem, self.dcnt = {}, {}, {}, {}
        for e in self.ENGS:
            self.esem[e] = self.stack.enter_context(self.nc.semaphore(f"es_{e}"))
            self.ecnt[e] = 0
        self.arena = PhasedArena(self.nc)
        self.prefix = ""
        self.guard = []
        self.guard_pending = set()
        self.ccsem = self.stack.enter_context(self.nc.semaphore("ccsem"))
        self.cccnt = 0

    def phase(self, prefix):
        toks = [(self.esem[e], self.ecnt[e]) for e in self.ENGS if self.ecnt[e]]
        toks += [(self.dsem[k], self.dcnt[k]) for k in self.dsem]
        self.guard = toks
        self.guard_pending = set(self.ENGS)
        self.prefix = prefix
        self.arena.reset()

    def _inject(self, eng, deps):
        if eng in self.guard_pending:
            self.guard_pending.discard(eng)
            return list(deps) + self.guard
        return list(deps)

    def sb(self, name, shape, dtype):
        return self.arena.sb(self.prefix + name, shape, dtype)

    def ps(self, name, shape, dtype=F32):
        return self.stack.enter_context(self.nc.psum_tensor(name, shape, dtype))

    def dram(self, name, shape, dtype, kind=None, addr_space="Local"):
        if kind:
            return self.nc.dram_tensor(name, shape, dtype, kind=kind).ap()
        return self.nc.dram_tensor(self.prefix + name, shape, dtype,
                                   addr_space=addr_space).ap()

    def step(self, eng, fn, deps=()):
        sem = self.esem[eng]
        self.ecnt[eng] += 1
        val = self.ecnt[eng]
        deps = self._inject(eng, deps)
        deps2 = _dedup([d for d in deps if d is not None and d[0] is not sem])
        if eng in ("gpsimd", "vector", "scalar") and val > 1:
            deps2.append((sem, val - 1))
        deps2 = tuple(deps2)

        def run(e, fn=fn, deps2=deps2, sem=sem):
            for (s, v) in deps2:
                e.wait_ge(s, v)
            inst = fn(e)
            if inst is None:
                inst = e.nop()
            inst.then_inc(sem, 1)
        self.q[eng].append(run)
        return (sem, val)

    def dma(self, eng, fn, n, deps=(), lane="0"):
        key = (eng, lane)
        if key not in self.dsem:
            self.dsem[key] = self.stack.enter_context(
                self.nc.semaphore(f"ds_{eng}_{lane}"))
            self.dcnt[key] = 0
        sem = self.dsem[key]
        self.dcnt[key] += 16 * n
        val = self.dcnt[key]
        deps = self._inject(eng, deps)
        deps2 = tuple(_dedup([d for d in deps if d is not None]))

        def run(e, fn=fn, deps2=deps2, sem=sem):
            for (s, v) in deps2:
                e.wait_ge(s, v)
            fn(e, sem)
        self.q[eng].append(run)
        return (sem, val)

    def cc(self, fn, deps=()):
        self.cccnt += 1
        val = self.cccnt
        deps = self._inject("gpsimd", deps)
        deps2 = tuple(_dedup([d for d in deps if d is not None]))

        def run(e, fn=fn, deps2=deps2, sem=self.ccsem):
            for (s, v) in deps2:
                e.wait_ge(s, v)
            fn(e).then_inc(sem, 1)
        self.q["gpsimd"].append(run)
        return (self.ccsem, val)

    def finish(self, final_deps):
        def fin(e, deps=tuple(d for d in final_deps if d is not None)):
            for (s, v) in deps:
                e.wait_ge(s, v)
        self.q["sync"].append(fin)
        nc = self.nc
        self.arena.allocate(self.stack)
        with nc.Block() as block:
            for eng in self.ENGS:
                ops = self.q[eng]
                if not ops:
                    continue
                def mk(ops):
                    def run(e):
                        for f in ops:
                            f(e)
                    return run
                getattr(block, eng)(mk(ops))
        self.stack.close()
        return nc


def _dedup(deps):
    best = {}
    for (s, v) in deps:
        k = id(s)
        if k not in best or best[k][1] < v:
            best[k] = (s, v)
    return list(best.values())


def brd(ap, pattern):
    return bass.AP(ap.tensor, ap.offset, [list(ap.ap[0])] + [list(p) for p in pattern])


def row_bcast(dram_row_ap, ncols):
    return bass.AP(dram_row_ap.tensor, dram_row_ap.offset, [[0, 128], [1, ncols]])


def emit_ident(b, name, dtype):
    t = b.sb(name, [128, 128], dtype)
    b.step("gpsimd", lambda g: g.memset(t[:, :], 1.0))
    tok = b.step("gpsimd", lambda g: g.affine_select(
        out=t[:, :], in_=t[:, :], compare_op=ALU.is_equal, fill=0.0, base=0,
        pattern=[[-1, 128]], channel_multiplier=1))
    return t, tok



def emit_ln_finalize(b, ps_s, ps_q, stat, scr, scr_row, rrep, brep, dep, ntok,
                     srep=None, nchan=D, eps=EPS_LN):
    stat1, stat2 = stat
    M = stat1[0:1, 0:ntok]
    Q = stat2[0:1, 0:ntok]
    V_ = rrep[0:1, 0:ntok]
    R = brep[0:1, 0:ntok]
    S2 = stat2[0:1, 0:ntok]
    Bv = rrep[0:1, 0:ntok]

    b.step("vector", lambda v: v.tensor_single_scalar(
        M, ps_s[0:1, 0:ntok], 1.0 / nchan, ALU.mult), deps=[dep])
    b.step("vector", lambda v: v.tensor_single_scalar(
        Q, ps_q[0:1, 0:ntok], 1.0 / nchan, ALU.mult))
    b.step("vector", lambda v: v.tensor_tensor(V_, M, M, ALU.mult))
    b.step("vector", lambda v: v.tensor_tensor(V_, Q, V_, ALU.subtract))
    b.step("vector", lambda v: v.tensor_single_scalar(Q, V_, eps, ALU.add))
    t1 = b.step("vector", lambda v: v.reciprocal(R, Q))
    if srep is not None:
        sr = srep[0:1, 0:ntok]
        b.step("vector", lambda v: v.tensor_tensor(sr, V_, R, ALU.mult))
        b.step("vector", lambda v: v.tensor_single_scalar(sr, sr, RMS_EPS, ALU.add))
        t1 = b.step("vector", lambda v: v.reciprocal(sr, sr))
        S2 = sr
    t2 = b.step("scalar", lambda sc: sc.activation(R, R, AF.Sqrt), deps=[t1])
    if srep is not None:
        t2 = b.step("scalar", lambda sc: sc.activation(S2, S2, AF.Sqrt), deps=[t2])
    t3 = b.step("vector", lambda v: v.tensor_tensor(Bv, M, R, ALU.mult), deps=[t2])

    nrows = 3 if srep is not None else 2

    def w1(e, s):
        e.dma_start(out=scr[scr_row:scr_row + 1, 0:ntok], in_=R).then_inc(s, 16)
        e.dma_start(out=scr[scr_row + 1:scr_row + 2, 0:ntok], in_=Bv).then_inc(s, 16)
        if srep is not None:
            e.dma_start(out=scr[scr_row + 2:scr_row + 3, 0:ntok], in_=S2).then_inc(s, 16)
    t4 = b.dma("gpsimd", w1, nrows, deps=[t3], lane="lnrep")

    def w2(e, s):
        e.dma_start(out=rrep[:, 0:ntok],
                    in_=row_bcast(scr[scr_row:scr_row + 1, 0:ntok], ntok)).then_inc(s, 16)
        e.dma_start(out=brep[:, 0:ntok],
                    in_=row_bcast(scr[scr_row + 1:scr_row + 2, 0:ntok], ntok)).then_inc(s, 16)
        if srep is not None:
            e.dma_start(out=srep[:, 0:ntok],
                        in_=row_bcast(scr[scr_row + 2:scr_row + 3, 0:ntok], ntok)).then_inc(s, 16)
    t5 = b.dma("gpsimd", w2, nrows, deps=[t4], lane="lnrep")
    return t5



def emit_ffn0(b, P, xln, v_xln, srep, v_srep, out_cb):
    ntok = TPC
    ntc = ntok // 128
    wcat_sb, wbr_sb, hb, idf, scr = P["wcat_sb"], P["wbr_sb"], P["hb"], P["idf"], P["scr"]
    ps_log, ps_misc = P["ps_log"], P["ps_misc"]
    ps_y = P["ps_y"] if "ps_y" in P else P["ps_mm"][0]

    logits = b.sb("f0_logits", [24, ntok], F32)
    toks = []
    ev_prev = None
    for tb in range(ntok // 512):
        def mml(t, tb=tb):
            for cc in range(32):
                i = t.matmul(ps_log[0:24, 0:512], wcat_sb[:, 24 * cc:24 * (cc + 1)],
                             xln[:, ntok * cc + 512 * tb:ntok * cc + 512 * tb + 512],
                             start=(cc == 0), stop=(cc == 31))
            return i
        tok = b.step("tensor", mml, deps=[v_xln, P["tok_w"]] + ([ev_prev] if ev_prev else []))
        def ev1(v, tb=tb):
            return v.tensor_tensor(logits[0:24, 512 * tb:512 * (tb + 1)],
                                   ps_log[0:24, 0:512], srep[0:24, 512 * tb:512 * (tb + 1)],
                                   ALU.mult)
        tok = b.step("vector", ev1, deps=[tok, v_srep])
        ev_prev = tok
        toks.append(tok)
    tok = b.step("scalar", lambda sc: sc.activation(
        logits[:, :], logits[:, :], AF.Identity, bias=hb[0:24, 0:1]), deps=toks)
    v_log = tok

    Ztm = b.sb("f0_Ztm", [128, ntc * 24], F32)
    tok_t = None
    for tc in range(ntc):
        tok_t = b.step("tensor", lambda t, tc=tc: t.transpose(
            ps_misc[:, 0:24], logits[0:24, 128 * tc:128 * (tc + 1)], idf[0:24, 0:24]),
            deps=[v_log] if tc == 0 else [tok_t])
        tok_t = b.step("vector", lambda v, tc=tc: v.tensor_copy(
            Ztm[:, 24 * tc:24 * (tc + 1)], ps_misc[:, 0:24]), deps=[tok_t])
    v_tm = tok_t

    Zs = b.sb("f0_Zs", [128, ntc * 16], F32)
    Tt = b.sb("f0_T", [128, ntc * 16], F32)
    u = b.sb("f0_u", [128, ntc * 4], F32)
    vv_ = b.sb("f0_v", [128, ntc * 4], F32)
    red = b.sb("f0_red", [128, 2 * ntc * 4], F32)
    Hres = b.sb("f0_Hres", [128, ntc * 16], F32)
    sp = b.sb("f0_sp", [128, ntc * 4], F32)
    spo = b.sb("f0_spo", [128, ntc * 4], F32)

    def prep(v):
        for tc in range(ntc):
            v.tensor_single_scalar(Zs[:, 16 * tc:16 * (tc + 1)],
                                   Ztm[:, 24 * tc:24 * tc + 16], 10.0, ALU.mult)
            v.tensor_copy(sp[:, 4 * tc:4 * (tc + 1)], Ztm[:, 24 * tc + 16:24 * tc + 20])
            v.tensor_copy(spo[:, 4 * tc:4 * (tc + 1)], Ztm[:, 24 * tc + 20:24 * tc + 24])
        v.memset(u[:, :], 0.0)
        return v.memset(vv_[:, :], 0.0)
    tok = b.step("vector", prep, deps=[v_tm])

    zv = brd(Zs[:, :], [[16, ntc], [4, 4], [1, 4]])
    tv = brd(Tt[:, :], [[16, ntc], [4, 4], [1, 4]])
    ub = brd(u[:, :], [[4, ntc], [1, 4], [0, 4]])
    vb = brd(vv_[:, :], [[4, ntc], [0, 4], [1, 4]])
    tred_j = brd(Tt[:, :], [[4, ntc * 4], [1, 4]])
    tred_i = brd(Tt[:, :], [[16, ntc], [1, 4], [4, 4]])
    red_j = brd(red[:, 0:ntc * 4], [[1, ntc * 4]])
    red_i = brd(red[:, 0:ntc * 4], [[4, ntc], [1, 4]])

    for it in range(50):
        tok = b.step("vector", lambda v: v.tensor_tensor(tv, zv, vb, ALU.subtract), deps=[tok])
        tok = b.step("scalar", lambda sc: sc.activation(Tt[:, :], Tt[:, :], AF.Exp), deps=[tok])
        tok = b.step("vector", lambda v: v.tensor_reduce(red_j, tred_j, AX.X, ALU.add), deps=[tok])
        tok = b.step("scalar", lambda sc: sc.activation(u[:, :], red[:, 0:ntc * 4], AF.Ln), deps=[tok])
        tok = b.step("vector", lambda v: v.tensor_tensor(tv, zv, ub, ALU.subtract), deps=[tok])
        tok = b.step("scalar", lambda sc: sc.activation(Tt[:, :], Tt[:, :], AF.Exp), deps=[tok])
        tok = b.step("vector", lambda v: v.tensor_reduce(red_i, tred_i, AX.X, ALU.add), deps=[tok])
        tok = b.step("scalar", lambda sc: sc.activation(vv_[:, :], red[:, 0:ntc * 4], AF.Ln), deps=[tok])
    tok = b.step("vector", lambda v: v.tensor_tensor(tv, zv, ub, ALU.subtract), deps=[tok])
    tok = b.step("vector", lambda v: v.tensor_tensor(tv, tv, vb, ALU.subtract), deps=[tok])
    tok = b.step("scalar", lambda sc: sc.activation(Hres[:, :], Tt[:, :], AF.Exp), deps=[tok])
    v_hraw = tok

    gsml = b.sb("f0_gsml", [1, 2], F32)
    g_pp = b.sb("f0_gpp", [128, 1], F32)
    og_pp = b.sb("f0_ogpp", [128, 1], F32)
    hbg = b.sb("f0_hbg", [1, 1], F32)
    tokg0 = b.dma("gpsimd", lambda e, s: (
        e.dma_start(out=hbg[0:1, 0:1],
                    in_=P["hbase_dram"][24:25].rearrange("(p o) -> p o", o=1)).then_inc(s, 16),), 1,
        lane="gate")
    tg = b.step("scalar", lambda sc: sc.activation(gsml[0:1, 0:1], hbg[0:1, 0:1],
                                                   AF.Sigmoid), deps=[tokg0])
    tg = b.step("vector", lambda v: v.tensor_scalar(
        gsml[0:1, 1:2], gsml[0:1, 0:1], -1.0, 1.0, ALU.mult, ALU.add), deps=[tg])
    tg = b.dma("gpsimd", lambda e, s: (
        e.dma_start(out=scr[2:3, 0:2], in_=gsml[0:1, 0:2]).then_inc(s, 16),), 1, deps=[tg],
        lane="gate")
    tg = b.dma("gpsimd", lambda e, s: (
        e.dma_start(out=g_pp[:, :], in_=row_bcast(scr[2:3, 0:1], 1)).then_inc(s, 16),
        e.dma_start(out=og_pp[:, :], in_=row_bcast(scr[2:3, 1:2], 1)).then_inc(s, 16)),
        2, deps=[tg], lane="gate")

    Ipat = b.sb("f0_I", [128, ntc * 16], BF16)
    b.step("gpsimd", lambda g: g.memset(Ipat[:, :], 1.0))
    ti = b.step("gpsimd", lambda g: g.affine_select(
        out=Ipat[:, :], in_=Ipat[:, :], compare_op=ALU.is_equal, fill=0.0, base=0,
        pattern=[[0, ntc], [1, 4], [-1, 4]], channel_multiplier=0))

    b.step("vector", lambda v: v.tensor_scalar(
        Hres[:, :], Hres[:, :], g_pp[:, :], 0.0, ALU.mult, ALU.add),
        deps=[v_hraw, tg, ti])
    tok = b.step("vector", lambda v: v.scalar_tensor_tensor(
        Hres[:, :], Ipat[:, :], og_pp[:, :], Hres[:, :], ALU.mult, ALU.add))
    v_hres = tok

    last_sm = tok
    for src in (sp, spo):
        sv = brd(src[:, :], [[4, ntc], [1, 4]])
        r0 = red[:, 0:ntc]
        r0v = brd(r0, [[1, ntc]])
        r0b = brd(r0, [[1, ntc], [0, 4]])
        r1 = red[:, ntc:2 * ntc]
        r1v = brd(r1, [[1, ntc]])
        r1b = brd(r1, [[1, ntc], [0, 4]])
        b.step("vector", lambda v, sv=sv, r0v=r0v: v.tensor_reduce(
            r0v, sv, AX.X, ALU.max), deps=[v_tm, last_sm])
        last_sm = b.step("vector", lambda v, src=src, r0b=r0b: v.tensor_tensor(
            src[:, :], src[:, :], r0b, ALU.subtract))
        last_sm = b.step("scalar", lambda sc, src=src: sc.activation(
            src[:, :], src[:, :], AF.Exp), deps=[last_sm])
        b.step("vector", lambda v, sv=sv, r0v=r0v: v.tensor_reduce(
            r0v, sv, AX.X, ALU.add), deps=[last_sm])
        b.step("vector", lambda v, r0v=r0v, r1v=r1v: v.reciprocal(r1v, r0v))
        last_sm = b.step("vector", lambda v, src=src, r1b=r1b: v.tensor_tensor(
            src[:, :], src[:, :], r1b, ALU.mult))
    v_sm = last_sm

    tmp = P.get("f0_tmp") or b.sb("f0_tmp", [128, ntok], F32)
    acc = P.get("f0_acc") or b.sb("f0_acc", [128, ntok], F32)
    H_fm = acc
    tok_tb = None
    for tc in range(ntc):
        for (src, w, dst0) in ((Hres, 16, 0), (sp, 4, 32), (spo, 4, 64)):
            tok_tb = b.step("tensor", lambda t, tc=tc, src=src, w=w: t.transpose(
                ps_misc[0:w, 0:128], src[:, w * tc:w * (tc + 1)], idf[:, :]),
                deps=[v_hres, v_sm] if tok_tb is None else [tok_tb])
            tok_tb = b.step("vector", lambda v, tc=tc, w=w, dst0=dst0: v.tensor_copy(
                H_fm[dst0:dst0 + w, 128 * tc:128 * (tc + 1)], ps_misc[0:w, 0:128]),
                deps=[tok_tb])
    v_hfm = tok_tb

    tok_hscr = b.dma("gpsimd", lambda e, s: (
        e.dma_start(out=scr[8:24, 0:ntok], in_=H_fm[0:16, :]).then_inc(s, 16),
        e.dma_start(out=scr[24:28, 0:ntok], in_=H_fm[32:36, :]).then_inc(s, 16),
        e.dma_start(out=scr[28:32, 0:ntok], in_=H_fm[64:68, :]).then_inc(s, 16)), 3,
        deps=[v_hfm], lane="hrep")

    Hrep = P["fbf"][0:128, 8 * ntok:13 * ntok]

    def Hs(slot):
        return Hrep[:, ntok * slot:ntok * (slot + 1)]

    def rep_rows(rows, dep_tok, extra=()):
        def reps(e, s):
            for slot, r in enumerate(rows):
                e.dma_start(out=Hs(slot),
                            in_=row_bcast(scr[8 + r:9 + r, 0:ntok], ntok)).then_inc(s, 16)
        return b.dma("gpsimd", reps, len(rows), deps=[dep_tok] + list(extra), lane="hrep")

    v_hrep = rep_rows([16, 17, 18, 19], tok_hscr)
    xpre = P["fbf"][0:128, 0:8 * ntok]
    tok_xp = None
    for cc in range(8):
        dep0 = [v_hrep, v_xln] if cc == 0 else [tok_xp]
        for n in range(4):
            src = xln[:, ntok * (n * 8 + cc):ntok * (n * 8 + cc + 1)]
            if n == 0:
                tok_xp = b.step("vector", lambda v, src=src: v.tensor_tensor(
                    acc[:, :], src, Hs(0), ALU.mult), deps=dep0)
            else:
                b.step("vector", lambda v, src=src, n=n: v.tensor_tensor(
                    tmp[:, :], src, Hs(n), ALU.mult))
                dst = acc[:, :] if n < 3 else xpre[:, ntok * cc:ntok * (cc + 1)]
                tok_xp = b.step("vector", lambda v, dst=dst: v.tensor_tensor(
                    dst, acc[:, :], tmp[:, :], ALU.add))
    v_xpre = tok_xp

    ybc = P["fbf"][0:128, 13 * ntok:14 * ntok]
    tok_f = v_xpre
    tok_ev = None
    for i in range(4):
        tok_f = rep_rows([i * 4 + 0, i * 4 + 1, i * 4 + 2, i * 4 + 3, 20 + i],
                         tok_hscr, extra=[tok_f])
        for cc in range(8):
            for tb in range(ntok // 512):
                def mmy(t, cc=cc, tb=tb):
                    for ccc in range(8):
                        i = t.matmul(ps_y[:, :],
                                     wbr_sb[:, 1024 * ccc + 128 * cc:1024 * ccc + 128 * (cc + 1)],
                                     xpre[:, ntok * ccc + 512 * tb:ntok * ccc + 512 * tb + 512],
                                     start=(ccc == 0), stop=(ccc == 7))
                    return i
                tok_mmy = b.step("tensor", mmy,
                                 deps=([v_xpre, P["tok_w"]] + (P.get("ps_mm_guard") or []))
                                 if tok_ev is None else [tok_ev, tok_f])
                tok_ev = b.step("scalar", lambda sc, cc=cc, tb=tb: sc.copy(
                    ybc[:, 512 * tb:512 * (tb + 1)], ps_y[:, :]), deps=[tok_mmy])
            if P.get("pending_stats"):
                P["pending_stats"]()
                P["pending_stats"] = None

            for j in range(4):
                src = xln[:, ntok * (j * 8 + cc):ntok * (j * 8 + cc + 1)]
                if j == 0:
                    b.step("vector", lambda v, src=src: v.tensor_tensor(
                        acc[:, :], src, Hs(0), ALU.mult), deps=[tok_f, tok_ev])
                else:
                    b.step("vector", lambda v, src=src, j=j: v.tensor_tensor(
                        tmp[:, :], src, Hs(j), ALU.mult))
                    b.step("vector", lambda v: v.tensor_tensor(
                        acc[:, :], acc[:, :], tmp[:, :], ALU.add))
            b.step("vector", lambda v: v.tensor_tensor(
                tmp[:, :], ybc[:, :], Hs(4), ALU.mult), deps=[tok_ev])
            tok_f = b.step("vector", lambda v: v.tensor_tensor(
                acc[:, :], acc[:, :], tmp[:, :], ALU.add))
            tok_f = out_cb(i * 8 + cc, acc, tok_f)
            tok_ev = tok_f
    if P.get("pending_stats"):
        P["pending_stats"]()
        P["pending_stats"] = None
    P["ps_mm_guard"] = [(b.esem["vector"], b.ecnt["vector"]),
                        (b.esem["scalar"], b.ecnt["scalar"])]
    return tok_f



def emit_block_ffn0(b, P, xsrc, xres_out, xln_next_out, half_scale,
                    src_dep=None, first_ln_srep=True):
    ntok = TPC
    scr = P["scr"]
    xcb = P["xcb"]
    ps_s, ps_q = P["ps_s"], P["ps_q"]
    ones_f, ones_b = P["ones_f"], P["ones_b"]
    stat = P["stat"]
    xsq = P["xsq"]

    tok_mm = None
    prev_mm = [src_dep, src_dep]
    for cc in range(32):
        buf = xcb[cc % 2]
        tok_l = b.dma("sync", lambda e, s, cc=cc, buf=buf: (
            e.dma_start(out=buf[:, :], in_=xsrc[128 * cc:128 * (cc + 1), :]).then_inc(s, 16),),
            1, deps=[prev_mm[cc % 2]], lane=f"ld{cc % 2}")
        tok_sq = b.step("scalar", lambda sc, buf=buf: sc.activation(
            xsq[:, :], buf[:, :], AF.Square), deps=[tok_l, tok_mm])

        def mm(t, cc=cc, buf=buf):
            for tb in range(2):
                t.matmul(ps_s[0:1, 512 * tb:512 * (tb + 1)], ones_f[:, :],
                         buf[:, 512 * tb:512 * (tb + 1)], start=(cc == 0), stop=(cc == 31))
            for tb in range(2):
                i = t.matmul(ps_q[0:1, 512 * tb:512 * (tb + 1)], ones_b[:, :],
                             xsq[:, 512 * tb:512 * (tb + 1)], start=(cc == 0), stop=(cc == 31))
            return i
        tok_mm = b.step("tensor", mm, deps=[tok_l, tok_sq, P["tok_ones"]])
        prev_mm[cc % 2] = tok_mm
    v_stats = tok_mm

    rrep, brep, srep = P["rrep"], P["brep"], P["srep"]
    tok_rep = emit_ln_finalize(b, ps_s, ps_q, stat, scr, 0, rrep, brep, v_stats,
                               ntok, srep=srep)

    xln = P["xln"]
    tok_ap = None
    prev_ap = [v_stats, v_stats]
    for cc in range(32):
        buf = xcb[cc % 2]
        tok_l = b.dma("sync", lambda e, s, cc=cc, buf=buf: (
            e.dma_start(out=buf[:, :], in_=xsrc[128 * cc:128 * (cc + 1), :]).then_inc(s, 16),),
            1, deps=[prev_ap[cc % 2]], lane=f"ld{cc % 2}")

        b.step("vector", lambda v, buf=buf: v.tensor_tensor(
            buf[:, :], buf[:, :], rrep[:, :], ALU.mult), deps=[tok_l, tok_rep])
        tok_ap = b.step("vector", lambda v, cc=cc, buf=buf: v.tensor_tensor(
            xln[:, ntok * cc:ntok * (cc + 1)], buf[:, :], brep[:, :], ALU.subtract))
        prev_ap[cc % 2] = tok_ap
    v_xln = tok_ap

    x1sb = P["x1sb"]
    xsq2 = P["xsq2"]
    obuf = [P["xc3"], P["xc3"]]
    state = {"tok_mm": None, "tok_use": [None, None], "tok_st": None,
             "tok_l": {}}
    st_toks = []

    def pre_load(cc):
        if cc > 31 or cc in state["tok_l"]:
            return
        state["tok_l"][cc] = b.dma("sync", lambda e, s, cc=cc: (
            e.dma_start(out=obuf[cc % 2][:, :],
                        in_=xsrc[128 * cc:128 * (cc + 1), :]).then_inc(s, 16),),
            1, deps=[state["tok_use"][cc % 2],
                     v_xln if cc < 2 else None], lane="ocb")

    def out_cb(chunk, f_ap, dep):
        cc = chunk
        pre_load(cc)
        tok_l = state["tok_l"].pop(cc)
        tok1 = b.step("vector", lambda v, cc=cc: v.scalar_tensor_tensor(
            x1sb[:, :], f_ap[:, :], half_scale, obuf[cc % 2][:, :],
            ALU.mult, ALU.add),
            deps=[tok_l, dep, state["tok_mm"], state["tok_st"]])
        state["tok_use"][0] = state["tok_use"][1] = tok1
        pre_load(cc + 1)
        tok_st = b.dma("sync", lambda e, s, cc=cc: (
            e.dma_start(out=xres_out[128 * cc:128 * (cc + 1), :], in_=x1sb[:, :]).then_inc(s, 16),),
            1, deps=[tok1], lane="st1")
        state["tok_st"] = tok_st
        st_toks.append(tok_st)
        tok_sq = b.step("scalar", lambda sc: sc.activation(
            xsq2[:, :], x1sb[:, :], AF.Square), deps=[tok1, state["tok_mm"]])

        def flush(cc=cc, tok_sq=tok_sq, tok1=tok1):
            def mm(t, cc=cc):
                first = (cc == 0)
                last = (cc == 31)
                for tb in range(2):
                    t.matmul(ps_s[0:1, 512 * tb:512 * (tb + 1)], ones_f[:, :],
                             x1sb[:, 512 * tb:512 * (tb + 1)], start=first, stop=last)
                for tb in range(2):
                    i = t.matmul(ps_q[0:1, 512 * tb:512 * (tb + 1)], ones_b[:, :],
                                 xsq2[:, 512 * tb:512 * (tb + 1)], start=first, stop=last)
                return i
            state["tok_mm"] = b.step("tensor", mm, deps=[tok_sq, tok1])
        P["pending_stats"] = flush
        return tok1

    emit_ffn0(b, P, xln, v_xln, srep, tok_rep, out_cb)
    v_stats2 = state["tok_mm"]

    rrep2, brep2 = P["rrep2"], P["brep2"]
    srep2 = P.get("srep2")
    tok_rep2 = emit_ln_finalize(b, ps_s, ps_q, stat, scr, 4, rrep2, brep2,
                                v_stats2, ntok, srep=srep2)

    xlt = P["xlt"]
    st2 = []
    tok_prev = None
    for cc in range(32):
        buf = xcb[cc % 2]
        tok_l = b.dma("sync", lambda e, s, cc=cc, buf=buf: (
            e.dma_start(out=buf[:, :], in_=xres_out[128 * cc:128 * (cc + 1), :]).then_inc(s, 16),),
            1, deps=[st_toks[-1], tok_prev], lane=f"ld{cc % 2}")

        b.step("vector", lambda v, buf=buf: v.tensor_tensor(
            buf[:, :], buf[:, :], rrep2[:, :], ALU.mult), deps=[tok_l, tok_rep2])
        tok_a = b.step("vector", lambda v, buf=buf: v.tensor_tensor(
            xlt[:, :], buf[:, :], brep2[:, :], ALU.subtract))
        tok_s = b.dma("sync", lambda e, s, cc=cc: (
            e.dma_start(out=xln_next_out[128 * cc:128 * (cc + 1), :], in_=xlt[:, :]).then_inc(s, 16),),
            1, deps=[tok_a], lane="st2")
        st2.append(tok_s)
        tok_prev = tok_s
    return st_toks, st2, tok_rep2


def make_shared_pool(b, PS, with_ffn0=True, fbf=None):
    P = {}
    P["scr"] = b.dram("scr", [64, TPC], F32)
    P["idf"], tok_idf = emit_ident(b, "idf", F32)
    P["xcb"] = [b.sb(f"xc{i}", [128, TPC], F32) for i in range(2)]
    P["xc3"] = P["xcb"][1]
    P["xsq2"] = b.sb("xsq2", [128, TPC], BF16)
    P["xsq"] = P["xsq2"]
    P["x1sb"] = b.sb("x1sb", [128, TPC], F32)
    P["xlt"] = P["xsq2"]
    P["ones_f"] = b.sb("ones_f", [128, 1], F32)
    P["ones_b"] = b.sb("ones_b", [128, 1], BF16)
    P["tok_ones"] = b.step("gpsimd", lambda g: (g.memset(P["ones_f"][:, :], 1.0),
                                                g.memset(P["ones_b"][:, :], 1.0))[-1])
    P["stat"] = (b.sb("stat1", [128, TPC], F32), b.sb("stat2", [128, TPC], F32))
    P["rrep"] = b.sb("rrep", [128, TPC], F32)
    P["brep"] = b.sb("brep", [128, TPC], F32)
    P["rrep2"] = P["rrep"]
    P["brep2"] = P["brep"]
    P["ps_s"] = PS["ps_s"]
    P["ps_q"] = PS["ps_q"]
    if with_ffn0:
        P["srep"] = P["x1sb"]
        P["wcat_sb"] = b.sb("wcat_sb", [128, 32 * 24], BF16)
        P["wbr_sb"] = b.sb("wbr_sb", [128, 8 * 1024], BF16)
        P["hb"] = b.sb("hb", [32, 1], F32)
        P["xln"] = b.sb("xln", [128, 32 * TPC], BF16)
        P["ps_log"] = PS["ps_log"]
        P["ps_misc"] = PS["ps_misc"]
        if fbf is None:
            fbf = b.sb("fbf", [128, 14336], BF16)
        P["fbf"] = fbf
    return P


def load_ffn0_weights(b, P, wcat, wbr, hbase):
    P["hbase_dram"] = hbase
    P["tok_w"] = b.dma("gpsimd", lambda e, s: (
        e.dma_start(out=P["wcat_sb"][:, :].rearrange("p (cc o) -> p cc o", o=24),
                    in_=wcat.rearrange("(cc p) o -> p cc o", p=128)).then_inc(s, 16),
        e.dma_start(out=P["wbr_sb"][:, :].rearrange("p (cc o) -> p cc o", o=1024),
                    in_=wbr.rearrange("(cc p) o -> p cc o", p=128)).then_inc(s, 16),
        e.dma_start(out=P["hb"][:, :],
                    in_=hbase.rearrange("(p o) -> p o", o=1)).then_inc(s, 16)), 3, lane="wl")



def emit_dequant(b, PS, wp, ws, dq_own):
    idb = b.sb("dq_idb", [128, 128], BF16)
    b.step("gpsimd", lambda g: g.memset(idb[:, :], 1.0))
    tok_id = b.step("gpsimd", lambda g: g.affine_select(
        out=idb[:, :], in_=idb[:, :], compare_op=ALU.is_equal, fill=0.0, base=0,
        pattern=[[-1, 128]], channel_multiplier=1))
    wp_sb = b.sb("dq_wp", [128, 256], I32)
    ws_sb = b.sb("dq_ws", [128, 4], F32)
    lo = b.sb("dq_lo", [128, 256], I32)
    dqb = b.sb("dq_dqb", [128, 512], BF16)
    tp = b.sb("dq_tp", [128, 512], BF16)
    pst = PS["dq_ps"]
    dq_v = dq_own.rearrange("(ic p) o -> p ic o", p=128)
    tok_store = None
    tok_prev = None
    for oc in range(32):
        tok_l = b.dma("sync", lambda e, s, oc=oc: (
            e.dma_start(out=wp_sb[:, :], in_=wp[128 * oc:128 * (oc + 1), :]).then_inc(s, 16),
            e.dma_start(out=ws_sb[:, :], in_=ws[128 * oc:128 * (oc + 1), :]).then_inc(s, 16)),
            2, deps=[tok_prev], lane="dql")
        d0 = dqb[:, :]
        dl = bass.AP(d0.tensor, d0.offset, [list(d0.ap[0]), [2, 256]])
        dh = bass.AP(d0.tensor, dqb[:, 1:2].offset, [list(d0.ap[0]), [2, 256]])
        wv = brd(ws_sb[:, 0:4], [[1, 4], [0, 128]])
        dv = brd(d0, [[128, 4], [1, 128]])
        b.step("vector", lambda v: v.tensor_single_scalar(
            lo[:, :], wp_sb[:, :], 15, ALU.bitwise_and),
            deps=[tok_l, tok_store])
        b.step("vector", lambda v, dl=dl: v.tensor_single_scalar(
            dl, lo[:, :], 8, ALU.subtract))
        b.step("vector", lambda v: v.tensor_single_scalar(
            lo[:, :], wp_sb[:, :], 4, ALU.logical_shift_right))
        b.step("vector", lambda v, dh=dh: v.tensor_single_scalar(
            dh, lo[:, :], 8, ALU.subtract))
        tok_u = b.step("vector", lambda v, dv=dv, wv=wv: v.tensor_tensor(
            dv, dv, wv, ALU.mult))
        ev_toks = []
        tok_t = None
        for ic in range(4):
            tok_t = b.step("tensor", lambda t, ic=ic: t.transpose(
                pst[:, :], dqb[:, 128 * ic:128 * (ic + 1)], idb[:, :]),
                deps=[tok_u, tok_id] if ic == 0 else [ev_toks[-1]])
            tok_e = b.step("vector", lambda v, ic=ic: v.tensor_copy(
                tp[:, 128 * ic:128 * (ic + 1)], pst[:, :]), deps=[tok_t])
            ev_toks.append(tok_e)
        tok_prev = ev_toks[-1]
        tok_store = b.dma("sync", lambda e, s, oc=oc: (
            e.dma_start(
                out=dq_v[:, :, 128 * oc:128 * (oc + 1)],
                in_=tp[:, :].rearrange("p (ic o) -> p ic o", o=128)).then_inc(s, 16),),
            1, deps=[ev_toks[-1]], lane="dqs")
    return tok_store



NOC = 19
NTB = 16
NCH = 64


def emit_a2a(b, PS, d):
    winT = d["winT"]
    convw = d["convw"]
    convb = d["convb"]
    dtb = d["dtb"]
    xconv_d, gate_d, dt_d = d["xconv_d"], d["gate_d"], d["dt_d"]
    xgA, xgB = d["x_guards"]

    xvH = [a.rearrange("(r cc p) t -> p r cc t", p=128, cc=16)
           for a in d["xallH"]]

    convw_sb = b.sb("convw_sb", [128, 40], F32)
    convb_sb = b.sb("convb_sb", [128, 10], F32)
    dtb_sb = b.sb("dtb_sb", [16, 1], F32)
    tok_sml = b.dma("gpsimd", lambda e, s: (
        e.dma_start(out=convw_sb[:, :].rearrange("p (o k) -> p o k", k=4),
                    in_=convw.rearrange("(o p) k -> p o k", p=128)).then_inc(s, 16),
        e.dma_start(out=convb_sb[:, :],
                    in_=convb.rearrange("(o p) -> p o", p=128), allow_slow_non_contiguous=True).then_inc(s, 16),
        e.dma_start(out=dtb_sb[:, :], in_=dtb.rearrange("(p o) -> p o", o=1)).then_inc(s, 16)),
        3, lane="sml")

    wh = b.sb("wh", [128, 32 * 10 * 128], BF16)
    xtb = [b.sb(f"xt{i}", [128, 32 * 512], BF16) for i in range(2)]
    ext = b.sb("ext", [128, 515], F32)
    halos = b.sb("halos", [128, 30], F32)
    gevac = b.sb("gevac", [128, 512], BF16)
    cvout = b.sb("cvout", [128, 512], BF16)
    devac = b.sb("devac", [128, 512], F32)
    dwork = b.sb("dwork", [64, 512], F32)
    dwork2 = b.sb("dwork2", [16, 512], F32)
    ps1 = [PS["a2_ps1a"], PS["a2_ps1b"]]

    tok_hal = b.step("gpsimd", lambda g: g.memset(halos[:, :], 0.0))
    halves = [list(range(0, 10)), list(range(10, NOC))]
    ev_tok = {}
    st_g = None
    st_c = None
    st_d = None
    half_mm_barrier = None
    for hidx, ocs in enumerate(halves):
        no = len(ocs)
        tok_w = b.dma("gpsimd", lambda e, s, ocs=ocs, no=no: (
            e.dma_start(out=wh[:, 0:32 * no * 128].rearrange("p (cc o) -> p cc o", cc=32),
                        in_=winT.rearrange("(cc p) o -> p cc o", p=128)[:, :, 128 * ocs[0]:128 * (ocs[-1] + 1)]).then_inc(s, 16),),
            1, deps=[half_mm_barrier], lane="wl")
        tb_barrier = [None, None]
        for tb in range(NTB):
            xt = xtb[tb % 2]
            tok_xa = b.dma("sync", lambda e, s, tb=tb, xt=xt: (
                e.dma_start(out=xt[:, 0:16 * 512].rearrange("p (cc t) -> p cc t", cc=16),
                            in_=xvH[0][:, tb // 2, :, (tb % 2) * 512:(tb % 2) * 512 + 512]).then_inc(s, 16),),
                1, deps=[tb_barrier[tb % 2], xgA], lane="xl")
            tok_xb = b.dma("sync", lambda e, s, tb=tb, xt=xt: (
                e.dma_start(out=xt[:, 16 * 512:32 * 512].rearrange("p (cc t) -> p cc t", cc=16),
                            in_=xvH[1][:, tb // 2, :, (tb % 2) * 512:(tb % 2) * 512 + 512]).then_inc(s, 16),),
                1, deps=[xgB], lane="xl")
            for j, oc in enumerate(ocs):
                bank = (tb * no + j) % 2
                ps = ps1[bank]

                def mmA(t, j=j, no=no, ps=ps, xt=xt):
                    for cc in range(16):
                        i = t.matmul(ps[:, :], wh[:, (cc * no + j) * 128:(cc * no + j) * 128 + 128],
                                     xt[:, 512 * cc:512 * cc + 512],
                                     start=(cc == 0), stop=False)
                    return i

                def mmB(t, j=j, no=no, ps=ps, xt=xt):
                    for cc in range(16, 32):
                        i = t.matmul(ps[:, :], wh[:, (cc * no + j) * 128:(cc * no + j) * 128 + 128],
                                     xt[:, 512 * cc:512 * cc + 512],
                                     start=False, stop=(cc == 31))
                    return i
                b.step("tensor", mmA, deps=[tok_w, tok_xa, ev_tok.get(bank)])
                tok_mm = b.step("tensor", mmB, deps=[tok_xb])
                if oc <= 7:
                    tok_e = b.step("scalar", lambda sc, ps=ps: sc.copy(gevac[:, :], ps[:, :]),
                                   deps=[tok_mm, st_g])
                    ev_tok[bank] = tok_e
                    st_g = b.dma("sync", lambda e, s, oc=oc, tb=tb: (
                        e.dma_start(out=gate_d[128 * oc:128 * (oc + 1), 512 * tb:512 * (tb + 1)],
                                    in_=gevac[:, :]).then_inc(s, 16),), 1,
                        deps=[tok_e], lane="sg")
                elif oc <= 17:
                    ock = oc - 8
                    tok_e = b.step("scalar", lambda sc, ps=ps: sc.copy(ext[:, 3:515], ps[:, :]),
                                   deps=[tok_mm])
                    ev_tok[bank] = tok_e
                    tok_h = b.step("vector", lambda v, ock=ock: v.tensor_copy(
                        ext[:, 0:3], halos[:, 3 * ock:3 * ock + 3]),
                        deps=[tok_e, tok_hal, tok_sml])
                    b.step("vector", lambda v, ock=ock: v.tensor_scalar(
                        devac[:, :], ext[:, 0:512], convw_sb[:, 4 * ock:4 * ock + 1],
                        0.0, ALU.mult, ALU.add), deps=[tok_h, st_d])
                    for k in range(1, 4):
                        b.step("vector", lambda v, ock=ock, k=k: v.scalar_tensor_tensor(
                            devac[:, :], ext[:, k:k + 512], convw_sb[:, 4 * ock + k:4 * ock + k + 1],
                            devac[:, :], ALU.mult, ALU.add))
                    b.step("vector", lambda v, ock=ock: v.tensor_scalar(
                        devac[:, :], devac[:, :], convb_sb[:, ock:ock + 1], 0.0,
                        ALU.add, ALU.add))
                    tok_hs = b.step("vector", lambda v, ock=ock: v.tensor_copy(
                        halos[:, 3 * ock:3 * ock + 3], ext[:, 512:515]))
                    tok_sl = b.step("scalar", lambda sc: sc.activation(
                        cvout[:, :], devac[:, :], AF.Silu), deps=[tok_hs, st_c])
                    st_c = b.dma("sync", lambda e, s, ock=ock, tb=tb: (
                        e.dma_start(out=xconv_d[128 * ock:128 * (ock + 1), 512 * tb:512 * (tb + 1)],
                                    in_=cvout[:, :]).then_inc(s, 16),), 1,
                        deps=[tok_sl], lane="sc")
                else:
                    tok_e = b.step("scalar", lambda sc, ps=ps: sc.copy(devac[:, :], ps[:, :]),
                                   deps=[tok_mm, st_d])
                    ev_tok[bank] = tok_e
                    tok_v1 = b.step("vector", lambda v: v.tensor_scalar(
                        devac[0:16, :], devac[0:16, :], dtb_sb[0:16, 0:1], 0.0,
                        ALU.add, ALU.add), deps=[tok_e, tok_sml])
                    b.step("scalar", lambda sc: sc.activation(
                        dwork[0:16, :], devac[0:16, :], AF.Abs), deps=[tok_v1])
                    b.step("scalar", lambda sc: sc.activation(
                        dwork[0:16, :], dwork[0:16, :], AF.Exp, scale=-1.0))
                    tok_l1 = b.step("scalar", lambda sc: sc.activation(
                        dwork[0:16, :], dwork[0:16, :], AF.Ln, bias=1.0))
                    tok_l2 = b.step("scalar", lambda sc: sc.activation(
                        dwork2[0:16, :], devac[0:16, :], AF.Relu))
                    b.step("vector", lambda v: v.tensor_tensor(
                        devac[0:16, :], dwork[0:16, :], dwork2[0:16, :], ALU.add),
                        deps=[tok_l1, tok_l2])
                    b.step("vector", lambda v: v.tensor_single_scalar(
                        devac[0:16, :], devac[0:16, :], 100.0, ALU.min))
                    tok_d = b.step("vector", lambda v: v.tensor_single_scalar(
                        devac[0:16, :], devac[0:16, :], 0.001, ALU.max))
                    st_d = b.dma("sync", lambda e, s, tb=tb: (
                        e.dma_start(out=dt_d[:, 512 * tb:512 * (tb + 1)],
                                    in_=devac[0:16, :]).then_inc(s, 16),), 1,
                        deps=[tok_d], lane="sd")
            tb_barrier[tb % 2] = (b.esem["tensor"], b.ecnt["tensor"])
        half_mm_barrier = (b.esem["tensor"], b.ecnt["tensor"])
    return [st_g, st_c, st_d, (b.esem["vector"], b.ecnt["vector"]),
            (b.esem["scalar"], b.ecnt["scalar"])]


def emit_a2b(b, PS, d, ag_cb):
    xconv_d, gate_d, dt_d, y_d, scr2 = (d["xconv_d"], d["gate_d"], d["dt_d"],
                                        d["y_d"], d["scr2"])
    avec, dexp, nwv = d["avec"], d["dexp"], d["nwv"]

    idf, tok_idf = emit_ident(b, "idf", F32)
    idb, tok_idb = emit_ident(b, "idb", BF16)
    oneh = b.sb("oneh", [16, 2048], F32)
    b.step("gpsimd", lambda g: g.memset(oneh[:, :], 1.0))
    tok_oneh = b.step("gpsimd", lambda g: g.affine_select(
        out=oneh[:, :], in_=oneh[:, :], compare_op=ALU.is_equal, fill=0.0,
        base=0, pattern=[[-1, 16], [0, 128]], channel_multiplier=1))
    ones16 = b.sb("ones16", [16, 128], F32)
    tok_o16 = b.step("gpsimd", lambda g: g.memset(ones16[:, :], 1.0))
    I16f = b.sb("I16f", [16, 16], F32)
    b.step("gpsimd", lambda g: g.memset(I16f[:, :], 1.0))
    tok_i16 = b.step("gpsimd", lambda g: g.affine_select(
        out=I16f[:, :], in_=I16f[:, :], compare_op=ALU.is_equal, fill=0.0,
        base=0, pattern=[[-1, 16]], channel_multiplier=1))
    avec_sb = b.sb("avec_sb", [16, 1], F32)
    dcol_sb = b.sb("dcol_sb", [128, 8], F32)
    nw_sb = b.sb("nw_sb", [128, 8], F32)
    tok_sml = b.dma("gpsimd", lambda e, s: (
        e.dma_start(out=avec_sb[:, :], in_=avec.rearrange("(p o) -> p o", o=1)).then_inc(s, 16),
        e.dma_start(out=dcol_sb[:, :], in_=dexp.rearrange("(o p) -> p o", p=128), allow_slow_non_contiguous=True).then_inc(s, 16),
        e.dma_start(out=nw_sb[:, :], in_=nwv.rearrange("(o p) -> p o", p=128), allow_slow_non_contiguous=True).then_inc(s, 16)),
        3, lane="sml")

    BUF = []
    for i in range(2):
        BUF.append(dict(
            xc_b=b.sb(f"xc_b{i}", [128, 8 * 128], BF16),
            BC=b.sb(f"BC{i}", [128, 2 * 128], BF16),
            sta=b.sb(f"sta{i}", [128, 128], F32),
            staT=b.sb(f"staT{i}", [128, 128], F32),
            negl=b.sb(f"negl{i}", [48, 1], F32),
            rep=b.sb(f"rep{i}", [128, 2048], F32),
            repm=b.sb(f"repm{i}", [128, 2048], F32),
            e_all=b.sb(f"e_all{i}", [128, 2048], BF16),
            G=b.sb(f"G{i}", [128, 2048], BF16),
            Mt=b.sb(f"Mt{i}", [128, 2048], BF16),
            rhs_st=b.sb(f"rhs_st{i}", [128, 2048], BF16),
            rhs_yo=b.sb(f"rhs_yo{i}", [128, 2048], BF16),
            xdtT=b.sb(f"xdtT{i}", [128, 1024], BF16),
            BcT=b.sb(f"BcT{i}", [128, 128], BF16),
            scT=b.sb(f"scT{i}", [128, 128], BF16),
            cdrep=b.sb(f"cdrep{i}", [128, 16], F32),
            cdd=b.sb(f"cdd{i}", [16, 16], F32),
            csb=b.sb(f"csb{i}", [16, 128], F32),
        ))
    prevT = b.sb("prevT", [128, 1024], F32)
    prevTb = b.sb("prevTb", [128, 1024], BF16)
    cdexp = b.sb("cdexp", [128, 1024], F32)
    ysb = [b.sb(f"ysb{i}", [128, 1024], BF16) for i in range(2)]
    maskT = b.sb("maskT", [128, 128], F32)
    ps_sc = PS["a2_ps_sc"]
    ps_yp = [PS["a2_ps_ypa"], PS["a2_ps_ypb"]]
    ps_bc = PS["a2_ps_bc"]
    ps_st2 = PS["a2_ps_st2"]
    ps_trb = PS["b16"]
    ps_q3 = PS["a2_ps_q3"]

    b.step("gpsimd", lambda g: g.memset(maskT[:, :], 0.0))
    tok_msk = b.step("gpsimd", lambda g: g.affine_select(
        out=maskT[:, :], in_=maskT[:, :], compare_op=ALU.is_ge, fill=-1e30,
        base=0, pattern=[[1, 128]], channel_multiplier=-1))
    tok_prev = b.step("vector", lambda v: v.memset(prevT[:, :], 0.0))
    tok_prevb = b.step("vector", lambda v: v.memset(prevTb[:, :], 0.0))

    heads_done = {}
    prep_done = {}
    load_done = {}
    sce_done = {}
    state = {"tok_su": tok_prevb, "sub_tok": tok_prevb, "evp": {},
             "y_st": [None, None], "bc_guard": [None, None]}

    def emit_load(c):
        i = c % 2
        B_ = BUF[i]
        guard = heads_done.get(c - 2)
        ga = prep_a_done.get(c - 2, [])
        tok_ld = b.dma("sync", lambda e, s, c=c, B_=B_: (
            e.dma_start(out=B_["xc_b"][:, :].rearrange("p (cc l) -> p cc l", cc=8),
                        in_=xconv_d[0:1024].rearrange("(cc p) t -> p cc t", p=128)[:, :, 128 * c:128 * (c + 1)]).then_inc(s, 16),
            e.dma_start(out=B_["BC"][:, 0:128], in_=xconv_d[1024:1152, 128 * c:128 * (c + 1)]).then_inc(s, 16),
            e.dma_start(out=B_["BC"][:, 128:256], in_=xconv_d[1152:1280, 128 * c:128 * (c + 1)]).then_inc(s, 16)),
            3, deps=[guard] + ga, lane=f"p2x{i}")
        tok_dt = b.dma("sync", lambda e, s, c=c, B_=B_: (
            e.dma_start(out=B_["sta"][0:16, :], in_=dt_d[:, 128 * c:128 * (c + 1)]).then_inc(s, 16),),
            1, deps=[guard] + ga, lane=f"p2d{i}")
        load_done[c] = (tok_ld, tok_dt)

    prep_a_done = {}

    def emit_prep_a(c):
        i = c % 2
        B_ = BUF[i]
        tok_ld, tok_dt = load_done[c]
        b.step("vector", lambda v: v.tensor_scalar(
            B_["sta"][32:48, :], B_["sta"][0:16, :], avec_sb[0:16, 0:1], 0.0,
            ALU.mult, ALU.add), deps=[tok_dt, tok_sml])
        tok_cs = b.step("vector", lambda v: v.tensor_tensor_scan(
            B_["sta"][32:48, :], B_["sta"][32:48, :], B_["sta"][32:48, :], 0.0,
            ALU.add, ALU.bypass))
        tok_ng = b.step("vector", lambda v: v.tensor_copy(
            B_["negl"][32:48, 0:1], B_["sta"][32:48, 127:128]))
        tok_dec = b.step("scalar", lambda sc: sc.activation(
            B_["sta"][64:80, :], B_["sta"][32:48, :], AF.Exp,
            bias=B_["negl"][32:48, 0:1], scale=-1.0), deps=[tok_ng])
        tok_tr = b.step("tensor", lambda t: t.transpose(
            ps_sc[:, :], B_["sta"][:, :], idf[:, :]),
            deps=[tok_dec, tok_idf, tok_cs, sce_done.get(c - 1)])
        tok_trev = b.step("vector", lambda v: v.tensor_copy(
            B_["staT"][:, :], ps_sc[:, :]), deps=[tok_tr])
        tok_csb = b.step("vector", lambda v: v.tensor_copy(
            B_["csb"][:, :], B_["sta"][32:48, :]), deps=[tok_cs])
        tok_cdd = b.step("vector", lambda v: v.tensor_tensor(
            B_["cdd"][:, :], brd(B_["csb"][:, 127:128], [[0, 16]]),
            I16f[:, :], ALU.mult), deps=[tok_csb, tok_i16])
        bg = state["bc_guard"]
        tok_cdm = b.step("tensor", lambda t: t.matmul(
            ps_bc[:, 0:16], ones16[:, :], B_["cdd"][:, :],
            start=True, stop=True),
            deps=[tok_cdd, tok_o16, bg[0]])
        tok_cdr = b.step("vector", lambda v: v.tensor_copy(
            B_["cdrep"][:, :], ps_bc[:, 0:16]), deps=[tok_cdm])
        ev_bank = [tok_cdr, bg[1]]
        for r4 in range(4):
            bank = r4 % 2
            tok_rm = b.step("tensor", lambda t, r4=r4, bank=bank: [
                t.matmul(ps_bc[:, 512 * bank + 128 * hh:512 * bank + 128 * (hh + 1)],
                         oneh[:, 128 * (4 * r4 + hh):128 * (4 * r4 + hh + 1)],
                         B_["csb"][:, :], start=True, stop=True)
                for hh in range(4)][-1],
                deps=[tok_csb, tok_oneh, ev_bank[bank]])
            ev_bank[bank] = b.step("vector", lambda v, r4=r4, bank=bank: v.tensor_copy(
                B_["rep"][:, 512 * r4:512 * (r4 + 1)],
                ps_bc[:, 512 * bank:512 * bank + 512]),
                deps=[tok_rm])
        tok_rr = ev_bank[1]
        state["bc_guard"] = ev_bank
        tok_scm = b.step("tensor", lambda t: t.matmul(
            ps_sc[:, :], B_["BC"][:, 0:128], B_["BC"][:, 128:256],
            start=True, stop=True), deps=[tok_ld, tok_trev])
        tok_sce = b.step("vector", lambda v: v.tensor_copy(
            B_["scT"][:, :], ps_sc[:, :]), deps=[tok_scm])
        sce_done[c] = tok_sce
        B_["toks"] = dict(ld=tok_ld, trev=tok_trev, cdr=tok_cdr, rr=tok_rr,
                          sce=tok_sce)
        prep_a_done[c] = [tok_rr, tok_cdr]

    def emit_prep_b(c):
        i = c % 2
        B_ = BUF[i]
        tk = B_["toks"]
        tok_ld, tok_trev, tok_rr = tk["ld"], tk["trev"], tk["rr"]
        tok_cde = b.step("scalar", lambda sc: sc.activation(
            B_["cdrep"][:, :], B_["cdrep"][:, :], AF.Exp), deps=[tk["cdr"]])
        tok_ea = b.step("scalar", lambda sc: sc.activation(
            B_["e_all"][:, :], B_["rep"][:, :], AF.Exp), deps=[tok_rr])
        b.step("vector", lambda v: v.tensor_tensor(
            B_["repm"][:, :], B_["rep"][:, :], brd(maskT[:, :], [[0, 16], [1, 128]]),
            ALU.add), deps=[tok_rr, tok_msk])
        tok_m = None
        for hf in range(2):
            sl = slice(1024 * hf, 1024 * (hf + 1))
            tok_g1 = b.step("vector", lambda v, sl=sl, hf=hf: v.tensor_tensor(
                B_["repm"][:, sl], B_["repm"][:, sl],
                brd(B_["staT"][:, 32 + 8 * hf:40 + 8 * hf], [[1, 8], [0, 128]]),
                ALU.subtract), deps=[tok_trev])
            tok_g = b.step("scalar", lambda sc, sl=sl: sc.activation(
                B_["G"][:, sl], B_["repm"][:, sl], AF.Exp),
                deps=[tok_g1, tok_ea])
            tok_m = b.step("vector", lambda v, sl=sl, hf=hf: v.tensor_tensor(
                B_["Mt"][:, sl], B_["G"][:, sl],
                brd(B_["scT"][:, :], [[0, 8], [1, 128]]),
                ALU.mult), deps=[tok_g, tk["sce"]])
        prev_pb = prep_done.get(c - 1)
        tok_bt = b.step("tensor", lambda t: t.transpose(
            ps_trb[:, :], B_["BC"][:, 0:128], idb[:, :]),
            deps=[tok_ld, tok_idb, tk["sce"],
                  prev_pb[3] if prev_pb else None])
        tok_bte = b.step("vector", lambda v: v.tensor_copy(
            B_["BcT"][:, :], ps_trb[:, :]), deps=[tok_bt])
        tok_rs = b.step("vector", lambda v: v.tensor_tensor(
            B_["rhs_st"][:, :].rearrange("p (h n) -> p h n", h=16),
            brd(B_["BcT"][:, :], [[0, 16], [1, 128]]),
            brd(B_["staT"][:, 64:80], [[1, 16], [0, 128]]), ALU.mult),
            deps=[tok_bte, tok_trev])
        tok_ry = b.step("vector", lambda v: v.tensor_tensor(
            B_["rhs_yo"][:, :].rearrange("p (h l) -> p h l", h=16),
            brd(B_["BC"][:, 128:256], [[0, 16], [1, 128]]),
            B_["e_all"][:, :].rearrange("p (h l) -> p h l", h=16),
            ALU.mult), deps=[tok_ea, tok_ld])
        tok_xdt = tok_bte
        for hp in range(8):
            tok_t2 = b.step("tensor", lambda t, hp=hp: t.transpose(
                ps_trb[:, :], B_["xc_b"][:, 128 * hp:128 * (hp + 1)], idb[:, :]),
                deps=[tok_ld, tok_xdt])
            tok_xdt = b.step("vector", lambda v, hp=hp: v.tensor_tensor(
                B_["xdtT"][:, 128 * hp:128 * (hp + 1)].rearrange("p (h q) -> p h q", h=2),
                ps_trb[:, :].rearrange("p (h q) -> p h q", h=2),
                brd(B_["staT"][:, 2 * hp:2 * hp + 2], [[1, 2], [0, 64]]), ALU.mult),
                deps=[tok_t2, tok_trev])
        prep_done[c] = [tok_m, tok_rs, tok_ry, tok_xdt, tok_cde]

    def emit_heads(c):
        i = c % 2
        B_ = BUF[i]
        evp = state["evp"]
        pd = prep_done[c]
        tok_c2 = b.step("tensor", lambda t, B_=B_: [
            t.matmul(ps_st2[:, 64 * h:64 * h + 64],
                     B_["rhs_st"][:, 128 * h:128 * (h + 1)],
                     B_["xdtT"][:, 64 * h:64 * h + 64], start=True, stop=True)
            for h in range(16)][-1],
            deps=pd + [state["tok_su"]])
        for hp in range(8):
            ps = ps_yp[hp % 2]

            def mmab(t, hp=hp, ps=ps, B_=B_):
                for hh in range(2):
                    h = 2 * hp + hh
                    t.matmul(ps[64 * hh:64 * hh + 64, :],
                             B_["xdtT"][:, 64 * h:64 * h + 64],
                             B_["Mt"][:, 128 * h:128 * (h + 1)],
                             start=True, stop=False)
                    i2 = t.matmul(ps[64 * hh:64 * hh + 64, :],
                                  prevTb[:, 64 * h:64 * h + 64],
                                  B_["rhs_yo"][:, 128 * h:128 * (h + 1)],
                                  start=False, stop=True)
                return i2
            tok_ab = b.step("tensor", mmab,
                            deps=pd + [evp.get(hp % 2), state["sub_tok"]])
            evp[hp % 2] = b.step("vector", lambda v, hp=hp, ps=ps, B_=B_, i=i: v.scalar_tensor_tensor(
                ysb[i][:, 128 * hp:128 * (hp + 1)], B_["xc_b"][:, 128 * hp:128 * (hp + 1)],
                dcol_sb[:, hp:hp + 1], ps[:, :], ALU.mult, ALU.add),
                deps=[tok_ab, tok_sml, state["y_st"][i]])
        tok_cde = pd[4]
        b.step("vector", lambda v, B_=B_: v.tensor_copy(
            cdexp[:, :].rearrange("p (h q) -> p h q", q=64),
            brd(B_["cdrep"][:, :], [[1, 16], [0, 64]])), deps=[tok_cde])
        tok_su = b.step("vector", lambda v: v.tensor_tensor(
            prevT[:, :], prevT[:, :], cdexp[:, :], ALU.mult), deps=[tok_c2])
        tok_su = b.step("vector", lambda v: v.tensor_tensor(
            prevT[:, :], prevT[:, :], ps_st2[:, 0:1024], ALU.add), deps=[tok_su])
        state["tok_su"] = tok_su
        state["sub_tok"] = b.step("vector", lambda v: v.tensor_copy(
            prevTb[:, :], prevT[:, :]))
        heads_done[c] = b.step("vector", lambda v, B_=B_: v.tensor_copy(
            B_["staT"][0:1, 0:1], B_["staT"][0:1, 0:1]),
            deps=[evp.get(0), evp.get(1), state["sub_tok"]])
        state["y_st"][i] = b.dma("sync", lambda e, s, c=c, i=i: (
            e.dma_start(out=y_d.rearrange("(hp p) t -> p hp t", p=128)[:, :, 128 * c:128 * (c + 1)],
                        in_=ysb[i][:, :].rearrange("p (hp l) -> p hp l", hp=8)).then_inc(s, 16),),
            1, deps=[evp.get(0), evp.get(1)], lane=f"p2y{i}")

    emit_load(0)
    emit_load(1)
    emit_prep_a(0)
    emit_prep_a(1)
    emit_prep_b(0)
    for c in range(NCH):
        emit_heads(c)
        if c + 2 < NCH:
            emit_load(c + 2)
            emit_prep_a(c + 2)
        if c + 1 < NCH:
            emit_prep_b(c + 1)
    y_done = [state["y_st"][0], state["y_st"][1]]

    ones_b2 = b.sb("ones_b2", [128, 1], BF16)
    tok_ones2 = b.step("gpsimd", lambda g: g.memset(ones_b2[:, :], 1.0))
    NBP3 = 4
    ybuf = [b.sb(f"ybuf{i}", [128, 512], BF16) for i in range(NBP3)]
    gbuf = [b.sb(f"gbuf{i}", [128, 512], BF16) for i in range(NBP3)]
    sqb = [b.sb(f"sqb{i}", [128, 512], BF16) for i in range(NBP3)]
    youtb = [b.sb(f"youtb{i}", [128, 512], BF16) for i in range(NBP3)]
    rsacc = b.sb("rsacc", [1, 8192], F32)
    rrepA = b.sb("rrepA", [128, 8192], F32)

    use_guard = [None] * NBP3
    last_mm3 = None
    tok_ev3 = None
    for tb3 in range(16):
        for cc in range(8):
            i = (tb3 * 8 + cc) % NBP3
            tok_ly = b.dma("sync", lambda e, s, tb3=tb3, cc=cc, i=i: (
                e.dma_start(out=ybuf[i][:, :],
                            in_=y_d[128 * cc:128 * (cc + 1), 512 * tb3:512 * (tb3 + 1)]).then_inc(s, 16),
                e.dma_start(out=gbuf[i][:, :],
                            in_=gate_d[128 * cc:128 * (cc + 1), 512 * tb3:512 * (tb3 + 1)]).then_inc(s, 16)),
                2, deps=[y_done[0], y_done[1], use_guard[i]], lane=f"p3l{i % 2}")
            tok_sg = b.step("scalar", lambda sc, i=i: sc.activation(
                gbuf[i][:, :], gbuf[i][:, :], AF.Silu), deps=[tok_ly])
            tok_yg = b.step("vector", lambda v, i=i: v.tensor_tensor(
                sqb[i][:, :], ybuf[i][:, :], gbuf[i][:, :], ALU.mult),
                deps=[tok_sg])
            tok_sq = b.step("scalar", lambda sc, i=i: sc.activation(
                sqb[i][:, :], sqb[i][:, :], AF.Square), deps=[tok_yg])
            tok_mm = b.step("tensor", lambda t, cc=cc, i=i: t.matmul(
                ps_q3[0:1, 0:512], ones_b2[:, :], sqb[i][:, :],
                start=(cc == 0), stop=(cc == 7)),
                deps=[tok_sq, last_mm3, tok_ones2])
            last_mm3 = tok_mm
            use_guard[i] = tok_mm
        tok_ev3 = b.step("vector", lambda v, tb3=tb3: v.tensor_single_scalar(
            rsacc[0:1, 512 * tb3:512 * (tb3 + 1)], ps_q3[0:1, 0:512],
            1.0 / 1024.0, ALU.mult), deps=[last_mm3, tok_ev3])
    b.step("vector", lambda v: v.tensor_single_scalar(
        rsacc[0:1, :], rsacc[0:1, :], MAMBA_EPS, ALU.add), deps=[tok_ev3])
    tok_rv = b.step("vector", lambda v: v.reciprocal(rsacc[0:1, :], rsacc[0:1, :]))
    tok_rs3 = b.step("scalar", lambda sc: sc.activation(
        rsacc[0:1, :], rsacc[0:1, :], AF.Sqrt), deps=[tok_rv])
    tok_rw3 = b.dma("gpsimd", lambda e, s: (
        e.dma_start(out=scr2[4:5, 0:8192], in_=rsacc[0:1, :]).then_inc(s, 16),),
        1, deps=[tok_rs3], lane="r3")
    tok_rp3 = b.dma("gpsimd", lambda e, s: (
        e.dma_start(out=rrepA[:, :], in_=row_bcast(scr2[4:5, 0:8192], 8192)).then_inc(s, 16),),
        1, deps=[tok_rw3], lane="r3")

    ag_toks = []
    use_guard = [last_mm3] * NBP3
    st_guard = [None] * NBP3
    for cc in range(8):
        v_cc = d["ygv_cc"][cc]
        st_toks = []
        for tb3 in range(16):
            i = (cc * 16 + tb3) % NBP3
            tok_ly = b.dma("sync", lambda e, s, tb3=tb3, cc=cc, i=i: (
                e.dma_start(out=ybuf[i][:, :],
                            in_=y_d[128 * cc:128 * (cc + 1), 512 * tb3:512 * (tb3 + 1)]).then_inc(s, 16),
                e.dma_start(out=gbuf[i][:, :],
                            in_=gate_d[128 * cc:128 * (cc + 1), 512 * tb3:512 * (tb3 + 1)]).then_inc(s, 16)),
                2, deps=[use_guard[i]], lane=f"p3l{i % 2}")
            tok_sg = b.step("scalar", lambda sc, i=i: sc.activation(
                gbuf[i][:, :], gbuf[i][:, :], AF.Silu), deps=[tok_ly])
            tok_yg = b.step("vector", lambda v, i=i: v.tensor_tensor(
                sqb[i][:, :], ybuf[i][:, :], gbuf[i][:, :], ALU.mult),
                deps=[tok_sg, tok_rp3])
            use_guard[i] = tok_yg
            tok_o1 = b.step("vector", lambda v, tb3=tb3, i=i: v.tensor_tensor(
                youtb[i][:, :], sqb[i][:, :], rrepA[:, 512 * tb3:512 * (tb3 + 1)],
                ALU.mult), deps=[st_guard[i]])
            tok_o2 = b.step("vector", lambda v, cc=cc, i=i: v.tensor_scalar(
                youtb[i][:, :], youtb[i][:, :], nw_sb[:, cc:cc + 1], 0.0,
                ALU.mult, ALU.add), deps=[tok_sml])
            tok_st = b.dma("sync", lambda e, s, tb3=tb3, i=i, v_cc=v_cc: (
                e.dma_start(out=v_cc[:, 512 * tb3:512 * (tb3 + 1)],
                            in_=youtb[i][:, :]).then_inc(s, 16),), 1,
                deps=[tok_o2], lane=f"p3s{i % 2}")
            st_guard[i] = tok_st
            st_toks.append(tok_st)
        tok = ag_cb(cc, st_toks)
        if tok is not None:
            ag_toks.append(tok)
    return ag_toks



def emit_bigmm_block(b, P, lhsT_dram, cast, nk, rhs_sb, scale, biascol,
                     addin_dram, xres_in, xres_out, name, k0=0,
                     want_stats=True, acc_out_dram=None, final_out=False):
    ntok = TPC
    whb = P["whb"]
    oacc = P["oacc"]
    ps_guard = P.get("ps_mm_guard") or []
    ps_mm = P["ps_mm"]
    ps_s, ps_q = P["ps_s"], P["ps_q"]
    ones_f, ones_b = P["ones_f"], P["ones_b"]
    xsq2 = P["xsq2"]
    xc3 = P["xc3"]
    K = lhsT_dram.shape[0]
    st_tok = None
    mm_tok = {}
    prev_use = [None, None]
    st_stats = None
    for oc in range(32):
        buf = whb[oc % 2]
        eng = "gpsimd" if cast else "sync"
        tok_w = b.dma(eng, lambda e, s, oc=oc, buf=buf: (
            e.dma_start(out=buf[:, :].rearrange("p (kc o) -> p kc o", o=128),
                        in_=lhsT_dram.rearrange("(kc p) o -> p kc o", p=128)[:, k0:k0 + nk, 128 * oc:128 * (oc + 1)]).then_inc(s, 16),),
            1, deps=[prev_use[oc % 2], P.get("w_guard")], lane=f"{name}w{oc % 2}")
        for tb in range(2):
            bank = tb
            ps = ps_mm[bank]

            def mm(t, oc=oc, tb=tb, buf=buf, ps=ps):
                for kc in range(nk):
                    i = t.matmul(ps[:, :], buf[:, 128 * kc:128 * (kc + 1)],
                                 rhs_sb[:, ntok * kc + 512 * tb:ntok * kc + 512 * tb + 512],
                                 start=(kc == 0), stop=(kc == nk - 1))
                return i
            rhs_tok = P.get("rhs_tok")
            rhs_tok = rhs_tok if isinstance(rhs_tok, list) else [rhs_tok]
            tok_mm = b.step("tensor", mm,
                            deps=[tok_w, mm_tok.get(bank)] + rhs_tok +
                                 ([] if mm_tok.get(bank) else ps_guard))
            prev_use[oc % 2] = tok_mm
            if acc_out_dram is not None:
                tok_e = b.step("vector", lambda v, tb=tb, ps=ps: v.tensor_copy(
                    oacc[:, 512 * tb:512 * (tb + 1)], ps[:, :]),
                    deps=[tok_mm, st_tok] + ([] if st_tok else ps_guard))
                mm_tok[bank] = tok_e
            else:
                if biascol is not None:
                    tok_e = b.step("vector", lambda v, tb=tb, oc=oc, ps=ps: v.tensor_scalar(
                        oacc[:, 512 * tb:512 * (tb + 1)], ps[:, :],
                        biascol[:, oc:oc + 1], 0.0, ALU.add, ALU.add),
                        deps=[tok_mm, st_tok, P.get("bias_tok")] +
                             ([] if st_tok else ps_guard))
                    if scale != 1.0:
                        tok_e = b.step("vector", lambda v, tb=tb: v.tensor_single_scalar(
                            oacc[:, 512 * tb:512 * (tb + 1)],
                            oacc[:, 512 * tb:512 * (tb + 1)], scale, ALU.mult))
                else:
                    tok_e = b.step("vector", lambda v, tb=tb, ps=ps: v.tensor_single_scalar(
                        oacc[:, 512 * tb:512 * (tb + 1)], ps[:, :], scale, ALU.mult),
                        deps=[tok_mm, st_tok] + ([] if st_tok else ps_guard))
                mm_tok[bank] = tok_e
        if acc_out_dram is not None:
            st_tok = b.dma("sync", lambda e, s, oc=oc: (
                e.dma_start(out=acc_out_dram[128 * oc:128 * (oc + 1), :],
                            in_=oacc[:, :]).then_inc(s, 16),), 1,
                deps=[mm_tok[0], mm_tok[1]], lane=f"{name}s")
            continue
        tok_add = [mm_tok[0], mm_tok[1]]
        if addin_dram is not None:
            tok_la = b.dma("sync", lambda e, s, oc=oc: (
                e.dma_start(out=xc3[:, :], in_=addin_dram[128 * oc:128 * (oc + 1), :]).then_inc(s, 16),),
                1, deps=[P[f"{name}_a_guard"]], lane=f"{name}a")
            tok_add = [b.step("vector", lambda v: v.tensor_tensor(
                oacc[:, :], oacc[:, :], xc3[:, :], ALU.add),
                deps=[tok_la] + tok_add)]
        tok_lx = b.dma("sync", lambda e, s, oc=oc: (
            e.dma_start(out=xc3[:, :], in_=xres_in[128 * oc:128 * (oc + 1), :]).then_inc(s, 16),),
            1, deps=tok_add, lane=f"{name}a")
        tok_x2 = b.step("vector", lambda v: v.tensor_tensor(
            oacc[:, :], oacc[:, :], xc3[:, :], ALU.add), deps=[tok_lx])
        P[f"{name}_a_guard"] = tok_x2
        st_tok = b.dma("sync", lambda e, s, oc=oc: (
            e.dma_start(out=xres_out[128 * oc:128 * (oc + 1), :], in_=oacc[:, :]).then_inc(s, 16),),
            1, deps=[tok_x2], lane=f"{name}s")
        if want_stats:
            tok_sq = b.step("scalar", lambda sc: sc.activation(
                xsq2[:, :], oacc[:, :], AF.Square), deps=[tok_x2, st_stats])

            def mms(t, oc=oc):
                for tb in range(2):
                    t.matmul(ps_s[0:1, 512 * tb:512 * (tb + 1)], ones_f[:, :],
                             oacc[:, 512 * tb:512 * (tb + 1)], start=(oc == 0), stop=(oc == 31))
                for tb in range(2):
                    i = t.matmul(ps_q[0:1, 512 * tb:512 * (tb + 1)], ones_b[:, :],
                                 xsq2[:, 512 * tb:512 * (tb + 1)], start=(oc == 0), stop=(oc == 31))
                return i
            st_stats = b.step("tensor", mms, deps=[tok_sq, tok_x2, P["tok_ones"]])
    P["ps_mm_guard"] = [(b.esem["vector"], b.ecnt["vector"]),
                        (b.esem["scalar"], b.ecnt["scalar"]), st_tok]
    return st_tok, st_stats


def emit_ln_apply(b, P, xsrc_dram, dst_sb_or_none, dst_dram_or_none, tok_rep,
                  name, prev_guard=None, dst2_dram=None):
    ntok = TPC
    xcb = P["xcb"]
    rrep2, brep2 = P["rrep2"], P["brep2"]
    xlt = P["xlt"]
    tok_prev = None
    prev_ap = [prev_guard, prev_guard]
    last = None
    half_tok = None
    for cc in range(32):
        buf = xcb[cc % 2]
        tok_l = b.dma("sync", lambda e, s, cc=cc, buf=buf: (
            e.dma_start(out=buf[:, :], in_=xsrc_dram[128 * cc:128 * (cc + 1), :]).then_inc(s, 16),),
            1, deps=[prev_ap[cc % 2]], lane=f"ld{cc % 2}")
        b.step("vector", lambda v, buf=buf: v.tensor_tensor(
            buf[:, :], buf[:, :], rrep2[:, :], ALU.mult), deps=[tok_l, tok_rep])
        if dst_sb_or_none is not None:
            tok_a = b.step("vector", lambda v, cc=cc, buf=buf: v.tensor_tensor(
                dst_sb_or_none[:, ntok * cc:ntok * (cc + 1)], buf[:, :], brep2[:, :],
                ALU.subtract))
            last = tok_a
            prev_ap[cc % 2] = tok_a
            if dst_dram_or_none is not None:
                last = b.dma("sync", lambda e, s, cc=cc: (
                    e.dma_start(out=dst_dram_or_none[128 * cc:128 * (cc + 1), :],
                                in_=dst_sb_or_none[:, ntok * cc:ntok * (cc + 1)]).then_inc(s, 16),),
                    1, deps=[tok_a], lane="st2")
        else:
            tok_a = b.step("vector", lambda v, buf=buf: v.tensor_tensor(
                xlt[:, :], buf[:, :], brep2[:, :], ALU.subtract), deps=[tok_prev])
            tok_st = b.dma("sync", lambda e, s, cc=cc: (
                e.dma_start(out=dst_dram_or_none[128 * cc:128 * (cc + 1), :],
                            in_=xlt[:, :]).then_inc(s, 16),), 1,
                deps=[tok_a], lane="st2")
            tok_prev = tok_st
            prev_ap[cc % 2] = tok_a
            last = tok_st
            if cc == 15:
                half_tok = tok_st
    return [half_tok, last] if dst_sb_or_none is None else last


def emit_perm_gather(b, P, src_dram, pidx_sb, dst_sb, dep, name):
    last = [None, None]
    for ic in range(32):
        def g(e, s, ic=ic):
            e.indirect_dma_start(
                out=dst_sb[:, TPC * ic:TPC * (ic + 1)],
                out_offset=None,
                in_=src_dram,
                in_offset=bass.IndirectOffsetOnAxis(pidx_sb[:, ic:ic + 1], 0),
            ).then_inc(s, 16)
        last[ic % 2] = b.dma("gpsimd", g, 1,
                             deps=[dep, P.get("pg_guard")] if ic < 2 else [last[ic % 2]],
                             lane=f"pg{ic % 2}")
    return last


def emit_idx_gather(b, src_dram, idx_sb, dst_sb, k0, nk, deps, lane):
    last = [None, None]
    for i in range(nk):
        kc = k0 + i
        src = src_dram[kc] if isinstance(src_dram, list) else src_dram

        def g(e, s, i=i, kc=kc, src=src):
            e.indirect_dma_start(
                out=dst_sb[:, TPC * i:TPC * (i + 1)],
                out_offset=None,
                in_=src,
                in_offset=bass.IndirectOffsetOnAxis(idx_sb[:, kc:kc + 1], 0),
            ).then_inc(s, 16)
        last[i % 2] = b.dma("gpsimd", g, 1,
                            deps=list(deps) if i < 2 else [last[i % 2]],
                            lane=f"{lane}{i % 2}")
    return last



def emit_a3(b, PS, d):
    yall = d["yall"]
    x1 = d["x1"]
    woT = d["woT"]
    wcat = d["wcat"]
    hbase = d["hbase"]
    wbr = d["wbr"]
    deqT = d["deqT"]
    fbias = d["fbias"]
    perm = d["perm"]
    yidx = d["yidx"]
    x4o = d["x4"]
    xln5o = d["xln5_own"]
    y_guardA = d["y_guardA"]
    y_guardB = d["y_guardB"]
    dq_guard = d["dq_guard"]

    x2_d = b.dram("x2_d", [D, TPC], F32)
    x3_d = b.dram("x3_d", [D, TPC], F32)
    acc_d = b.dram("acc_d", [D, TPC], F32)
    xln4_d = b.dram("xln4_d", [D, TPC], BF16)

    big2 = b.sb("big2", [128, 32 * TPC], BF16)
    P = make_shared_pool(b, dict(ps_s=PS["pool_ps_s"], ps_q=PS["pool_ps_q"],
                                 ps_log=PS["psc"], ps_misc=PS["psc"]),
                         with_ffn0=True, fbf=big2)
    P["ps_mm"] = [PS["psa"], PS["psb"]]
    P["whb"] = [b.sb(f"whb{i}", [128, 32 * 128], BF16) for i in range(2)]
    P["oacc"] = b.sb("oacc", [128, TPC], F32)
    P["f0_acc"] = P["oacc"]
    P["f0_tmp"] = P["xcb"][0]
    load_ffn0_weights(b, P, wcat, wbr, hbase)
    fb_sb = b.sb("fb_sb", [128, 32], F32)
    pidx = b.sb("pidx", [128, 32], I32)
    yidx_sb = b.sb("yidx_sb", [128, 64], I32)
    tok_fb = b.dma("gpsimd", lambda e, s: (
        e.dma_start(out=fb_sb[:, :], in_=fbias.rearrange("(o p) -> p o", p=128),
                    allow_slow_non_contiguous=True).then_inc(s, 16),
        e.dma_start(out=pidx[:, :], in_=perm.rearrange("(c p) -> p c", p=128),
                    allow_slow_non_contiguous=True).then_inc(s, 16),
        e.dma_start(out=yidx_sb[:, :], in_=yidx).then_inc(s, 16)), 3, lane="fb")

    tok_y1 = emit_idx_gather(b, yall, yidx_sb, big2, 0, 32,
                             [tok_fb, y_guardA], "yl")
    P["rhs_tok"] = tok_y1
    P["w_guard"] = d["woT_guard"]
    st1, _ = emit_bigmm_block(b, P, woT, False, 32, big2, 1.0, None, None, None,
                              None, "op1", k0=0, want_stats=False,
                              acc_out_dram=acc_d)
    tok_y2 = emit_idx_gather(b, yall, yidx_sb, big2, 32, 32,
                             tok_y1 + [y_guardB,
                              (b.esem["tensor"], b.ecnt["tensor"])], "yl")
    P["rhs_tok"] = tok_y2
    P["op2_a_guard"] = st1
    st2, stats3 = emit_bigmm_block(b, P, woT, False, 32, big2, 1.0, None, acc_d,
                                   x1, x2_d, "op2", k0=32)
    tok_rep3 = emit_ln_finalize(b, P["ps_s"], P["ps_q"], P["stat"], P["scr"], 0,
                                P["rrep"], P["brep"], stats3, TPC, srep=P["srep"])
    P["x2_ready"] = st2
    P["rrep2"], P["brep2"] = P["rrep"], P["brep"]
    tok_xln3 = emit_ln_apply(b, P, x2_d, P["xln"], None, tok_rep3, "l3",
                             prev_guard=st2)
    ntok = TPC
    obuf = [P["xc3"], P["xc3"]]
    state = {"tok_mm": None, "tok_use": [None, None], "tok_st": None,
             "tok_l": {}}
    st_toks = []

    def pre_load(cc):
        if cc > 31 or cc in state["tok_l"]:
            return
        state["tok_l"][cc] = b.dma("sync", lambda e, s, cc=cc: (
            e.dma_start(out=obuf[cc % 2][:, :],
                        in_=x2_d[128 * cc:128 * (cc + 1), :]).then_inc(s, 16),),
            1, deps=[state["tok_use"][cc % 2], P["x2_ready"],
                     tok_xln3 if cc < 2 else None], lane="ocb")

    def out_cb(cc, f_ap, dep):
        pre_load(cc)
        tok_l = state["tok_l"].pop(cc)
        tok1 = b.step("vector", lambda v, cc=cc: v.scalar_tensor_tensor(
            P["x1sb"][:, :], f_ap[:, :], 0.5, obuf[cc % 2][:, :],
            ALU.mult, ALU.add),
            deps=[tok_l, dep, state["tok_mm"], state["tok_st"]])
        state["tok_use"][0] = state["tok_use"][1] = tok1
        pre_load(cc + 1)
        tok_st = b.dma("sync", lambda e, s, cc=cc: (
            e.dma_start(out=x3_d[128 * cc:128 * (cc + 1), :], in_=P["x1sb"][:, :]).then_inc(s, 16),),
            1, deps=[tok1], lane="st1")
        state["tok_st"] = tok_st
        st_toks.append(tok_st)
        tok_sq = b.step("scalar", lambda sc: sc.activation(
            P["xsq2"][:, :], P["x1sb"][:, :], AF.Square), deps=[tok1, state["tok_mm"]])

        def flush(cc=cc, tok_sq=tok_sq, tok1=tok1):
            def mm(t, cc=cc):
                for tb in range(2):
                    t.matmul(P["ps_s"][0:1, 512 * tb:512 * (tb + 1)], P["ones_f"][:, :],
                             P["x1sb"][:, 512 * tb:512 * (tb + 1)], start=(cc == 0), stop=(cc == 31))
                for tb in range(2):
                    i = t.matmul(P["ps_q"][0:1, 512 * tb:512 * (tb + 1)], P["ones_b"][:, :],
                                 P["xsq2"][:, 512 * tb:512 * (tb + 1)], start=(cc == 0), stop=(cc == 31))
                return i
            state["tok_mm"] = b.step("tensor", mm, deps=[tok_sq, tok1])
        P["pending_stats"] = flush
        return tok1

    P["ps_y"] = PS["psa"]
    emit_ffn0(b, P, P["xln"], tok_xln3, P["srep"], tok_rep3, out_cb)
    stats4 = state["tok_mm"]
    tok_rep4 = emit_ln_finalize(b, P["ps_s"], P["ps_q"], P["stat"], P["scr"], 4,
                                P["rrep"], P["brep"], stats4, TPC)
    tok_xln4 = emit_ln_apply(b, P, x3_d, None, xln4_d, tok_rep4, "l4",
                             prev_guard=st_toks[-1])
    P["pg_guard"] = tok_fb
    tok_pg = emit_perm_gather(b, P, xln4_d, pidx, big2, tok_xln4[-1], "pg")
    P["rhs_tok"] = tok_pg
    P["f1_a_guard"] = None
    P["bias_tok"] = tok_fb
    P["w_guard"] = dq_guard
    st4, stats5 = emit_bigmm_block(b, P, deqT, False, 32, big2, 0.5, fb_sb,
                                   None, x3_d, x4o, "f1")
    tok_rep5 = emit_ln_finalize(b, P["ps_s"], P["ps_q"], P["stat"], P["scr"], 8,
                                P["rrep"], P["brep"], stats5, TPC)
    tok_xln5 = emit_ln_apply(b, P, x4o, None, xln5o, tok_rep5, "l5",
                             prev_guard=st4)
    return tok_xln5, st4



def emit_c(b, PS, d):
    aall = d["aall"]
    x4 = d["x4"]
    wo = d["wo"]
    bo = d["bo"]
    deqT = d["deqT"]
    fbias = d["fbias"]
    perm = d["perm"]
    aidx = d["aidx"]
    x6o = d["x6"]
    a_guard = d["a_guard"]
    dq_guard = d["dq_guard"]

    x5_d = b.dram("x5_d", [D, TPC], F32)
    xln6_d = b.dram("xln6_d", [D, TPC], BF16)

    big2 = b.sb("big2", [128, 32 * TPC], BF16)
    P = make_shared_pool(b, dict(ps_s=PS["pool_ps_s"], ps_q=PS["pool_ps_q"]),
                         with_ffn0=False)
    P["ps_mm"] = [PS["psa"], PS["psb"]]
    P["whb"] = [b.sb(f"whb{i}", [128, 32 * 128], BF16) for i in range(2)]
    P["oacc"] = b.sb("oacc", [128, TPC], F32)
    fb_sb = b.sb("fb_sb", [128, 32], F32)
    bo_sb = b.sb("bo_sb", [128, 32], F32)
    pidx = b.sb("pidx", [128, 32], I32)
    aidx_sb = b.sb("aidx_sb", [128, 32], I32)
    tok_fb = b.dma("gpsimd", lambda e, s: (
        e.dma_start(out=fb_sb[:, :], in_=fbias.rearrange("(o p) -> p o", p=128),
                    allow_slow_non_contiguous=True).then_inc(s, 16),
        e.dma_start(out=bo_sb[:, :], in_=bo.rearrange("(o p) -> p o", p=128),
                    allow_slow_non_contiguous=True).then_inc(s, 16),
        e.dma_start(out=pidx[:, :], in_=perm.rearrange("(c p) -> p c", p=128),
                    allow_slow_non_contiguous=True).then_inc(s, 16),
        e.dma_start(out=aidx_sb[:, :], in_=aidx).then_inc(s, 16)), 4, lane="fb")
    tok_af = emit_idx_gather(b, aall, aidx_sb, big2, 0, 32,
                             [tok_fb, a_guard], "yl")
    P["rhs_tok"] = tok_af
    P["ao_a_guard"] = None
    P["bias_tok"] = tok_fb
    P["w_guard"] = d["wo_guard"]
    st1, stats6 = emit_bigmm_block(b, P, wo, False, 32, big2, 1.0, bo_sb,
                                   None, x4, x5_d, "ao")
    tok_rep6 = emit_ln_finalize(b, P["ps_s"], P["ps_q"], P["stat"], P["scr"], 0,
                                P["rrep"], P["brep"], stats6, TPC)
    P["rrep2"], P["brep2"] = P["rrep"], P["brep"]
    tok_xln6 = emit_ln_apply(b, P, x5_d, None, xln6_d, tok_rep6, "l6",
                             prev_guard=st1)
    P["pg_guard"] = tok_fb
    tok_pg = emit_perm_gather(b, P, xln6_d, pidx, big2, tok_xln6[-1], "pg")
    P["rhs_tok"] = tok_pg
    P["w_guard"] = dq_guard
    st2, _ = emit_bigmm_block(b, P, deqT, False, 32, big2, 0.5, fb_sb,
                              None, x5_d, x6o, "f1", want_stats=False)
    return st2



def emit_b(b, PS, d):
    wqkv = d["wqkv"]
    cosf = d["cosf"]
    sinf = d["sinf"]
    aov = d["aov"]
    xgA, xgB = d["x_guards"]

    xvH = [a.rearrange("(r cc p) t -> p r cc t", p=128, cc=16)
           for a in d["xall5H"]]
    scr = b.dram("scrB", [4, T], F32)

    wsb = b.sb("wsb", [128, 32 * 768], BF16)
    xt = b.sb("xt", [128, 32 * 256], BF16)
    Q = b.sb("Q", [128, 4 * T], BF16)
    K = b.sb("K", [128, T], BF16)
    Vf = b.sb("Vf", [128, T], BF16)
    Vt = b.sb("Vt", [128, 64 * 128], BF16)
    Pt = b.sb("Pt", [128, 2048], BF16)
    msk = b.sb("msk", [128, 4 * 512], BF16)
    sumr = b.sb("sumr", [1, 512], F32)
    rrep = b.sb("rrepB", [128, 512], F32)
    aob = b.sb("aob", [128, 512], BF16)
    nrm = b.sb("nrm", [1, 128], F32)
    bpp = b.sb("bpp", [128, 1], F32)
    ones_b = b.sb("ones_bB", [128, 1], BF16)
    idb = b.sb("idbB", [128, 128], BF16)
    ps_sc = PS["psbig"]
    ps_o = [PS["psa"], PS["psb"]]
    ps_sumf = PS["psc"]
    ps_tb = PS["b16"]

    tok_on = b.step("gpsimd", lambda g: g.memset(ones_b[:, :], 1.0))
    ones128 = b.sb("ones128B", [128, 128], BF16)
    tok_on2 = b.step("gpsimd", lambda g: g.memset(ones128[:, :], 1.0))
    b.step("gpsimd", lambda g: g.memset(idb[:, :], 1.0))
    tok_id = b.step("gpsimd", lambda g: g.affine_select(
        out=idb[:, :], in_=idb[:, :], compare_op=ALU.is_equal, fill=0.0, base=0,
        pattern=[[-1, 128]], channel_multiplier=1))
    tok_mg = None
    for j in range(4):
        b.step("gpsimd", lambda g, j=j: g.memset(msk[:, 512 * j:512 * (j + 1)], 1.0))
        tok_mg = b.step("gpsimd", lambda g, j=j: g.affine_select(
            out=msk[:, 512 * j:512 * (j + 1)], in_=msk[:, 512 * j:512 * (j + 1)],
            compare_op=ALU.is_ge, fill=0.0, base=-128 * j, pattern=[[1, 512]],
            channel_multiplier=-1))
    tok_w = b.dma("gpsimd", lambda e, s: (
        e.dma_start(out=wsb[:, :].rearrange("p (cc o) -> p cc o", o=768),
                    in_=wqkv.rearrange("(cc p) o -> p cc o", p=128)).then_inc(s, 16),),
        1, lane="wl")

    ev = {}
    prev_x = None
    for tb in range(32):
        tok_xa = b.dma("sync", lambda e, s, tb=tb: (
            e.dma_start(out=xt[:, 0:16 * 256].rearrange("p (cc t) -> p cc t", cc=16),
                        in_=xvH[0][:, tb // 4, :, (tb % 4) * 256:(tb % 4) * 256 + 256]).then_inc(s, 16),),
            1, deps=[prev_x, xgA], lane="xl")
        tok_xb = b.dma("sync", lambda e, s, tb=tb: (
            e.dma_start(out=xt[:, 16 * 256:32 * 256].rearrange("p (cc t) -> p cc t", cc=16),
                        in_=xvH[1][:, tb // 4, :, (tb % 4) * 256:(tb % 4) * 256 + 256]).then_inc(s, 16),),
            1, deps=[xgB], lane="xl")
        for oc in range(6):
            bank = oc % 2
            ps = ps_o[bank]

            def mmA(t, oc=oc, ps=ps):
                for cc in range(16):
                    i = t.matmul(ps[:, 0:256], wsb[:, 768 * cc + 128 * oc:768 * cc + 128 * (oc + 1)],
                                 xt[:, 256 * cc:256 * (cc + 1)],
                                 start=(cc == 0), stop=False)
                return i

            def mmB(t, oc=oc, ps=ps):
                for cc in range(16, 32):
                    i = t.matmul(ps[:, 0:256], wsb[:, 768 * cc + 128 * oc:768 * cc + 128 * (oc + 1)],
                                 xt[:, 256 * cc:256 * (cc + 1)],
                                 start=False, stop=(cc == 31))
                return i
            b.step("tensor", mmA, deps=[tok_w, tok_xa, ev.get(bank)])
            tok_mm = b.step("tensor", mmB, deps=[tok_xb])
            if oc < 4:
                dst = Q[:, T * oc + 256 * tb:T * oc + 256 * (tb + 1)]
            elif oc == 4:
                dst = K[:, 256 * tb:256 * (tb + 1)]
            else:
                dst = Vf[:, 256 * tb:256 * (tb + 1)]
            ev[bank] = b.step("scalar", lambda sc, dst=dst, ps=ps: sc.copy(dst, ps[:, 0:256]),
                              deps=[tok_mm])
        prev_x = (b.esem["tensor"], b.ecnt["tensor"])
    v_proj = [ev.get(0), ev.get(1)]

    tok_vt = None
    for kb in range(64):
        tok_t = b.step("tensor", lambda t, kb=kb: t.transpose(
            ps_tb[:, :], Vf[:, 128 * kb:128 * (kb + 1)], idb[:, :]),
            deps=[v_proj[0], v_proj[1], tok_id] if kb == 0 else [tok_vt])
        tok_vt = b.step("vector", lambda v, kb=kb: v.tensor_copy(
            Vt[:, 128 * kb:128 * (kb + 1)], ps_tb[:, :]), deps=[tok_t])
    v_vt = tok_vt

    cos2 = wsb[0:128, 0:T]
    sin2 = wsb[0:128, T:2 * T]
    tmp3 = wsb[0:64, 2 * T:3 * T]
    tok_cs2 = b.dma("gpsimd", lambda e, s: (
        e.dma_start(out=cos2[0:64, 0:T], in_=cosf).then_inc(s, 16),
        e.dma_start(out=cos2[64:128, 0:T], in_=cosf).then_inc(s, 16),
        e.dma_start(out=sin2[0:64, 0:T], in_=sinf).then_inc(s, 16),
        e.dma_start(out=sin2[64:128, 0:T], in_=sinf).then_inc(s, 16)), 4,
        deps=[v_proj[0], v_proj[1], (b.esem["tensor"], b.ecnt["tensor"])], lane="wl")
    tok_r = v_vt
    for hch in range(5):
        src = Q[:, T * hch:T * (hch + 1)] if hch < 4 else K[:, :]
        tok_r = b.step("vector", lambda v, src=src: v.tensor_tensor(
            Vf[0:64, 0:T], src[0:64, 0:T], cos2[0:64, 0:T], ALU.mult),
            deps=[tok_r, v_vt, tok_cs2])
        tok_r = b.step("vector", lambda v, src=src: v.tensor_tensor(
            Vf[64:128, 0:T], src[0:64, 0:T], sin2[0:64, 0:T], ALU.mult))
        tok_r = b.step("vector", lambda v, src=src: v.tensor_tensor(
            tmp3[0:64, 0:T], src[64:128, 0:T], sin2[64:128, 0:T], ALU.mult))
        tok_r = b.step("vector", lambda v, src=src: v.tensor_tensor(
            src[64:128, 0:T], src[64:128, 0:T], cos2[64:128, 0:T], ALU.mult))
        tok_r = b.step("vector", lambda v, src=src: v.tensor_tensor(
            src[0:64, 0:T], Vf[0:64, 0:T], tmp3[0:64, 0:T], ALU.subtract))
        tok_r = b.step("vector", lambda v, src=src: v.tensor_tensor(
            src[64:128, 0:T], Vf[64:128, 0:T], src[64:128, 0:T], ALU.add))
    v_rope = tok_r

    sq = b.sb("sqB", [128, 512], BF16)
    tok_nm = None
    tok_q2 = None
    for hch in range(5):
        src = Q[:, T * hch:T * (hch + 1)] if hch < 4 else K[:, :]
        for tb in range(16):
            idx = hch * 16 + tb
            tok_q = b.step("scalar", lambda sc, src=src, tb=tb: sc.activation(
                sq[:, :], src[:, 512 * tb:512 * (tb + 1)], AF.Square),
                deps=[v_rope, tok_q2])
            tok_q2 = b.step("tensor", lambda t: t.matmul(
                ps_sumf[0:1, :], ones_b[:, :], sq[:, :], start=True, stop=True),
                deps=[tok_q, tok_on, tok_nm])
            tok_nm = b.step("vector", lambda v, idx=idx: v.tensor_reduce(
                nrm[0:1, idx:idx + 1], ps_sumf[0:1, 0:512], AX.X, ALU.max),
                deps=[tok_q2])
    tok_b1 = b.step("vector", lambda v: v.tensor_reduce(
        nrm[0:1, 80:81], nrm[0:1, 0:64], AX.X, ALU.max), deps=[tok_nm])
    tok_b2 = b.step("vector", lambda v: v.tensor_reduce(
        nrm[0:1, 81:82], nrm[0:1, 64:80], AX.X, ALU.max))
    tok_b3 = b.step("scalar", lambda sc: sc.activation(
        nrm[0:1, 80:82], nrm[0:1, 80:82], AF.Sqrt), deps=[tok_b2])
    tok_b4 = b.step("vector", lambda v: v.tensor_tensor(
        nrm[0:1, 82:83], nrm[0:1, 80:81], nrm[0:1, 81:82], ALU.mult), deps=[tok_b3])
    tok_b5 = b.step("vector", lambda v: v.tensor_single_scalar(
        nrm[0:1, 82:83], nrm[0:1, 82:83], -1.0 / (128.0 ** 0.5), ALU.mult))
    tok_b6 = b.dma("gpsimd", lambda e, s: (
        e.dma_start(out=scr[0:1, 0:1], in_=nrm[0:1, 82:83]).then_inc(s, 16),), 1,
        deps=[tok_b5], lane="bb")
    tok_bpp = b.dma("gpsimd", lambda e, s: (
        e.dma_start(out=bpp[:, :], in_=row_bcast(scr[0:1, 0:1], 1)).then_inc(s, 16),),
        1, deps=[tok_b6], lane="bb")

    ISQ = 1.0 / (128.0 ** 0.5)
    tok_o_ev = {}
    tok_ao = None
    tok_rrep = None
    tok_exp_h = [None, None]
    tok_cons_h = [None, None]
    tok_se = tok_nm
    for qb in range(16):
        for h in range(4):
            bank = (qb * 4 + h) % 2
            pso = ps_o[bank]
            qsl = Q[:, T * h + 512 * qb:T * h + 512 * (qb + 1)]
            nsb = 2 * (qb + 1)
            exp_s = {}
            for s in range(nsb + 1):
                if s < nsb:
                    half = s % 2
                    psh = ps_sc[:, 1024 * half:1024 * (half + 1)]
                    pth = Pt[:, 1024 * half:1024 * (half + 1)]

                    def mmsc(t, s=s, qsl=qsl, psh=psh):
                        for j in range(2):
                            kb = 2 * s + j
                            i = t.matmul(psh[:, 512 * j:512 * (j + 1)],
                                         K[:, 128 * kb:128 * (kb + 1)], qsl,
                                         start=True, stop=True)
                        return i
                    tok_sc = b.step("tensor", mmsc,
                                    deps=[v_rope, tok_exp_h[half], tok_bpp])
                    tok_exp = b.step("scalar", lambda sc, psh=psh, pth=pth: sc.activation(
                        pth, psh, AF.Exp, bias=bpp[:, 0:1], scale=ISQ),
                        deps=[tok_sc, tok_bpp, tok_cons_h[half]])
                    if s >= nsb - 2:
                        moff = 1024 * (s - (nsb - 2))
                        tok_exp = b.step("vector", lambda v, pth=pth, moff=moff: v.tensor_tensor(
                            pth, pth, msk[:, moff:moff + 1024], ALU.mult),
                            deps=[tok_exp, tok_mg])
                    tok_exp_h[half] = tok_exp
                    exp_s[s] = tok_exp
                if s >= 1:
                    sp = s - 1
                    half = sp % 2
                    pth = Pt[:, 1024 * half:1024 * (half + 1)]

                    def mmsum(t, sp=sp, nsb=nsb, pth=pth):
                        for j in range(2):
                            i = t.matmul(ps_sumf[:, :], ones128[:, :],
                                         pth[:, 512 * j:512 * (j + 1)],
                                         start=(sp == 0 and j == 0), stop=(sp == nsb - 1 and j == 1))
                        return i
                    tok_sm = b.step("tensor", mmsum,
                                    deps=[exp_s[sp], tok_on2] + ([tok_se] if sp == 0 else []))

                    def mmpv(t, sp=sp, nsb=nsb, pso=pso, pth=pth):
                        for j in range(2):
                            kb = 2 * sp + j
                            i = t.matmul(pso[:, :], Vt[:, 128 * kb:128 * (kb + 1)],
                                         pth[:, 512 * j:512 * (j + 1)],
                                         start=(sp == 0 and j == 0), stop=(sp == nsb - 1 and j == 1))
                        return i
                    tok_pv = b.step("tensor", mmpv,
                                    deps=[v_vt, tok_o_ev.get(bank)] if sp == 0 else [])
                    tok_cons_h[half] = tok_pv
            tok_se = b.step("vector", lambda v: v.reciprocal(
                rrep[:, :], ps_sumf[:, :]), deps=[tok_sm])
            tok_oe = b.step("vector", lambda v, pso=pso: v.tensor_tensor(
                aob[:, :], pso[:, :], rrep[:, :], ALU.mult),
                deps=[tok_pv, tok_se, tok_ao])
            tok_o_ev[bank] = tok_oe
            tok_ao = b.dma("sync", lambda e, s, qb=qb, h=h: (
                e.dma_start(out=aov[128 * h:128 * (h + 1), 512 * qb:512 * (qb + 1)],
                            in_=aob[:, :]).then_inc(s, 16),), 1,
                deps=[tok_oe], lane="aos")
    return tok_ao



def build_full():
    b = KB()
    PSBIG = b.ps("PSBIG", [128, 2048])
    PSA = b.ps("PSA", [128, 512])
    PSB = b.ps("PSB", [128, 512])
    PSC = b.ps("PSC", [128, 512])
    PB16 = b.ps("PB16", [128, 128], BF16)
    PS = {
        "psbig": PSBIG, "psa": PSA, "psb": PSB, "psc": PSC, "b16": PB16,
        "pool_ps_s": PSBIG[0:1, 0:1024],
        "pool_ps_q": PSBIG[0:1, 1024:2048],
        "dq_ps": PB16,
        "a2_ps1a": PSBIG[:, 0:512],
        "a2_ps1b": PSBIG[:, 512:1024],
        "a2_ps_sc": PSC[:, 0:128],
        "a2_ps_q3": PSBIG[0:1, 1536:2048],
        "a2_ps_ypa": PSA[:, 0:128],
        "a2_ps_ypb": PSB[:, 0:128],
        "a2_ps_st2": PSBIG[:, 0:1024],
        "a2_ps_bc": PSBIG[:, 1024:2048],
        "b_ps_sum": PSC[0:1, 0:512],
    }

    x = b.dram("x", [D, TPC], F32, kind="ExternalInput")
    wcat = b.dram("wcat", [D, 24], F32, kind="ExternalInput")
    hbase = b.dram("hbase", [32], F32, kind="ExternalInput")
    wbr = b.dram("wbr", [1024, 1024], F32, kind="ExternalInput")
    wp = b.dram("wp", [4096, 256], I32, kind="ExternalInput")
    ws = b.dram("ws", [4096, 4], F32, kind="ExternalInput")
    winT = b.dram("winT", [D, NOC * 128], F32, kind="ExternalInput")
    convw = b.dram("convw", [1280, 4], F32, kind="ExternalInput")
    convb = b.dram("convb", [1280], F32, kind="ExternalInput")
    dtb = b.dram("dtb", [16], F32, kind="ExternalInput")
    avec = b.dram("avec", [16], F32, kind="ExternalInput")
    dexp = b.dram("dexp", [1024], F32, kind="ExternalInput")
    nwv = b.dram("nwv", [1024], F32, kind="ExternalInput")
    woTs = b.dram("woTs", [1024, D], F32, kind="ExternalInput")
    fbias = b.dram("fbias", [D], F32, kind="ExternalInput")
    perm = b.dram("perm", [D], I32, kind="ExternalInput")
    yidx = b.dram("yidx", [128, 64], I32, kind="ExternalInput")
    wqkv = b.dram("wqkv", [D, 768], F32, kind="ExternalInput")
    cosf = b.dram("cosf", [64, T], F32, kind="ExternalInput")
    sinf = b.dram("sinf", [64, T], F32, kind="ExternalInput")
    wos = b.dram("wos", [512, D], F32, kind="ExternalInput")
    bo = b.dram("bo", [D], F32, kind="ExternalInput")
    aidx = b.dram("aidx", [128, 32], I32, kind="ExternalInput")
    x6o = b.dram("x6", [D, TPC], F32, kind="ExternalOutput")

    x1_d = b.dram("x1_d", [D, TPC], F32)
    x4_d = b.dram("x4_d", [D, TPC], F32)
    xln2_own = b.dram("xln2_own", [D, TPC], BF16)
    xln2_allH = [b.dram(f"xln2_all{h}", [NCORES * 2048, TPC], BF16,
                        addr_space="Shared") for h in range(2)]
    dq_own = b.dram("dq_own", [512, D], BF16)
    deqT_all = b.dram("deqT_all", [D, D], BF16, addr_space="Shared")
    yg_ownH = [b.dram(f"ygoh{h}", [4096, 1024], BF16) for h in range(2)]
    yg_agH = [b.dram(f"ygagh{h}", [NCORES * 4096, 1024], BF16, addr_space="Shared")
              for h in range(2)]
    xconv_d = b.dram("xconv_d", [1280, T], BF16)
    gate_d = b.dram("gate_d", [1024, T], BF16)
    dt_d = b.dram("dt_d", [16, T], F32)
    y_d = b.dram("y_d", [1024, T], BF16)
    scr2 = b.dram("scr2", [8, 8192], F32)
    xln5_own = b.dram("xln5_own", [D, TPC], BF16)
    xln5_allH = [b.dram(f"xln5_all{h}", [NCORES * 2048, TPC], BF16,
                        addr_space="Shared") for h in range(2)]
    ao_own = b.dram("ao_own", [4096, 1024], BF16)
    ao_all = b.dram("ao_all", [NCORES * 4096, 1024], BF16, addr_space="Shared")
    woT_bf_own = b.dram("woT_bf_own", [1024, D], BF16)
    woT_bf = b.dram("woT_bf", [8192, D], BF16, addr_space="Shared")
    wo_bf_own = b.dram("wo_bf_own", [512, D], BF16)
    wo_bf = b.dram("wo_bf", [D, D], BF16, addr_space="Shared")

    def ag(src, dst, deps):
        return b.cc(lambda g: g.collective_compute(
            "AllGather", ALU.bypass, replica_groups=RG,
            ins=[src.opt()], outs=[dst.opt()]), deps)

    b.phase("a1_")
    wcb = [b.sb(f"wcast{i}", [128, D], BF16) for i in range(2)]
    tok_wc = [None, None]

    def cast_rows(src, dst, n):
        for i in range(n):
            tok_l = b.dma("gpsimd", lambda e, s, i=i, src=src: (
                e.dma_start(out=wcb[i % 2][:, :], in_=src[128 * i:128 * (i + 1), :]).then_inc(s, 16),),
                1, deps=[tok_wc[i % 2]], lane="wc")
            tok_wc[i % 2] = b.dma("sync", lambda e, s, i=i, dst=dst: (
                e.dma_start(out=dst[128 * i:128 * (i + 1), :], in_=wcb[i % 2][:, :]).then_inc(s, 16),),
                1, deps=[tok_l], lane="wc")
        return list(tok_wc)

    tok_wcast1 = cast_rows(woTs, woT_bf_own, 8)
    tok_wcast2 = cast_rows(wos, wo_bf_own, 4)
    cc_woT = ag(woT_bf_own, woT_bf, tok_wcast1)
    cc_wo = ag(wo_bf_own, wo_bf, tok_wcast2)
    P = make_shared_pool(b, dict(ps_s=PS["pool_ps_s"], ps_q=PS["pool_ps_q"],
                                 ps_log=PSC, ps_misc=PSC))
    P["ps_y"] = PSA
    load_ffn0_weights(b, P, wcat, wbr, hbase)
    st1, st2, _ = emit_block_ffn0(b, P, x, x1_d, xln2_own, 0.5)
    tok_dq = emit_dequant(b, PS, wp, ws, dq_own)
    cc_x2a = ag(xln2_own[0:2048, :], xln2_allH[0], [st2[15]])
    cc_x2b = ag(xln2_own[2048:4096, :], xln2_allH[1], [st2[-1]])
    cc_dq = ag(dq_own, deqT_all, [tok_dq])

    b.phase("a2a_")
    ph1 = emit_a2a(b, PS, dict(
        xallH=xln2_allH, winT=winT, convw=convw, convb=convb, dtb=dtb,
        xconv_d=xconv_d, gate_d=gate_d, dt_d=dt_d,
        x_guards=[cc_x2a, cc_x2b]))
    b.phase("a2b_")
    agstate = {}

    def yg_ag_cb(cc, st_toks):
        if cc == 3:
            agstate["A"] = ag(yg_ownH[0], yg_agH[0], st_toks[-2:])
            return agstate["A"]
        if cc == 7:
            agstate["B"] = ag(yg_ownH[1], yg_agH[1], st_toks[-2:])
            return agstate["B"]
        return None

    ygv_cc = []
    for cc in range(8):
        vH = yg_ownH[cc // 4].rearrange("(q c bb) t -> q c (bb t)", q=4, bb=8)
        ygv_cc.append(vH[cc % 4])
    emit_a2b(b, PS, dict(
        xconv_d=xconv_d, gate_d=gate_d, dt_d=dt_d, y_d=y_d, scr2=scr2,
        avec=avec, dexp=dexp, nwv=nwv, ygv_cc=ygv_cc), yg_ag_cb)
    yg_srcs = [yg_agH[0]] * 32 + [yg_agH[1]] * 32

    b.phase("a3_")
    tok_xln5, st4 = emit_a3(b, PS, dict(
        yall=yg_srcs, x1=x1_d, woT=woT_bf, wcat=wcat, hbase=hbase, wbr=wbr,
        deqT=deqT_all, fbias=fbias, perm=perm, yidx=yidx,
        x4=x4_d, xln5_own=xln5_own, y_guardA=agstate["A"],
        y_guardB=agstate["B"], dq_guard=cc_dq, woT_guard=cc_woT))
    cc_x5a = ag(xln5_own[0:2048, :], xln5_allH[0], [tok_xln5[0]])
    cc_x5b = ag(xln5_own[2048:4096, :], xln5_allH[1], [tok_xln5[1]])

    b.phase("b_")
    tok_ao = emit_b(b, PS, dict(
        xall5H=xln5_allH, wqkv=wqkv, cosf=cosf, sinf=sinf,
        aov=ao_own.rearrange("(c bb) t -> c (bb t)", bb=8),
        x_guards=[cc_x5a, cc_x5b]))
    cc_ao = ag(ao_own, ao_all, [tok_ao])

    b.phase("c_")
    st6 = emit_c(b, PS, dict(
        aall=ao_all, x4=x4_d, wo=wo_bf, bo=bo, deqT=deqT_all, fbias=fbias,
        perm=perm, aidx=aidx, x6=x6o, a_guard=cc_ao, dq_guard=cc_dq,
        wo_guard=cc_wo))
    return b.finish([st6])



def _t(a):
    return np.ascontiguousarray(np.asarray(a).T)


def _rope_tables():
    dim = 128
    freqs = 1.0 / 10000.0 ** (np.arange(0, dim, 2, dtype=np.float32) / dim)
    angle = np.arange(T, dtype=np.float32)[:, None] * freqs[None, :]
    return np.cos(angle).astype(np.float32), np.sin(angle).astype(np.float32)


def stage_a2(inp, g):
    wT = np.ascontiguousarray(inp["L0_attn_in_proj_w"].T)
    winT = np.zeros((4096, 2432), np.float32)
    winT[:, 0:1024] = wT[:, 1024 * g:1024 * (g + 1)]
    winT[:, 1024:2048] = wT[:, 8192 + 1024 * g:8192 + 1024 * (g + 1)]
    winT[:, 2048:2176] = wT[:, 16384 + 128 * g:16384 + 128 * (g + 1)]
    winT[:, 2176:2304] = wT[:, 17408 + 128 * g:17408 + 128 * (g + 1)]
    winT[:, 2304:2320] = wT[:, 18432 + 16 * g:18432 + 16 * (g + 1)]
    cw = inp["L0_attn_conv_w"][:, 0, :]
    rows = np.concatenate([np.arange(1024 * g, 1024 * (g + 1)),
                           np.arange(8192 + 128 * g, 8192 + 128 * (g + 1)),
                           np.arange(9216 + 128 * g, 9216 + 128 * (g + 1))])
    return {
        "winT": winT,
        "convw": np.ascontiguousarray(cw[rows]),
        "convb": np.ascontiguousarray(inp["L0_attn_conv_b"][rows]),
        "dtb": np.ascontiguousarray(inp["L0_attn_dt_bias"][16 * g:16 * (g + 1)]),
        "avec": -(np.arange(16, dtype=np.float32) + 16 * g + 1),
        "dexp": np.repeat(inp["L0_attn_D"][16 * g:16 * (g + 1)], 64).astype(np.float32),
        "nwv": np.ascontiguousarray(inp["L0_attn_norm_w"][1024 * g:1024 * (g + 1)]),
    }


def stage_b(inp, g):
    sh = np.concatenate([np.arange(0, 128, 2), np.arange(1, 128, 2)])
    wq = inp["L1_attn_Wq"]
    wk = inp["L1_attn_Wk"]
    wv = inp["L1_attn_Wv"]
    cols = []
    for h in range(4):
        cols.append(wq[:, (4 * g + h) * 128:(4 * g + h + 1) * 128][:, sh])
    cols.append(wk[:, g * 128:(g + 1) * 128][:, sh])
    cols.append(wv[:, g * 128:(g + 1) * 128])
    return np.ascontiguousarray(np.concatenate(cols, axis=1))


_BUILD_CACHE = {}


def _built(name, fn):
    if name not in _BUILD_CACHE:
        _BUILD_CACHE[name] = fn()
    return _BUILD_CACHE[name]


def kernel(**inputs):
    inp = {k: np.asarray(v) for k, v in inputs.items()}
    cores = list(range(NCORES))

    xT = _t(inp["L0_attn_hidden_states"][0])
    wcat = _t(np.concatenate([inp["L0_ffn_W_res"], inp["L0_ffn_W_pre"],
                              inp["L0_ffn_W_post"]], axis=0))
    hbase = np.zeros(32, np.float32)
    hbase[0:16] = np.asarray(inp["L0_ffn_H_res_base"]).reshape(16)
    hbase[16:20] = inp["L0_ffn_H_pre_base"]
    hbase[20:24] = inp["L0_ffn_H_post_base"]
    hbase[24] = np.asarray(inp["L0_ffn_gate"])[0]
    wbr = np.ascontiguousarray(inp["L0_ffn_W_branch"])
    woT = _t(inp["L0_attn_out_proj_w"])
    kcs_perm = np.array([g * 8 + cc for g in range(8) for cc in range(4)]
                        + [g * 8 + cc for g in range(8) for cc in range(4, 8)])
    woT = np.ascontiguousarray(
        woT.reshape(64, 128, 4096)[kcs_perm].reshape(8192, 4096))
    cos, sin = _rope_tables()
    cosf = np.ascontiguousarray(cos.T)
    sinf = np.ascontiguousarray(sin.T)
    wo = np.ascontiguousarray(inp["L1_attn_Wo"])
    bo = np.ascontiguousarray(inp["L1_attn_bo"])
    fbias = np.ascontiguousarray(inp["L1_ffn_bias"])
    perm = np.ascontiguousarray(inp["L1_ffn_perm"])

    p_ = np.arange(128)
    kcs_all = np.array([g * 8 + cc for g in range(8) for cc in range(4)]
                       + [g * 8 + cc for g in range(8) for cc in range(4, 8)])
    kc64 = kcs_all
    yidx = ((kc64[None, :] // 8) * 4096 + ((kc64[None, :] % 8) % 4) * 1024
            + p_[:, None] * 8).astype(np.int32)
    kc32 = np.arange(32)
    C2 = kc32[None, :] * 128 + p_[:, None]
    aidx_base = ((C2 // 512) * 4096 + (C2 % 512) * 8).astype(np.int32)

    nc = _built("full", build_full)
    in_maps = []
    for j in range(NCORES):
        m = {
            "x": np.ascontiguousarray(xT[:, TPC * j:TPC * (j + 1)]),
            "wcat": wcat, "hbase": hbase, "wbr": wbr,
            "wp": np.ascontiguousarray(inp["L1_ffn_w_packed"][:, 256 * j:256 * (j + 1)]),
            "ws": np.ascontiguousarray(inp["L1_ffn_w_scales"][:, 4 * j:4 * (j + 1)]),
            "woTs": np.ascontiguousarray(woT[1024 * j:1024 * (j + 1)]),
            "fbias": fbias, "perm": perm,
            "yidx": (yidx + j).astype(np.int32),
            "cosf": cosf, "sinf": sinf,
            "wos": np.ascontiguousarray(wo[512 * j:512 * (j + 1)]),
            "bo": bo,
            "aidx": (aidx_base + j).astype(np.int32),
            "wqkv": stage_b(inp, j),
        }
        m.update(stage_a2(inp, j))
        in_maps.append(m)
    res = run_bass_kernel_spmd(nc, in_maps, core_ids=cores)
    x6 = np.concatenate([np.asarray(res.results[j]["x6"]) for j in range(NCORES)], axis=1)
    return np.ascontiguousarray(x6.T)[None].astype(np.float32)
